# version 03
# baseline (speedup 1.0000x reference)
import contextlib
import os
import numpy as np
import ml_dtypes
import concourse.bass as bass
import concourse.mybir as mybir
from concourse.bass_utils import run_bass_kernel_spmd

F32 = mybir.dt.float32
BF16 = mybir.dt.bfloat16
AX = mybir.AluOpType
ACT = mybir.ActivationFunctionType
XL = mybir.AxisListType.X

BF = ml_dtypes.bfloat16
MAGIC = float(np.float32(1.5 * 2 ** 23))
NEG = -1.0e9
D = 2048
NCH = 16
LAM = float(np.float32(1.0 / np.sqrt(8192.0)))
SQD = float(np.sqrt(2048.0))
QPERM = [0, 4, 1, 5, 2, 6, 3, 7, 8, 12, 9, 13, 10, 14, 11, 15]
QMATS = [("w1", 2048, 8192), ("w2", 8192, 2048), ("wg", 2048, 8192),
         ("wu", 2048, 8192), ("wd", 8192, 2048)]
NELEM_Q = float(2048 * 8192)


def _hadamard(n):
    H = np.array([[1.0]], np.float32)
    while H.shape[0] < n:
        H = np.block([[H, H], [H, -H]])
    return H.astype(np.float32)


def _host_prep(inputs):
    f = lambda k: np.ascontiguousarray(np.asarray(inputs[k], np.float32))
    x = f("L0_attn_x")
    xT = np.ascontiguousarray(x.transpose(0, 2, 1))
    w_qkvT = np.ascontiguousarray(f("L0_attn_w_qkv").T)
    w_outT = np.ascontiguousarray(f("L0_attn_w_out").T)
    norm_w = f("L0_attn_norm_w").reshape(1, 2048)
    pm = f("L0_attn_pm")
    w1T = np.ascontiguousarray(f("L0_ffn_w1").T)
    w2T = np.ascontiguousarray(f("L0_ffn_w2").T)
    gT = np.ascontiguousarray(f("L1_ffn_w_gate").T)
    uT = np.ascontiguousarray(f("L1_ffn_w_up").T)
    dT = np.ascontiguousarray(f("L1_ffn_w_down").T)
    cidx = np.arange(64)[:, None, None]
    hidx = np.arange(2)[None, :, None]
    aidx = np.arange(64)[None, None, :]
    jmap = (aidx * 128 + 2 * cidx + hidx).reshape(-1)
    dTP = np.ascontiguousarray(dT[jmap])
    wmod0 = f("L0_wmod"); wmod1 = f("L1_wmod")
    c0 = f("L0_c"); c1 = f("L1_c")
    Wq = f("L1_attn_Wq"); Wk = f("L1_attn_Wk")
    Wv = f("L1_attn_Wv"); Wo = f("L1_attn_Wo")

    idb = np.eye(128, dtype=np.float32).astype(BF)
    idf = np.eye(128, dtype=np.float32)
    onescol = np.ones((128, 1), np.float32).astype(BF)
    onesrow = np.ones((1, 128), np.float32)
    one11 = np.ones((1, 1), np.float32)
    inv0 = 1.0 / (10000.0 ** (np.arange(0, 128, 2, dtype=np.float32) / 128.0))
    fr = np.arange(2048, dtype=np.float32)[:, None] * inv0[None, :]
    fr = np.repeat(fr, 2, axis=-1)
    l0cos_all = np.cos(fr).T.astype(BF)
    l0sin_all = np.sin(fr).T.astype(BF)
    P0 = np.zeros((128, 128), np.float32)
    for i in range(64):
        P0[2 * i, 2 * i + 1] = -1.0
        P0[2 * i + 1, 2 * i] = 1.0
    prot0 = np.ascontiguousarray(P0.T).astype(BF)
    inv1 = 1.0 / (10000.0 ** (np.arange(32, dtype=np.float32) * 2.0 / 64.0))
    ang = np.arange(2048, dtype=np.float32)[:, None] * inv1[None, :]
    c1t = np.cos(ang).T; s1t = np.sin(ang).T
    l1cos = np.zeros((128, 2048), np.float32)
    l1sin = np.zeros((128, 2048), np.float32)
    for hh in range(2):
        for half in range(2):
            rows = slice(hh * 64 + half * 32, hh * 64 + half * 32 + 32)
            l1cos[rows] = c1t
            l1sin[rows] = s1t
    l1cos = l1cos.astype(BF); l1sin = l1sin.astype(BF)
    P1 = np.zeros((128, 128), np.float32)
    for hh in range(2):
        b0 = hh * 64
        for i in range(32):
            P1[b0 + i, b0 + 32 + i] = -1.0
            P1[b0 + 32 + i, b0 + i] = 1.0
    prot1 = np.ascontiguousarray(P1.T).astype(BF)
    mb = np.full((512, 528), NEG, np.float32)
    mb[:, :32] = 0.0
    qi = np.arange(496)[:, None]
    kj = np.arange(496)[None, :]
    mb[16:, 32:] = np.where(kj <= qi, np.float32(0), np.float32(NEG))
    mbT = mb.T
    l0mask = np.ascontiguousarray(np.concatenate(
        [mbT[16 + kc * 128:16 + (kc + 1) * 128, :] for kc in range(4)],
        axis=1)).astype(BF)
    kk = np.arange(128)[:, None]
    qq = np.arange(512)[None, :]
    l1mask = np.concatenate(
        [np.where(kk <= qq - 128 * j, np.float32(0), np.float32(NEG))
         for j in range(4)], axis=1).astype(BF)
    h128 = _hadamard(128).astype(BF)
    h64 = _hadamard(64)
    bd = np.zeros((128, 128), np.float32)
    bd[:64, :64] = h64
    bd[64:, 64:] = h64
    bd64 = bd.astype(BF)
    pmkt_sb = np.zeros((128, 256), np.float32)
    for h in range(16):
        pmkt_sb[:, h * 16:(h + 1) * 16] = pm[0][h].T
    pmv_sb = np.zeros((16, 2048), np.float32)
    for h in range(16):
        pmv_sb[:, h * 128:(h + 1) * 128] = pm[1][h]

    shared = dict(wqkvT=w_qkvT, woutT=w_outT, normw=norm_w, Wo=Wo,
                  idb=idb, idf=idf, onescol=onescol, onesrow=onesrow,
                  one11=one11, prot0=prot0, prot1=prot1, l1cos=l1cos,
                  l1sin=l1sin, l0mask=l0mask, l1mask=l1mask, h128=h128,
                  bd64=bd64, pmkt=pmkt_sb, pmv=pmv_sb)
    in_maps = []
    for c in range(8):
        pair, par = c // 2, c % 2
        m = dict(shared)
        m["xT"] = np.ascontiguousarray(xT[pair][:, par * 1024:(par + 1) * 1024])
        m["l0cos"] = np.ascontiguousarray(l0cos_all[:, par * 1024:(par + 1) * 1024])
        m["l0sin"] = np.ascontiguousarray(l0sin_all[:, par * 1024:(par + 1) * 1024])
        heads = [par * 16 + j for j in QPERM]
        m["Wq_own"] = np.ascontiguousarray(
            np.concatenate([Wq[:, h * 64:(h + 1) * 64] for h in heads], axis=1))
        m["Wk_own"] = np.ascontiguousarray(Wk[:, par * 256:(par + 1) * 256])
        m["Wv_own"] = np.ascontiguousarray(Wv[:, par * 256:(par + 1) * 256])
        for name, I, O in QMATS:
            src = {"w1": w1T, "w2": w2T, "wg": gT, "wu": uT, "wd": dTP}[name]
            rs = I // 8
            m[name + "_sl"] = np.ascontiguousarray(src[c * rs:(c + 1) * rs])
        m["wmod0_sl"] = np.ascontiguousarray(wmod0[c * 256:(c + 1) * 256])
        m["wmod1_sl"] = np.ascontiguousarray(wmod1[c * 256:(c + 1) * 256])
        oh = np.zeros((128, 768), np.float32)
        oh[:, pair::4] = 1.0
        m["ohmask"] = oh
        m["c0s"] = np.ascontiguousarray(c0[:, c * 256:(c + 1) * 256])
        m["c1s"] = np.ascontiguousarray(c1[:, c * 256:(c + 1) * 256])
        in_maps.append(m)
    return in_maps


class Ctx:
    def __init__(self):
        self.q = {e: [] for e in ("tensor", "vector", "scalar", "gpsimd", "sync")}
        self.semh = {}
        self.cnt = {}

    def push(self, eng, fn, waits=(), inc=None, incn=1):
        waits = tuple(w for w in waits if w is not None)
        if eng == "vector":
            if self.cnt.get("eV", 0) > 0:
                waits = waits + (("eV", self.cnt["eV"]),)
            if inc is None:
                inc = "eV"
        elif eng == "scalar":
            if self.cnt.get("eS", 0) > 0:
                waits = waits + (("eS", self.cnt["eS"]),)
            if inc is None:
                inc = "eS"
        tok = None
        if inc is not None:
            self.cnt[inc] += incn
            tok = (inc, self.cnt[inc])
        semh = self.semh

        def thunk(e, fn=fn, waits=waits, inc=inc, incn=incn):
            for (s, v) in waits:
                e.wait_ge(semh[s], v)
            inst = fn(e)
            if inc is not None:
                inst.then_inc(semh[inc], incn)
        self.q[eng].append(thunk)
        return tok

    def barrier(self):
        snap = [(s, v) for s, v in self.cnt.items() if v > 0]
        semh = self.semh
        for eng in self.q:
            def thunk(e, snap=tuple(snap)):
                for (s, v) in snap:
                    e.wait_ge(semh[s], v)
            self.q[eng].append(thunk)


def build_program():
    nc = bass.Bass("TRN2", target_bir_lowering=False)
    C = Ctx()

    def din(name, shape, dt=F32):
        return nc.dram_tensor(name, shape, dt, kind="ExternalInput").ap()

    xT_d = din("xT", [2048, 1024])
    wqkvT_d = din("wqkvT", [2048, 6144])
    woutT_d = din("woutT", [2048, 2048])
    normw_d = din("normw", [1, 2048])
    pmkt_d = din("pmkt", [128, 256])
    pmv_d = din("pmv", [16, 2048])
    Wo_d = din("Wo", [2048, 2048])
    Wq_d = din("Wq_own", [2048, 1024])
    Wk_d = din("Wk_own", [2048, 256])
    Wv_d = din("Wv_own", [2048, 256])
    idb_d = din("idb", [128, 128], BF16)
    idf_d = din("idf", [128, 128])
    onescol_d = din("onescol", [128, 1], BF16)
    onesrow_d = din("onesrow", [1, 128])
    one11_d = din("one11", [1, 1])
    prot0_d = din("prot0", [128, 128], BF16)
    prot1_d = din("prot1", [128, 128], BF16)
    l0cos_d = din("l0cos", [128, 1024], BF16)
    l0sin_d = din("l0sin", [128, 1024], BF16)
    l1cos_d = din("l1cos", [128, 2048], BF16)
    l1sin_d = din("l1sin", [128, 2048], BF16)
    l0mask_d = din("l0mask", [128, 2048], BF16)
    l1mask_d = din("l1mask", [128, 2048], BF16)
    h128_d = din("h128", [128, 128], BF16)
    bd64_d = din("bd64", [128, 128], BF16)
    wmod_sl_d = {0: din("wmod0_sl", [256, 12288]), 1: din("wmod1_sl", [256, 12288])}
    oh_d = din("ohmask", [128, 768])
    c0s_d = din("c0s", [4, 256])
    c1s_d = din("c1s", [4, 256])
    qsl_d = {name: din(name + "_sl", [I // 8, O]) for name, I, O in QMATS}
    out_d = nc.dram_tensor("out", [1024, 2048], F32, kind="ExternalOutput").ap()
    DBG = os.environ.get("MK_DEBUG") == "1"
    dbg = {}
    if DBG:
        dbg["modv"] = nc.dram_tensor("dbg_modv", [128, 192], F32, kind="ExternalOutput").ap()
        dbg["cw"] = nc.dram_tensor("dbg_cw", [1, 8], F32, kind="ExternalOutput").ap()
        dbg["cwq"] = nc.dram_tensor("dbg_cwq", [1, 8], F32, kind="ExternalOutput").ap()
        dbg["stg"] = nc.dram_tensor("dbg_stg", [1, 8], F32, kind="ExternalOutput").ap()
        dbg["mp"] = nc.dram_tensor("dbg_mp", [128, 768], F32, kind="ExternalOutput").ap()
        dbg["ra"] = nc.dram_tensor("dbg_ra", [1, 512], F32, kind="ExternalOutput").ap()
        dbg["rb"] = nc.dram_tensor("dbg_rb", [1, 512], F32, kind="ExternalOutput").ap()
        dbg["sm"] = nc.dram_tensor("dbg_sm", [1, 512], F32, kind="ExternalOutput").ap()
        dbg["ssr"] = nc.dram_tensor("dbg_ssr", [1, 512], F32, kind="ExternalOutput").ap()
        dbg["flag"] = [0]
        dbg["h"] = nc.dram_tensor("dbg_h", [128, 8192], BF16, kind="ExternalOutput").ap()
        dbg["xn"] = nc.dram_tensor("dbg_xn", [128, 8192], BF16, kind="ExternalOutput").ap()
        dbg["x1"] = nc.dram_tensor("dbg_x1", [128, 16384], F32, kind="ExternalOutput").ap()
        dbg["x2"] = nc.dram_tensor("dbg_x2", [128, 16384], F32, kind="ExternalOutput").ap()
        dbg["x3"] = nc.dram_tensor("dbg_x3", [128, 16384], F32, kind="ExternalOutput").ap()
        dbg["ctx"] = nc.dram_tensor("dbg_ctx", [128, 8192], BF16, kind="ExternalOutput").ap()
        dbg["xq"] = nc.dram_tensor("dbg_xq", [128, 4096], BF16, kind="ExternalOutput").ap()
        dbg["mid"] = nc.dram_tensor("dbg_mid", [128, 16384], BF16, kind="ExternalOutput").ap()
        dbg["mint"] = nc.dram_tensor("dbg_mint", [128, 16384], BF16, kind="ExternalOutput").ap()
        dbg["q1ss"] = nc.dram_tensor("dbg_q1ss", [1, 256], F32, kind="ExternalOutput").ap()
        dbg["q1mx"] = nc.dram_tensor("dbg_q1mx", [1, 256], F32, kind="ExternalOutput").ap()
        dbg["q1cb"] = nc.dram_tensor("dbg_q1cb", [1, 256], F32, kind="ExternalOutput").ap()
        dbg["q1os"] = nc.dram_tensor("dbg_q1os", [1, 256], F32, kind="ExternalOutput").ap()
        dbg["q2cb"] = nc.dram_tensor("dbg_q2cb", [1, 256], F32, kind="ExternalOutput").ap()
        dbg["q2os"] = nc.dram_tensor("dbg_q2os", [1, 256], F32, kind="ExternalOutput").ap()
        dbg["hffn"] = nc.dram_tensor("dbg_hffn", [128, 8192], BF16, kind="ExternalOutput").ap()
        dbg["w1q"] = nc.dram_tensor("dbg_w1q", [2048, 8192], BF16, kind="ExternalOutput").ap()
        dbg["w2q"] = nc.dram_tensor("dbg_w2q", [8192, 2048], BF16, kind="ExternalOutput").ap()

    modb_i = nc.dram_tensor("modb_i", [128, 768], F32)
    modb_o = nc.dram_tensor("modb_o", [128, 768], F32)
    absb_i = nc.dram_tensor("absb_i", [1, 8], F32)
    absb_o = nc.dram_tensor("absb_o", [1, 8], F32)
    wq_i = {name: nc.dram_tensor("wqi_" + name, [I // 8, O], BF16)
            for name, I, O in QMATS}
    wq_o = {name: nc.dram_tensor("wqo_" + name, [I, O], BF16)
            for name, I, O in QMATS}
    hb_i = nc.dram_tensor("hb_i", [2048, 1024], BF16)
    hb_o = nc.dram_tensor("hb_o", [4096, 1024], BF16)
    shufb = nc.dram_tensor("shufb", [128, 8192], BF16)
    ctx2b = nc.dram_tensor("ctx2b", [2048, 1024], BF16)

    st = contextlib.ExitStack()
    sb = lambda n, s, d: st.enter_context(nc.sbuf_tensor(n, s, d))
    psa = lambda n, s, d: st.enter_context(nc.psum_tensor(n, s, d))

    X = sb("X", [128, NCH * 1024], F32)
    H = sb("H", [128, NCH * 512], BF16)
    WB = [sb("WBa", [128, 4096], BF16), sb("WBb", [128, 4096], BF16)]
    TMP = sb("TMPa", [128, 512], F32)
    TMP2 = sb("TMPb", [128, 512], F32)
    TMP3 = sb("TMPc", [128, 4], F32)
    BCA = sb("BCAx", [128, 512], F32)
    BCB = sb("BCBx", [128, 512], F32)
    XB = sb("XBx", [128, 1024], BF16)
    XB2 = sb("XB2x", [128, 1024], BF16)
    IDB = sb("IDBx", [128, 128], BF16)
    IDF = sb("IDFx", [128, 128], F32)
    ONESC = sb("ONESCx", [128, 1], BF16)
    ONESR = sb("ONESRx", [1, 128], F32)
    ONE11 = sb("ONE11x", [1, 1], F32)
    PROT1 = sb("PROT1x", [128, 128], BF16)
    H128 = sb("H128x", [128, 128], BF16)
    BD64 = sb("BD64x", [128, 128], BF16)
    NORMW = sb("NORMWx", [128, 16], F32)
    MODV = sb("MODVx", [128, 192], F32)
    R_SM = sb("R_SMx", [1, 512], F32)
    R_SS = sb("R_SSx", [1, 512], F32)
    R_A = sb("R_Ax", [1, 512], F32)
    R_B = sb("R_Bx", [1, 512], F32)
    R_MX = sb("R_MXx", [1, 512], F32)
    R_T = sb("R_Tx", [1, 512], F32)
    R_OS = sb("R_OSx", [1, 512], F32)
    R_CW = sb("R_CWx", [1, 8], F32)
    R_SQ = sb("R_SQx", [1, 8], F32)
    R_CWQ = sb("R_CWQx", [1, 8], F32)
    MCOL = sb("MCOLx", [128, 20], F32)
    SCL = sb("SCLx", [128, 2], F32)
    DCOL = sb("DCOLx", [128, 1], F32)
    EPSC = sb("EPSCx", [1, 2], F32)

    pbank = [psa("pb%d" % i, [128, 512], F32) for i in range(4)]
    pSTAT = psa("pSTATx", [128, 512], F32)
    pSTAT2 = psa("pSTAT2x", [128, 512], F32)
    pBF = psa("pBFx", [128, 1024], BF16)
    pSM = psa("pSMx", [128, 512], F32)

    DSTREAMS = ("CS", "WB0", "WB1", "QS", "WQ", "MB", "HS", "HB", "CX",
                "CXL", "OS", "FW", "XS")
    for s in ("cC", "eT", "eV", "eS") + DSTREAMS:
        C.semh[s] = st.enter_context(nc.semaphore(s))
        C.cnt[s] = 0

    QMAP = {"CS": "sync", "WB0": "gpsimd", "WB1": "gpsimd", "QS": "sync",
            "WQ": "sync", "MB": "sync", "HS": "sync", "HB": "sync",
            "CX": "sync", "CXL": "sync", "OS": "sync", "FW": "gpsimd",
            "XS": "sync"}

    SERIAL = {"CS", "MB", "HB", "CX", "OS", "WQ", "QS", "HS", "XS"}

    def dma(stream, out, in_, waits=(), queue=None):
        q = queue or QMAP[stream]
        w = tuple(waits)
        if stream in SERIAL and C.cnt[stream] > 0:
            w = w + ((stream, C.cnt[stream]),)
        return C.push(q, lambda e, o=out, i=in_: e.dma_start(out=o, in_=i),
                      waits=w, inc=stream, incn=16)

    def scur(stream):
        return (stream, C.cnt[stream])

    def mm(lhsT, rhs, out, start, stop, waits=(), inc=None):
        return C.push("tensor",
                      lambda e, l=lhsT, r=rhs, o=out, s1=start, s2=stop:
                      e.matmul(o, l, r, start=s1, stop=s2),
                      waits=waits, inc=inc)

    def tpose(in_, ident, out, waits=(), inc="eT"):
        return C.push("tensor", lambda e, i=in_, d=ident, o=out:
                      e.transpose(o, i, d), waits=waits, inc=inc)

    def vop(fn, waits=(), inc="eV"):
        return C.push("vector", fn, waits=waits, inc=inc)

    def sop(fn, waits=(), inc="eS"):
        return C.push("scalar", fn, waits=waits, inc=inc)

    def wsync(eng, waits):
        waits = [w for w in waits if w is not None]
        semh = C.semh

        def thunk(e, waits=tuple(waits)):
            for (sname, v) in waits:
                e.wait_ge(semh[sname], v)
        C.q[eng].append(thunk)

    banks = [None, None, None, None]
    pSM_free = [None]
    pBF_free = [None]
    stat_free = [None, None]
    res_tok = [None]
    wb_free = [None, None]
    wb_i = [0]

    def bcast_row(row_ap, out_sb, n, waits):
        t1 = mm(ONESR[:, :], row_ap, pSM[0:128, 0:n], True, True,
                waits=tuple(waits) + (pSM_free[0],), inc="eT")
        t2 = vop(lambda e, o=out_sb, n=n: e.tensor_copy(o[:, 0:n], pSM[0:128, 0:n]),
                 waits=(t1,))
        pSM_free[0] = t2
        return t2

    def negbias_from_col(col_ap, scale, outcol, waits):
        t1 = tpose(col_ap, IDF[:, :], pSM[0:1, 0:128],
                   waits=tuple(waits) + (pSM_free[0],))
        t2 = vop(lambda e: e.reduce_max(R_T[0:1, 0:1], pSM[0:1, 0:128], XL),
                 waits=(t1,))
        t3 = mm(ONESR[:, :], R_T[0:1, 0:1], pSM[0:128, 0:1], True, True,
                waits=(t2,), inc="eT")
        t4 = vop(lambda e, oc=outcol, sc=scale: e.tensor_scalar(
            MCOL[:, oc:oc + 1], pSM[0:128, 0:1], -sc, None, AX.mult), waits=(t3,))
        pSM_free[0] = t4
        return t4

    def absmax_to_row(tiles, N, waits):
        t = vop(lambda e, a=tiles[0], N=N: e.tensor_mul(TMP2[:, 0:N], a, a),
                waits=tuple(waits))
        for ap_ in tiles[1:]:
            vop(lambda e, a=ap_, N=N: e.tensor_mul(TMP[:, 0:N], a, a))
            t = vop(lambda e, N=N: e.tensor_max(TMP2[:, 0:N], TMP2[:, 0:N],
                                                TMP[:, 0:N]))
        last = None
        for j in range(N // 128):
            tt = tpose(TMP2[:, j * 128:(j + 1) * 128], IDF[:, :],
                       pSM[0:128, 0:128], waits=(t, pSM_free[0]))
            tr = vop(lambda e: e.reduce_max(TMP3[:, 0:1], pSM[0:128, 0:128], XL),
                     waits=(tt,))
            pSM_free[0] = tr
            tt2 = tpose(TMP3[:, 0:1], IDF[:, :], pSM[0:1, 0:128],
                        waits=(tr, pSM_free[0]))
            last = vop(lambda e, j=j: e.tensor_copy(
                R_MX[0:1, j * 128:(j + 1) * 128], pSM[0:1, 0:128]), waits=(tt2,))
            pSM_free[0] = last
        sop(lambda e, N=N: e.activation(R_MX[0:1, 0:N], R_MX[0:1, 0:N], ACT.Ln),
            waits=(last,))
        last = sop(lambda e, N=N: e.activation(R_MX[0:1, 0:N], R_MX[0:1, 0:N],
                                               ACT.Exp, scale=0.5))
        return last

    def ln_modulated(l, modofs, x_tok0, N, t_modv):
        base = l * 96 + modofs
        m1 = m2 = None
        m2s = [None] * NCH
        for cch in range(NCH):
            sl_ = (cch % 2) * 512
            xc = X[:, cch * 1024 + x_tok0: cch * 1024 + x_tok0 + N]
            t_cast = sop(lambda e, xc=xc, N=N, sl_=sl_: e.copy(
                XB[:, sl_:sl_ + N], xc),
                waits=(res_tok[0], m2s[cch - 2] if cch >= 2 else None))
            t_sq = vop(lambda e, N=N, sl_=sl_: e.tensor_mul(
                XB2[:, sl_:sl_ + N], XB[:, sl_:sl_ + N], XB[:, sl_:sl_ + N]),
                waits=(t_cast,))
            m1 = mm(ONESC[:, :], XB[:, sl_:sl_ + N], pSTAT[0:1, 0:N],
                    cch == 0, cch == NCH - 1,
                    waits=(t_cast, stat_free[0]) if cch == 0 else (t_cast,),
                    inc="eT")
            m2 = mm(ONESC[:, :], XB2[:, sl_:sl_ + N], pSTAT2[0:1, 0:N],
                    cch == 0, cch == NCH - 1,
                    waits=(t_sq, stat_free[1]) if cch == 0 else (t_sq,),
                    inc="eT")
            m2s[cch] = m2
        t_mean = vop(lambda e, N=N: e.tensor_scalar(
            R_SM[0:1, 0:N], pSTAT[0:1, 0:N], 1.0 / D, None, AX.mult), waits=(m1,))
        t_ss = vop(lambda e, N=N: e.tensor_scalar(
            R_SS[0:1, 0:N], pSTAT2[0:1, 0:N], 1.0 / D, None, AX.mult), waits=(m2,))
        stat_free[0] = t_mean
        stat_free[1] = t_ss
        vop(lambda e, N=N: e.tensor_mul(R_T[0:1, 0:N], R_SM[0:1, 0:N], R_SM[0:1, 0:N]))
        t_var = vop(lambda e, N=N: e.tensor_sub(R_SS[0:1, 0:N], R_SS[0:1, 0:N],
                                                R_T[0:1, 0:N]))
        t_ln = sop(lambda e, N=N: e.activation(R_A[0:1, 0:N], R_SS[0:1, 0:N],
                                               ACT.Ln, bias=EPSC[0:1, 0:1]),
                   waits=(t_var,))
        t_a = sop(lambda e, N=N: e.activation(R_A[0:1, 0:N], R_A[0:1, 0:N],
                                              ACT.Exp, scale=-0.5), waits=(t_ln,))
        t_bm = vop(lambda e, N=N: e.tensor_mul(R_B[0:1, 0:N], R_SM[0:1, 0:N],
                                               R_A[0:1, 0:N]), waits=(t_a,))
        t_bn = vop(lambda e, N=N: e.tensor_scalar(R_B[0:1, 0:N], R_B[0:1, 0:N],
                                                  -1.0, None, AX.mult))
        if DBG and not dbg["flag"][0]:
            dbg["flag"][0] = 1
            dma("CS", dbg["ra"], R_A[0:1, :], waits=(t_a,))
            dma("CS", dbg["rb"], R_B[0:1, :], waits=(t_bn,))
            dma("CS", dbg["sm"], R_SM[0:1, :], waits=(t_mean,))
            dma("CS", dbg["ssr"], R_SS[0:1, :], waits=(t_var,))
        tA = bcast_row(R_A[0:1, 0:N], BCA, N, [t_a])
        tB = bcast_row(R_B[0:1, 0:N], BCB, N, [t_bn])
        done = None
        for cch in range(NCH):
            xc = X[:, cch * 1024 + x_tok0: cch * 1024 + x_tok0 + N]
            hs = H[:, cch * 512: cch * 512 + N]
            vop(lambda e, xc=xc, N=N: e.tensor_mul(TMP[:, 0:N], xc, BCA[:, 0:N]),
                waits=(tA, tB) if cch == 0 else ())
            vop(lambda e, N=N: e.tensor_add(TMP[:, 0:N], TMP[:, 0:N], BCB[:, 0:N]))
            done = vop(lambda e, h=hs, cch=cch, N=N, base=base: e.tensor_scalar(
                h, TMP[:, 0:N], MODV[:, base + 16 + cch:base + 17 + cch],
                MODV[:, base + cch:base + 1 + cch], AX.mult, AX.add),
                waits=(t_modv,) if cch == 0 else ())
        return done

    def wload(src_slice_ap, waits=(), queue=None):
        b = wb_i[0]
        wb_i[0] = 1 - b
        tok = dma("WB%d" % b, WB[b][:, :].rearrange("p (kc m) -> p kc m", m=256),
                  src_slice_ap.rearrange("(kc p) m -> p kc m", p=128),
                  waits=tuple(waits) + (wb_free[b],), queue=queue)
        return b, tok

    for dst, src in [(IDB, idb_d), (IDF, idf_d), (ONESC, onescol_d),
                     (ONESR, onesrow_d), (ONE11, one11_d), (PROT1, prot1_d),
                     (H128, h128_d), (BD64, bd64_d)]:
        tok_consts = dma("CS", dst[:, :], src)
    t_x = dma("XS", X[:, :].rearrange("p (c t) -> p c t", t=1024),
              xT_d.rearrange("(c p) t -> p c t", p=128))
    res_tok[0] = t_x
    vop(lambda e: e.memset(EPSC[0:1, 0:1], 1e-5))
    vop(lambda e: e.memset(EPSC[0:1, 1:2], 1e-6))

    scope0 = contextlib.ExitStack()
    sb0 = lambda n, s, d: scope0.enter_context(nc.sbuf_tensor(n, s, d))
    NWROW = sb0("NWROWx", [1, 2048], F32)
    CSL = sb0("CSLx", [4, 512], F32)
    CSLB = sb0("CSLBx", [4, 512], BF16)
    SILC = sb0("SILCx", [128, 16], BF16)
    MP = sb0("MPx", [128, 768], F32)
    OH = sb0("OHx", [128, 768], F32)
    QSL = sb0("QSLx", [128, 8192], F32)
    QBW = sb0("QBWx", [128, 8192], BF16)
    STG = sb0("STGx", [1, 8], F32)

    t_nwr = dma("CS", NWROW[:, :], normw_d)
    nw_last = None
    for cch in range(NCH):
        nw_last = mm(NWROW[0:1, cch * 128:(cch + 1) * 128], ONE11[:, :],
                     pSM[0:128, cch:cch + 1], True, True,
                     waits=(t_nwr, tok_consts, pSM_free[0]) if cch == 0 else (),
                     inc="eT" if cch == NCH - 1 else None)
    t_nw = vop(lambda e: e.tensor_copy(NORMW[:, :], pSM[0:128, 0:16]),
               waits=(nw_last,))
    pSM_free[0] = t_nw

    t_c0 = dma("CS", CSL[:, 0:256], c0s_d)
    t_c1 = dma("CS", CSL[:, 256:512], c1s_d)
    t_sil = sop(lambda e: e.activation(CSL[:, :], CSL[:, :], ACT.Silu),
                waits=(t_c0, t_c1))
    t_silb = sop(lambda e: e.copy(CSLB[:, :], CSL[:, :]))
    tp_last = None
    for l in range(2):
        for kc in range(2):
            tp_last = tpose(CSLB[0:4, l * 256 + kc * 128: l * 256 + (kc + 1) * 128],
                            IDB[0:4, 0:4],
                            pBF[:, (l * 2 + kc) * 4:(l * 2 + kc) * 4 + 4],
                            waits=(t_silb, tok_consts, pBF_free[0]))
    t_silc = vop(lambda e: e.tensor_copy(SILC[:, :], pBF[0:128, 0:16]),
                 waits=(tp_last,))
    pBF_free[0] = t_silc

    for l in range(2):
        for sl in range(6):
            wsl = wmod_sl_d[l][:, sl * 2048:(sl + 1) * 2048]
            b = wb_i[0]
            wb_i[0] = 1 - b
            tok_w = dma("WB%d" % b,
                        WB[b][:, :].rearrange("p (kc m) -> p kc m", m=2048),
                        wsl.rearrange("(kc p) m -> p kc m", p=128),
                        waits=(wb_free[b],))
            last_mm = None
            for ml in range(16):
                mc = sl * 16 + ml
                bk = mc % 2
                mm(WB[b][:, ml * 128: (ml + 1) * 128], SILC[0:128, l * 8: l * 8 + 4],
                   pbank[bk][0:128, 0:4], True, False,
                   waits=(tok_w, t_silc, banks[bk]))
                last_mm = mm(WB[b][:, 2048 + ml * 128: 2048 + (ml + 1) * 128],
                             SILC[0:128, l * 8 + 4: l * 8 + 8],
                             pbank[bk][0:128, 0:4], False, True, inc="eT")
                ev = vop(lambda e, mc=mc, l=l, bk=bk: e.tensor_copy(
                    MP[:, l * 384 + mc * 4: l * 384 + mc * 4 + 4],
                    pbank[bk][0:128, 0:4]), waits=(last_mm,))
                banks[bk] = ev
                if ml == 15:
                    wb_free[b] = last_mm
    t_mpd = dma("MB", modb_i.ap(), MP[:, :], waits=(banks[0], banks[1]))
    t_ar_mod = C.push("gpsimd", lambda e: e.collective_compute(
        "AllReduce", AX.add, replica_groups=[list(range(8))],
        ins=[modb_i.ap()], outs=[modb_o.ap()]), waits=(t_mpd,), inc="cC")
    t_oh = dma("CS", OH[:, :], oh_d)
    t_mpb = dma("MB", MP[:, :], modb_o.ap(), waits=(t_ar_mod,))
    vop(lambda e: e.tensor_mul(MP[:, :], MP[:, :], OH[:, :]), waits=(t_mpb, t_oh))
    t_sel = vop(lambda e: e.tensor_reduce(
        MODV[:, :], MP[:, :].rearrange("p (m r) -> p m r", r=4), XL, AX.add))
    if DBG:
        dma("CS", dbg["mp"], MP[:, :], waits=(t_sel,))
    t_modv = t_sel
    prev_fix = t_sel
    for l in range(2):
        base = l * 96
        for gofs in (32, 80):
            cols = slice(base + gofs, base + gofs + 16)
            vop(lambda e, c=cols: e.tensor_scalar(
                TMP[:, 0:16], MODV[:, c], 0.0, None, AX.is_ge), waits=(t_sel,))
            tb = vop(lambda e: e.tensor_scalar(
                TMP[:, 0:16], TMP[:, 0:16], 2.0, -1.0, AX.mult, AX.add))
            tc2 = sop(lambda e, c=cols: e.activation(
                TMP2[:, 0:16], MODV[:, c], ACT.Abs), waits=(t_sel, prev_fix))
            td = vop(lambda e: e.tensor_scalar(
                TMP2[:, 0:16], TMP2[:, 0:16], 0.25, None, AX.max), waits=(tc2,))
            t_modv = vop(lambda e, c=cols: e.tensor_mul(
                MODV[:, c], TMP[:, 0:16], TMP2[:, 0:16]), waits=(tb, td))
            prev_fix = t_modv
        for cofs in (16, 64):
            cols = slice(base + cofs, base + cofs + 16)
            t_modv = vop(lambda e, c=cols: e.tensor_scalar(
                MODV[:, c], MODV[:, c], 1.0, None, AX.add), waits=(t_sel,))

    prev = None
    for mi, (name, I, O) in enumerate(QMATS):
        na = I // 8 // 128
        for hf in range(2):
            tok_l = dma("QS", QSL[:, :].rearrange("p (a o) -> p a o", o=O),
                        qsl_d[name].rearrange("(a p) o -> p a o", p=128)
                        [:, hf * na // 2:(hf + 1) * na // 2, :], waits=(prev,))
            prev = vop(lambda e, hf=hf: e.reduce_sum(
                TMP3[:, hf + 1:hf + 2], QSL[:, :], XL, apply_absolute_value=True),
                waits=(tok_l,))
        t_add = vop(lambda e: e.tensor_add(TMP3[:, 1:2], TMP3[:, 1:2], TMP3[:, 2:3]))
        tt = tpose(TMP3[:, 1:2], IDF[:, :], pSM[0:1, 0:128],
                   waits=(t_add, pSM_free[0]))
        ts_ = vop(lambda e, mi=mi: e.reduce_sum(
            STG[0:1, mi:mi + 1], pSM[0:1, 0:128], XL), waits=(tt,))
        pSM_free[0] = ts_
        prev = ts_
    if DBG:
        dma("CS", dbg["stg"], STG[0:1, :], waits=(prev,))
    t_absd = dma("MB", absb_i.ap(), STG[0:1, :], waits=(prev,))
    t_ar_abs = C.push("gpsimd", lambda e: e.collective_compute(
        "AllReduce", AX.add, replica_groups=[list(range(8))],
        ins=[absb_i.ap()], outs=[absb_o.ap()]), waits=(t_absd,), inc="cC")
    t_absb = dma("MB", R_CW[0:1, 0:8], absb_o.ap(), waits=(t_ar_abs,))
    vop(lambda e: e.tensor_scalar(R_CW[0:1, 0:8], R_CW[0:1, 0:8],
                                  1.0 / NELEM_Q, None, AX.mult), waits=(t_absb,))
    vop(lambda e: e.tensor_scalar(R_CW[0:1, 0:8], R_CW[0:1, 0:8],
                                  1e-5, None, AX.max))
    t_sq_r = vop(lambda e: e.reciprocal(R_SQ[0:1, 0:8], R_CW[0:1, 0:8]))
    vop(lambda e: e.tensor_scalar(R_CWQ[0:1, 0:2], R_CW[0:1, 0:2],
                                  1.0 / 127.0, None, AX.mult))
    t_cwq = vop(lambda e: e.tensor_scalar(R_CWQ[0:1, 2:5], R_CW[0:1, 2:5],
                                          1.0 / 128.0, None, AX.mult))
    if DBG:
        dma("CS", dbg["modv"], MODV[:, :], waits=(t_modv,))
        dma("CS", dbg["cw"], R_CW[0:1, :], waits=(t_cwq,))
        dma("CS", dbg["cwq"], R_CWQ[0:1, :], waits=(t_cwq,))
    ag_toks = {}
    last_q = None
    for mi, (name, I, O) in enumerate(QMATS):
        tb = mm(ONESR[:, :], R_SQ[0:1, mi:mi + 1], pSM[0:128, 0:1], True, True,
                waits=(t_sq_r, pSM_free[0]), inc="eT")
        tsc = vop(lambda e: e.tensor_copy(SCL[:, 0:1], pSM[0:128, 0:1]), waits=(tb,))
        pSM_free[0] = tsc
        last = None
        na = I // 8 // 128
        for hf in range(2):
            tok_l = dma("QS", QSL[:, :].rearrange("p (a o) -> p a o", o=O),
                        qsl_d[name].rearrange("(a p) o -> p a o", p=128)
                        [:, hf * na // 2:(hf + 1) * na // 2, :],
                        waits=(last_q, last))
            vop(lambda e: e.tensor_scalar(QSL[:, :], QSL[:, :], SCL[:, 0:1],
                                          None, AX.mult), waits=(tok_l, tsc))
            vop(lambda e: e.tensor_scalar(QSL[:, :], QSL[:, :], MAGIC, -MAGIC,
                                          AX.add, AX.add))
            t3 = vop(lambda e: e.tensor_scalar(QBW[:, :], QSL[:, :], -1.0, 1.0,
                                               AX.max, AX.min))
            last = dma("WQ", wq_i[name].ap().rearrange("(a p) o -> p a o", p=128)
                       [:, hf * na // 2:(hf + 1) * na // 2, :],
                       QBW[:, :].rearrange("p (a o) -> p a o", o=O),
                       waits=(t3,))
            last_q = t3
        ag_toks[name] = C.push("gpsimd", lambda e, n=name: e.collective_compute(
            "AllGather", AX.bypass, replica_groups=[list(range(8))],
            ins=[wq_i[n].ap()], outs=[wq_o[n].ap()]), waits=(scur("WQ"),), inc="cC")
    scope0.close()
    C.barrier()

    scopeA = contextlib.ExitStack()
    sbA = lambda n, s, d: scopeA.enter_context(nc.sbuf_tensor(n, s, d))
    L0COS = sbA("L0COSx", [128, 1024], BF16)
    L0SIN = sbA("L0SINx", [128, 1024], BF16)
    PROT0 = sbA("PROT0x", [128, 128], BF16)
    L0MASK = sbA("L0MASKx", [128, 2048], BF16)
    PMKT = sbA("PMKTx", [128, 256], BF16)
    PMV = sbA("PMVx", [16, 2048], BF16)
    QT = sbA("QTx", [128, 2 * 512], BF16)
    KT = sbA("KTx", [128, 2 * 512], BF16)
    VV = sbA("VVx", [128, 4 * 2048], BF16)
    PP = sbA("PPx", [128, 4 * 512], BF16)
    PPP = sbA("PPPx", [16, 512], BF16)
    CTX = sbA("CTXx", [128, 16 * 512], BF16)
    DNB = sbA("DNBx", [128, 512], F32)
    dma("CS", L0COS[:, :], l0cos_d)
    dma("CS", L0SIN[:, :], l0sin_d)
    tok_prot0 = dma("CS", PROT0[:, :], prot0_d)
    t_msk0 = dma("CS", L0MASK[:, :], l0mask_d)
    tok_ropetab = t_msk0
    t_pmk = dma("CS", PMKT[:, :], pmkt_d, queue="gpsimd")
    t_pmv = dma("CS", PMV[:, :], pmv_d, queue="gpsimd")
    SC0 = float(1.0 / np.sqrt(128.0))

    def rope_apply(dst, src_raw_bf, rot_psum_bank, costab, sintab, tok_mm):
        vop(lambda e: e.tensor_mul(TMP[:, 0:512], src_raw_bf, costab),
            waits=(tok_ropetab,))
        vop(lambda e: e.tensor_mul(TMP2[:, 0:512], rot_psum_bank[0:128, 0:512],
                                   sintab), waits=(tok_mm,))
        return vop(lambda e, d=dst: e.tensor_add(d, TMP[:, 0:512], TMP2[:, 0:512]))

    for w in range(2):
        tok0 = w * 512
        wsync("vector", [("eT", C.cnt["eT"])])
        t_h = ln_modulated(0, 0, tok0, 512, t_modv)
        m2 = None
        m2s = [None] * NCH
        for cch in range(NCH):
            sl_ = (cch % 2) * 512
            hs = H[:, cch * 512:(cch + 1) * 512]
            t_sq = vop(lambda e, h=hs, sl_=sl_: e.tensor_mul(
                XB2[:, sl_:sl_ + 512], h, h),
                waits=((t_h,) if cch == 0 else ()) +
                ((m2s[cch - 2],) if cch >= 2 else ()))
            m2 = mm(ONESC[:, :], XB2[:, sl_:sl_ + 512], pSTAT[0:1, 0:512],
                    cch == 0, cch == NCH - 1,
                    waits=(t_sq, stat_free[0]) if cch == 0 else (t_sq,),
                    inc="eT")
            m2s[cch] = m2
        t_ss = vop(lambda e: e.tensor_scalar(R_SS[0:1, 0:512], pSTAT[0:1, 0:512],
                                             1.0 / D, None, AX.mult), waits=(m2,))
        stat_free[0] = t_ss
        sop(lambda e: e.activation(R_A[0:1, 0:512], R_SS[0:1, 0:512], ACT.Ln,
                                   bias=EPSC[0:1, 1:2]), waits=(t_ss,))
        t_a = sop(lambda e: e.activation(R_A[0:1, 0:512], R_A[0:1, 0:512],
                                         ACT.Exp, scale=-0.5))
        if DBG and w == 0:
            dma("CS", dbg["h"], H[:, :], waits=(t_h,))
        tA = bcast_row(R_A[0:1, 0:512], BCA, 512, [t_a])
        t_xn = None
        for cch in range(NCH):
            hs = H[:, cch * 512:(cch + 1) * 512]
            vop(lambda e, h=hs: e.tensor_mul(h, h, BCA[:, 0:512]),
                waits=(tA, scur("CS")) if cch == 0 else ())
            t_xn = vop(lambda e, h=hs, cch=cch: e.tensor_scalar(
                h, h, NORMW[:, cch:cch + 1], None, AX.mult),
                waits=(t_nw,) if cch == 0 else ())
        if DBG and w == 0:
            dma("CS", dbg["xn"], H[:, :], waits=(t_xn,))
        for sl in range(8):
            b, tok_w = wload(wqkvT_d[:, 4096 + sl * 256: 4096 + (sl + 1) * 256])
            lastm = None
            for ts4 in range(4):
                bk = ts4 % 2
                for kc in range(NCH):
                    lastm = mm(H[:, kc * 512 + ts4 * 128: kc * 512 + (ts4 + 1) * 128],
                               WB[b][:, kc * 256:(kc + 1) * 256],
                               pbank[bk][0:128, 0:256], kc == 0, kc == NCH - 1,
                               waits=(tok_w, t_xn, banks[bk]) if kc == 0 else (),
                               inc="eT" if kc == NCH - 1 else None)
                ev = vop(lambda e, ts4=ts4, sl=sl, bk=bk: e.tensor_copy(
                    VV[:, ts4 * 2048 + sl * 256: ts4 * 2048 + (sl + 1) * 256],
                    pbank[bk][0:128, 0:256]), waits=(lastm,))
                banks[bk] = ev
            wb_free[b] = lastm
        tok_vv = ("eV", C.cnt["eV"])
        for hp in range(8):
            heads = (2 * hp, 2 * hp + 1)
            b_q, tok_wq = wload(wqkvT_d[:, hp * 256:(hp + 1) * 256])
            b_k, tok_wk = wload(wqkvT_d[:, 2048 + hp * 256: 2048 + (hp + 1) * 256])
            for which, (bw, tok_w, buf) in enumerate(
                    [(b_q, tok_wq, QT), (b_k, tok_wk, KT)]):
                for ml in range(2):
                    bk = 2 + ml
                    lastm = None
                    for kc in range(NCH):
                        lastm = mm(WB[bw][:, kc * 256 + ml * 128:
                                          kc * 256 + (ml + 1) * 128],
                                   H[:, kc * 512:(kc + 1) * 512],
                                   pbank[bk][0:128, 0:512], kc == 0, kc == NCH - 1,
                                   waits=(tok_w, t_xn, banks[bk]) if kc == 0 else (),
                                   inc="eT" if kc == NCH - 1 else None)
                    t_raw = vop(lambda e, bk=bk, ml=ml: e.tensor_copy(
                        XB[:, ml * 512:(ml + 1) * 512], pbank[bk][0:128, 0:512]),
                        waits=(lastm,))
                    banks[bk] = t_raw
                    t_rot = mm(PROT0[:, :], XB[:, ml * 512:(ml + 1) * 512],
                               pbank[bk][0:128, 0:512], True, True,
                               waits=(t_raw, tok_prot0), inc="eT")
                    tr = rope_apply(buf[:, ml * 512:(ml + 1) * 512],
                                    XB[:, ml * 512:(ml + 1) * 512],
                                    pbank[bk],
                                    L0COS[:, tok0:tok0 + 512],
                                    L0SIN[:, tok0:tok0 + 512], t_rot)
                    banks[bk] = tr
            wb_free[b_q] = ("eT", C.cnt["eT"])
            wb_free[b_k] = ("eT", C.cnt["eT"])
            tok_rope = ("eV", C.cnt["eV"])
            for hh_i, hh in enumerate(heads):
                qtile = QT[:, hh_i * 512:(hh_i + 1) * 512]
                ktile = KT[:, hh_i * 512:(hh_i + 1) * 512]
                sc_toks = []
                for kc in range(4):
                    tkk = mm(ktile[:, kc * 128:(kc + 1) * 128], qtile,
                             pbank[kc][0:128, 0:512], True, True,
                             waits=(banks[kc], tok_rope), inc="eT")
                    sc_toks.append(tkk)
                tpp = mm(PMKT[:, hh * 16:(hh + 1) * 16], qtile,
                         pSTAT2[0:16, 0:512], True, True,
                         waits=(t_pmk, stat_free[1], tok_rope), inc="eT")
                ev_toks = []
                for kc in range(4):
                    ev = vop(lambda e, kc=kc: e.tensor_tensor(
                        PP[:, kc * 512:(kc + 1) * 512], pbank[kc][0:128, 0:512],
                        L0MASK[:, kc * 512:(kc + 1) * 512], AX.add),
                        waits=(sc_toks[kc], t_msk0))
                    banks[kc] = ev
                    ev_toks.append(ev)
                evp = vop(lambda e: e.tensor_copy(PPP[0:16, 0:512],
                                                  pSTAT2[0:16, 0:512]), waits=(tpp,))
                stat_free[1] = evp
                for kc in range(4):
                    vop(lambda e, kc=kc: e.reduce_max(
                        MCOL[:, kc:kc + 1], PP[:, kc * 512:(kc + 1) * 512], XL),
                        waits=(ev_toks[kc],))
                vop(lambda e: e.reduce_max(MCOL[0:16, 4:5], PPP[0:16, 0:512], XL),
                    waits=(evp,))
                vop(lambda e: e.tensor_max(MCOL[:, 5:6], MCOL[:, 0:1], MCOL[:, 1:2]))
                vop(lambda e: e.tensor_max(MCOL[:, 6:7], MCOL[:, 2:3], MCOL[:, 3:4]))
                vop(lambda e: e.tensor_max(MCOL[:, 7:8], MCOL[:, 5:6], MCOL[:, 6:7]))
                t_mx = vop(lambda e: e.tensor_max(MCOL[0:16, 7:8], MCOL[0:16, 7:8],
                                                  MCOL[0:16, 4:5]))
                t_nb = negbias_from_col(MCOL[:, 7:8], SC0, 8, [t_mx])
                ex_toks = []
                for kc in range(4):
                    ex = sop(lambda e, kc=kc: e.activation(
                        PP[:, kc * 512:(kc + 1) * 512],
                        PP[:, kc * 512:(kc + 1) * 512],
                        ACT.Exp, bias=MCOL[:, 8:9], scale=SC0), waits=(t_nb,))
                    ex_toks.append(ex)
                exp_p = sop(lambda e: e.activation(
                    PPP[0:16, 0:512], PPP[0:16, 0:512], ACT.Exp,
                    bias=MCOL[0:16, 8:9], scale=SC0), waits=(t_nb,))
                dn = None
                for kc in range(4):
                    dn = mm(ONESC[:, :], PP[:, kc * 512:(kc + 1) * 512],
                            pSTAT[0:1, 0:512], kc == 0, False,
                            waits=(ex_toks[kc], stat_free[0] if kc == 0 else None))
                dn = mm(ONESC[0:16, :], PPP[0:16, 0:512], pSTAT[0:1, 0:512],
                        False, True, waits=(exp_p,), inc="eT")
                t_rc = vop(lambda e: e.reciprocal(R_T[0:1, 0:512],
                                                  pSTAT[0:1, 0:512]), waits=(dn,))
                stat_free[0] = t_rc
                t_dnb = bcast_row(R_T[0:1, 0:512], DNB, 512, [t_rc])
                av = None
                for kc in range(4):
                    av = mm(VV[:, kc * 2048 + hh * 128: kc * 2048 + (hh + 1) * 128],
                            PP[:, kc * 512:(kc + 1) * 512],
                            pbank[0][0:128, 0:512], kc == 0, False,
                            waits=(ex_toks[kc], banks[0], tok_vv) if kc == 0
                            else (ex_toks[kc],))
                av = mm(PMV[0:16, hh * 128:(hh + 1) * 128], PPP[0:16, 0:512],
                        pbank[0][0:128, 0:512], False, True,
                        waits=(exp_p, t_pmv), inc="eT")
                t_ctx = vop(lambda e, hh=hh: e.tensor_mul(
                    CTX[:, hh * 512:(hh + 1) * 512], pbank[0][0:128, 0:512],
                    DNB[:, 0:512]), waits=(av, t_dnb))
                banks[0] = t_ctx
        if DBG and w == 0:
            dma("CS", dbg["ctx"], CTX[:, :], waits=(("eV", C.cnt["eV"]),))
        tok_ctx = ("eV", C.cnt["eV"])
        for sl in range(8):
            b, tok_w = wload(woutT_d[:, sl * 256:(sl + 1) * 256])
            for ml in range(2):
                oc = sl * 2 + ml
                bk = ml
                lastm = None
                for kc in range(NCH):
                    lastm = mm(WB[b][:, kc * 256 + ml * 128:
                                     kc * 256 + (ml + 1) * 128],
                               CTX[:, kc * 512:(kc + 1) * 512],
                               pbank[bk][0:128, 0:512], kc == 0, kc == NCH - 1,
                               waits=(tok_w, banks[bk], tok_ctx) if kc == 0 else (),
                               inc="eT" if kc == NCH - 1 else None)
                ev = vop(lambda e, oc=oc, bk=bk, tok0=tok0: e.scalar_tensor_tensor(
                    X[:, oc * 1024 + tok0: oc * 1024 + tok0 + 512],
                    pbank[bk][0:128, 0:512], MODV[:, 32 + oc:33 + oc],
                    X[:, oc * 1024 + tok0: oc * 1024 + tok0 + 512],
                    AX.mult, AX.add), waits=(lastm, t_modv))
                banks[bk] = ev
            wb_free[b] = lastm
        res_tok[0] = banks[1]
    if DBG:
        dma("CS", dbg["x1"], X[:, :], waits=(res_tok[0],))
    scopeA.close()
    C.barrier()

    scopeB = contextlib.ExitStack()
    sbB = lambda n, s, d: scopeB.enter_context(nc.sbuf_tensor(n, s, d))
    XQ = sbB("XQx", [128, 16 * 256], BF16)
    MID = sbB("MIDx", [128, 64 * 256], BF16)

    def quant_rows_rms(N, cw_col, waits=(), sqd=SQD):
        sop(lambda e, N=N: e.activation(R_A[0:1, 0:N], R_SS[0:1, 0:N], ACT.Ln),
            waits=tuple(waits))
        t_exp = sop(lambda e, N=N: e.activation(R_A[0:1, 0:N], R_A[0:1, 0:N],
                                                ACT.Exp, scale=-0.5))
        vop(lambda e, N=N: e.tensor_scalar(R_A[0:1, 0:N], R_A[0:1, 0:N],
                                           1e12, None, AX.min), waits=(t_exp,))
        t_sr = vop(lambda e, N=N, sqd=sqd: e.tensor_scalar(
            R_A[0:1, 0:N], R_A[0:1, 0:N], sqd, None, AX.mult))
        vop(lambda e, N=N: e.tensor_mul(R_T[0:1, 0:N], R_A[0:1, 0:N],
                                        R_MX[0:1, 0:N]))
        vop(lambda e, N=N: e.tensor_scalar(R_T[0:1, 0:N], R_T[0:1, 0:N],
                                           1e-5, None, AX.max))
        vop(lambda e, N=N: e.reciprocal(R_B[0:1, 0:N], R_T[0:1, 0:N]))
        vop(lambda e, N=N: e.tensor_scalar(R_B[0:1, 0:N], R_B[0:1, 0:N],
                                           127.0, None, AX.mult))
        t_cb = vop(lambda e, N=N: e.tensor_mul(R_A[0:1, 0:N], R_A[0:1, 0:N],
                                               R_B[0:1, 0:N]))
        t_os = vop(lambda e, N=N, cw=cw_col: e.tensor_scalar(
            R_OS[0:1, 0:N], R_T[0:1, 0:N], R_CWQ[0:1, cw:cw + 1], None, AX.mult),
            waits=(t_cwq,))
        return t_cb, t_os

    if DBG:
        dma("CS", dbg["w1q"], wq_o["w1"].ap(), waits=(ag_toks["w1"],))
        dma("CS", dbg["w2q"], wq_o["w2"].ap(), waits=(ag_toks["w2"],))
    for g in range(4):
        tok0 = g * 256
        N = 256
        wsync("vector", [("eT", C.cnt["eT"])])
        t_h = ln_modulated(0, 48, tok0, N, t_modv)
        if DBG and g == 0:
            dma("CS", dbg["hffn"], H[:, :], waits=(t_h,))
        m2 = None
        m2s = [None] * NCH
        for cch in range(NCH):
            sl_ = (cch % 2) * 512
            hs = H[:, cch * 512: cch * 512 + N]
            t_sq = vop(lambda e, h=hs, sl_=sl_: e.tensor_mul(
                XB2[:, sl_:sl_ + N], h, h),
                waits=((t_h,) if cch == 0 else ()) +
                ((m2s[cch - 2],) if cch >= 2 else ()))
            m2 = mm(ONESC[:, :], XB2[:, sl_:sl_ + N], pSTAT[0:1, 0:N],
                    cch == 0, cch == NCH - 1,
                    waits=(t_sq, stat_free[0]) if cch == 0 else (t_sq,),
                    inc="eT")
            m2s[cch] = m2
        t_ssr = vop(lambda e: e.tensor_copy(R_SS[0:1, 0:N], pSTAT[0:1, 0:N]),
                    waits=(m2,))
        stat_free[0] = t_ssr
        t_amx = absmax_to_row([H[:, cch * 512: cch * 512 + N]
                               for cch in range(NCH)], N, [t_h])
        t_cb, t_os = quant_rows_rms(N, 0, waits=(t_ssr, t_amx))
        tA = bcast_row(R_A[0:1, 0:N], BCA, N, [t_cb])
        tq = None
        for cch in range(NCH):
            hs = H[:, cch * 512: cch * 512 + N]
            vop(lambda e, h=hs: e.tensor_mul(TMP[:, 0:N], h, BCA[:, 0:N]),
                waits=(tA,) if cch == 0 else ())
            tq = vop(lambda e, cch=cch: e.tensor_scalar(
                XQ[:, cch * 256: cch * 256 + N], TMP[:, 0:N], MAGIC, -MAGIC,
                AX.add, AX.add))
        tB = bcast_row(R_OS[0:1, 0:N], BCB, N, [t_os])
        if DBG and g == 0:
            dma("CS", dbg["q1ss"], R_SS[0:1, 0:256], waits=(t_ssr,))
            dma("CS", dbg["q1mx"], R_MX[0:1, 0:256], waits=(t_amx,))
            dma("CS", dbg["q1cb"], R_A[0:1, 0:256], waits=(t_cb,))
            dma("CS", dbg["q1os"], R_OS[0:1, 0:256], waits=(t_os,))
            dma("CS", dbg["xq"], XQ[:, :], waits=(tq,))
            wsync("vector", [scur("CS")])
            wsync("scalar", [scur("CS")])
        for sl in range(32):
            b, tok_w = wload(wq_o["w1"].ap()[:, sl * 256:(sl + 1) * 256],
                             waits=(ag_toks["w1"],), queue="sync")
            for ml in range(2):
                oc = sl * 2 + ml
                bk = ml
                lastm = None
                for kc in range(NCH):
                    lastm = mm(WB[b][:, kc * 256 + ml * 128:
                                     kc * 256 + (ml + 1) * 128],
                               XQ[:, kc * 256:(kc + 1) * 256],
                               pbank[bk][0:128, 0:N], kc == 0, kc == NCH - 1,
                               waits=(tok_w, tq, banks[bk]) if kc == 0 else (),
                               inc="eT" if kc == NCH - 1 else None)
                t1 = vop(lambda e, bk=bk: e.tensor_mul(
                    TMP2[:, 0:N], pbank[bk][0:128, 0:N], BCB[:, 0:N]),
                    waits=(lastm, tB))
                banks[bk] = t1
                sop(lambda e: e.activation(TMP2[:, 0:N], TMP2[:, 0:N], ACT.Gelu),
                    waits=(t1,))
                g2 = sop(lambda e, oc=oc: e.activation(
                    MID[:, oc * 256: oc * 256 + N], TMP2[:, 0:N], ACT.Gelu))
                vop(lambda e: e.tensor_copy(TMP3[:, 3:4], TMP3[:, 3:4]),
                    waits=(g2,))
            wb_free[b] = lastm
        tok_gelu = ("eS", C.cnt["eS"])
        if DBG and g == 0:
            dma("CS", dbg["mid"], MID[:, :], waits=(tok_gelu,))
            wsync("vector", [scur("CS")])
        m2 = None
        m2s = [None] * 64
        for cch in range(64):
            sl_ = (cch % 2) * 512
            ms = MID[:, cch * 256: cch * 256 + N]
            t_sq = vop(lambda e, h=ms, sl_=sl_: e.tensor_mul(
                XB2[:, sl_:sl_ + N], h, h),
                waits=((tok_gelu,) if cch == 0 else ()) +
                ((m2s[cch - 2],) if cch >= 2 else ()))
            m2 = mm(ONESC[:, :], XB2[:, sl_:sl_ + N], pSTAT[0:1, 0:N],
                    cch == 0, cch == 63,
                    waits=(t_sq, stat_free[0]) if cch == 0 else (t_sq,),
                    inc="eT")
            m2s[cch] = m2
        t_ssr = vop(lambda e: e.tensor_copy(R_SS[0:1, 0:N], pSTAT[0:1, 0:N]),
                    waits=(m2,))
        stat_free[0] = t_ssr
        t_amx2 = absmax_to_row([MID[:, cch * 256: cch * 256 + N]
                                for cch in range(64)], N, [tok_gelu])
        t_cb, t_os = quant_rows_rms(N, 1, waits=(t_ssr, t_amx2),
                                    sqd=float(np.sqrt(8192.0)))
        tA = bcast_row(R_A[0:1, 0:N], BCA, N, [t_cb])
        tq = None
        for cch in range(64):
            ms = MID[:, cch * 256: cch * 256 + N]
            vop(lambda e, h=ms: e.tensor_mul(TMP[:, 0:N], h, BCA[:, 0:N]),
                waits=(tA,) if cch == 0 else ())
            tq = vop(lambda e, h=ms: e.tensor_scalar(
                h, TMP[:, 0:N], MAGIC, -MAGIC, AX.add, AX.add))
        tB = bcast_row(R_OS[0:1, 0:N], BCB, N, [t_os])
        if DBG and g == 0:
            dma("CS", dbg["q2cb"], R_A[0:1, 0:256], waits=(t_cb,))
            dma("CS", dbg["q2os"], R_OS[0:1, 0:256], waits=(t_os,))
            dma("CS", dbg["mint"], MID[:, :], waits=(tq,))
            wsync("vector", [scur("CS")])
            wsync("scalar", [scur("CS")])
        for sl in range(8):
            lastml = [None, None]
            for jcg in range(4):
                b, tok_w = wload(wq_o["w2"].ap()
                                 [jcg * 2048:(jcg + 1) * 2048,
                                  sl * 256:(sl + 1) * 256],
                                 waits=(ag_toks["w2"],), queue="sync")
                for ml in range(2):
                    bk = ml
                    for kc in range(NCH):
                        jc = jcg * 16 + kc
                        lastml[ml] = mm(WB[b][:, kc * 256 + ml * 128:
                                              kc * 256 + (ml + 1) * 128],
                                        MID[:, jc * 256:(jc + 1) * 256],
                                        pbank[bk][0:128, 0:N],
                                        jcg == 0 and kc == 0,
                                        jcg == 3 and kc == NCH - 1,
                                        waits=(tok_w, tq, banks[bk])
                                        if (jcg == 0 and kc == 0) else (tok_w,)
                                        if kc == 0 else (),
                                        inc="eT" if kc == NCH - 1 else None)
                wb_free[b] = lastml[1]
            for ml in range(2):
                oc = sl * 2 + ml
                bk = ml
                t1 = vop(lambda e, bk=bk: e.tensor_mul(
                    TMP2[:, 0:N], pbank[bk][0:128, 0:N], BCB[:, 0:N]),
                    waits=(lastml[ml], tB))
                ev = vop(lambda e, oc=oc, tok0=tok0: e.scalar_tensor_tensor(
                    X[:, oc * 1024 + tok0: oc * 1024 + tok0 + N],
                    TMP2[:, 0:N], MODV[:, 80 + oc:81 + oc],
                    X[:, oc * 1024 + tok0: oc * 1024 + tok0 + N],
                    AX.mult, AX.add), waits=(t_modv,))
                banks[bk] = ev
        res_tok[0] = banks[1]
    if DBG:
        dma("CS", dbg["x2"], X[:, :], waits=(res_tok[0],))
    scopeB.close()
    C.barrier()

    for qt_own in range(2):
        wsync("vector", [("eT", C.cnt["eT"]), scur("HB")])
        t_h = ln_modulated(1, 0, qt_own * 512, 512, t_modv)
        dma("HB",
            hb_i.ap().rearrange("(c p) t -> p c t", p=128)
            [:, :, qt_own * 512:(qt_own + 1) * 512],
            H[:, :].rearrange("p (c t) -> p c t", t=512), waits=(t_h,))
    t_hag = C.push("gpsimd", lambda e: e.collective_compute(
        "AllGather", AX.bypass, replica_groups=[[0, 1], [2, 3], [4, 5], [6, 7]],
        ins=[hb_i.ap()], outs=[hb_o.ap()]),
        waits=(scur("HB"),), inc="cC")
    C.barrier()

    scopeC = contextlib.ExitStack()
    sbC = lambda n, s, d: scopeC.enter_context(nc.sbuf_tensor(n, s, d))
    L1COS = sbC("L1COSx", [128, 2048], BF16)
    L1SIN = sbC("L1SINx", [128, 2048], BF16)
    L1MASK = sbC("L1MASKx", [128, 2048], BF16)
    KT1 = sbC("KT1x", [128, 2 * 2048], BF16)
    VV1 = sbC("VV1x", [128, 16 * 260], BF16)
    QT1 = sbC("QT1x", [128, 8 * 512], BF16)
    PP1 = sbC("PP1x", [128, 16 * 512], BF16)
    C2Q = sbC("C2Qx", [128, 4 * 1024], BF16)
    dma("CS", L1COS[:, :], l1cos_d)
    tok_rt1 = dma("CS", L1SIN[:, :], l1sin_d)
    t_msk1 = dma("CS", L1MASK[:, :], l1mask_d)
    t_vones = vop(lambda e: e.memset(VV1[:, :], 1.0))
    SC1 = 0.125

    def rope512(dst, raw_bf, bank, cos_ap, sin_ap, tok_mm):
        vop(lambda e: e.tensor_mul(TMP[:, 0:512], raw_bf, cos_ap),
            waits=(tok_rt1,))
        vop(lambda e: e.tensor_mul(TMP2[:, 0:512], bank[0:128, 0:512], sin_ap),
            waits=(tok_mm,))
        return vop(lambda e, d=dst: e.tensor_add(d, TMP[:, 0:512], TMP2[:, 0:512]))

    ctx_dma_toks = []
    for qt in range(4):
        nkc = 4 * (qt + 1)
        rb = (qt // 2) * 2048
        c0c = (qt % 2) * 512
        t_hs = dma("HS", H[:, :].rearrange("p (c t) -> p c t", t=512),
                   hb_o.ap()[rb:rb + 2048, c0c:c0c + 512]
                   .rearrange("(c p) t -> p c t", p=128),
                   waits=(t_hag, ("eT", C.cnt["eT"])))
        b_k, tok_wk = wload(Wk_d, waits=())
        for oc in range(2):
            bk = 2
            lastm = None
            for kc in range(NCH):
                lastm = mm(WB[b_k][:, kc * 256 + oc * 128:
                                   kc * 256 + (oc + 1) * 128],
                           H[:, kc * 512:(kc + 1) * 512],
                           pbank[bk][0:128, 0:512], kc == 0, kc == NCH - 1,
                           waits=(tok_wk, t_hs, banks[bk]) if kc == 0 else (),
                           inc="eT" if kc == NCH - 1 else None)
            t_raw = vop(lambda e, bk=bk: e.tensor_copy(XB[:, 0:512],
                                                       pbank[bk][0:128, 0:512]),
                        waits=(lastm,))
            banks[bk] = t_raw
            t_rot = mm(PROT1[:, :], XB[:, 0:512], pbank[bk][0:128, 0:512],
                       True, True, waits=(t_raw,), inc="eT")
            tr = rope512(KT1[:, oc * 2048 + qt * 512: oc * 2048 + (qt + 1) * 512],
                         XB[:, 0:512], pbank[bk],
                         L1COS[:, qt * 512:(qt + 1) * 512],
                         L1SIN[:, qt * 512:(qt + 1) * 512], t_rot)
            banks[bk] = tr
        wb_free[b_k] = lastm
        b_v, tok_wv = wload(Wv_d, waits=())
        for ts4 in range(4):
            gkc = qt * 4 + ts4
            bk = 3
            lastm = None
            for kc in range(NCH):
                lastm = mm(H[:, kc * 512 + ts4 * 128: kc * 512 + (ts4 + 1) * 128],
                           WB[b_v][:, kc * 256:(kc + 1) * 256],
                           pbank[bk][0:128, 0:256], kc == 0, kc == NCH - 1,
                           waits=(tok_wv, t_hs, banks[bk]) if kc == 0 else (),
                           inc="eT" if kc == NCH - 1 else None)
            ev = None
            for lg in range(4):
                ev = vop(lambda e, gkc=gkc, lg=lg, bk=bk: e.tensor_copy(
                    VV1[:, gkc * 260 + lg * 65: gkc * 260 + lg * 65 + 64],
                    pbank[bk][0:128, lg * 64:(lg + 1) * 64]),
                    waits=(lastm, t_vones) if lg == 0 else ())
            banks[bk] = ev
        wb_free[b_v] = lastm
        for sl in range(4):
            b_q, tok_wq = wload(Wq_d[:, sl * 256:(sl + 1) * 256])
            for ml in range(2):
                pc = sl * 2 + ml
                bk = 2
                lastm = None
                for kc in range(NCH):
                    lastm = mm(WB[b_q][:, kc * 256 + ml * 128:
                                       kc * 256 + (ml + 1) * 128],
                               H[:, kc * 512:(kc + 1) * 512],
                               pbank[bk][0:128, 0:512], kc == 0, kc == NCH - 1,
                               waits=(tok_wq, t_hs, banks[bk]) if kc == 0 else (),
                               inc="eT" if kc == NCH - 1 else None)
                t_raw = vop(lambda e, bk=bk: e.tensor_copy(
                    XB[:, 0:512], pbank[bk][0:128, 0:512]), waits=(lastm,))
                banks[bk] = t_raw
                t_rot = mm(PROT1[:, :], XB[:, 0:512], pbank[bk][0:128, 0:512],
                           True, True, waits=(t_raw,), inc="eT")
                tr = rope512(QT1[:, pc * 512:(pc + 1) * 512], XB[:, 0:512],
                             pbank[bk], L1COS[:, qt * 512:(qt + 1) * 512],
                             L1SIN[:, qt * 512:(qt + 1) * 512], t_rot)
                banks[bk] = tr
            wb_free[b_q] = lastm
        tok_rope1 = ("eV", C.cnt["eV"])
        tok_c2free = scur("CX")
        for pc in range(8):
            for half in range(2):
                hloc = QPERM[2 * pc + half]
                lg = hloc // 4
                kchunk = lg // 2
                krow = (lg % 2) * 64
                qv = QT1[half * 64:half * 64 + 64, pc * 512:(pc + 1) * 512]
                ev_toks = []
                for kc in range(nkc):
                    bk = kc % 2
                    tkk = mm(KT1[krow:krow + 64,
                                 kchunk * 2048 + kc * 128:
                                 kchunk * 2048 + (kc + 1) * 128],
                             qv, pbank[bk][0:128, 0:512], True, True,
                             waits=(banks[bk], tok_rope1), inc="eT")
                    j = kc - 4 * qt
                    if 0 <= j < 4:
                        ev = vop(lambda e, kc=kc, j=j, bk=bk: e.tensor_tensor(
                            PP1[:, kc * 512:(kc + 1) * 512],
                            pbank[bk][0:128, 0:512],
                            L1MASK[:, j * 512:(j + 1) * 512], AX.add),
                            waits=(tkk, t_msk1))
                    else:
                        ev = vop(lambda e, kc=kc, bk=bk: e.tensor_copy(
                            PP1[:, kc * 512:(kc + 1) * 512],
                            pbank[bk][0:128, 0:512]), waits=(tkk,))
                    banks[bk] = ev
                    ev_toks.append(ev)
                for kc in range(nkc):
                    vop(lambda e, kc=kc: e.reduce_max(
                        MCOL[:, kc:kc + 1], PP1[:, kc * 512:(kc + 1) * 512], XL),
                        waits=(ev_toks[kc],))
                t_mx = vop(lambda e, nkc=nkc: e.tensor_reduce(
                    MCOL[:, 16:17], MCOL[:, 0:nkc], XL, AX.max))
                t_nb = negbias_from_col(MCOL[:, 16:17], SC1, 17, [t_mx])
                ex_toks = []
                for kc in range(nkc):
                    ex = sop(lambda e, kc=kc: e.activation(
                        PP1[:, kc * 512:(kc + 1) * 512],
                        PP1[:, kc * 512:(kc + 1) * 512], ACT.Exp,
                        bias=MCOL[:, 17:18], scale=SC1), waits=(t_nb,))
                    ex_toks.append(ex)
                for qsub in range(4):
                    av = None
                    for kc in range(nkc):
                        av = mm(PP1[:, kc * 512 + qsub * 128:
                                    kc * 512 + (qsub + 1) * 128],
                                VV1[:, kc * 260 + lg * 65: kc * 260 + lg * 65 + 65],
                                pbank[3][0:128, 0:65], kc == 0, kc == nkc - 1,
                                waits=(ex_toks[kc], banks[3], tok_rope1) if kc == 0
                                else (ex_toks[kc],),
                                inc="eT" if kc == nkc - 1 else None)
                    t_rc = vop(lambda e: e.reciprocal(DCOL[:, 0:1],
                                                      pbank[3][0:128, 64:65]),
                               waits=(av,))
                    t_ev = vop(lambda e, qsub=qsub, hloc=hloc: e.tensor_scalar(
                        C2Q[:, qsub * 1024 + hloc * 64:
                            qsub * 1024 + hloc * 64 + 64],
                        pbank[3][0:128, 0:64], DCOL[:, 0:1], None, AX.mult),
                        waits=(tok_c2free,) if (pc == 0 and half == 0 and
                                                qsub == 0) else ())
                    banks[3] = t_ev
        tok_c2 = dma("CX",
                     ctx2b.ap()[qt * 512:(qt + 1) * 512, :]
                     .rearrange("(qc p) hd -> p qc hd", p=128),
                     C2Q[:, :].rearrange("p (qc hd) -> p qc hd", hd=1024),
                     waits=(banks[3],))
        ctx_dma_toks.append(tok_c2)
    CTXR = PP1
    tok_pp1_free = ("eT", C.cnt["eT"])
    tok_pp1_s = ("eS", C.cnt["eS"])
    for ng in range(2):
        ld = None
        for qc in range(16):
            ld = dma("CXL", CTXR[:, qc * 512:(qc + 1) * 512],
                     ctx2b.ap()[qc * 128:(qc + 1) * 128,
                                ng * 512:(ng + 1) * 512],
                     waits=(scur("CX"), tok_pp1_free, tok_pp1_s,
                            ("eT", C.cnt["eT"]))
                     if qc == 0 else ())
        for sl in range(8):
            b, tok_w = wload(Wo_d[:, sl * 256:(sl + 1) * 256])
            for ml in range(2):
                oc = sl * 2 + ml
                bk = ml
                lastm = None
                for qc in range(16):
                    lastm = mm(WB[b][:, qc * 256 + ml * 128:
                                     qc * 256 + (ml + 1) * 128],
                               CTXR[:, qc * 512:(qc + 1) * 512],
                               pbank[bk][0:128, 0:512], qc == 0, qc == 15,
                               waits=(tok_w, scur("CXL"), banks[bk])
                               if qc == 0 else (),
                               inc="eT" if qc == 15 else None)
                ev = vop(lambda e, oc=oc, ng=ng, bk=bk: e.scalar_tensor_tensor(
                    X[:, oc * 1024 + ng * 512: oc * 1024 + (ng + 1) * 512],
                    pbank[bk][0:128, 0:512], MODV[:, 96 + 32 + oc:96 + 33 + oc],
                    X[:, oc * 1024 + ng * 512: oc * 1024 + (ng + 1) * 512],
                    AX.mult, AX.add), waits=(lastm, t_modv))
                banks[bk] = ev
            wb_free[b] = lastm
    res_tok[0] = banks[1]
    if DBG:
        dma("CS", dbg["x3"], X[:, :], waits=(res_tok[0],))
    scopeC.close()
    C.barrier()

    scopeD = contextlib.ExitStack()
    sbD = lambda n, s, d: scopeD.enter_context(nc.sbuf_tensor(n, s, d))
    XQ1 = sbD("XQ1x", [128, 16 * 256], BF16)
    INTER = sbD("INTERx", [128, 64 * 256], BF16)
    R1 = sbD("R1x", [128, 64 * 128], BF16)

    for g in range(4):
        tok0 = g * 256
        N = 256
        wsync("vector", [("eT", C.cnt["eT"])])
        t_h = ln_modulated(1, 48, tok0, N, t_modv)
        t_amx = absmax_to_row([H[:, cch * 512: cch * 512 + N]
                               for cch in range(NCH)], N, [t_h])
        vop(lambda e: e.tensor_scalar(R_T[0:1, 0:N], R_MX[0:1, 0:N],
                                      1e-5, None, AX.max), waits=(t_amx,))
        vop(lambda e: e.reciprocal(R_A[0:1, 0:N], R_T[0:1, 0:N]))
        t_cb = vop(lambda e: e.tensor_scalar(R_A[0:1, 0:N], R_A[0:1, 0:N],
                                             128.0, None, AX.mult))
        t_osg = vop(lambda e: e.tensor_scalar(R_OS[0:1, 0:N], R_T[0:1, 0:N],
                                              R_CWQ[0:1, 2:3], None, AX.mult),
                    waits=(t_cwq,))
        t_osu = vop(lambda e: e.tensor_scalar(R_SS[0:1, 0:N], R_T[0:1, 0:N],
                                              R_CWQ[0:1, 3:4], None, AX.mult))
        tA = bcast_row(R_A[0:1, 0:N], BCA, N, [t_cb])
        tq = None
        for cch in range(NCH):
            hs = H[:, cch * 512: cch * 512 + N]
            vop(lambda e, h=hs: e.tensor_mul(TMP[:, 0:N], h, BCA[:, 0:N]),
                waits=(tA,) if cch == 0 else ())
            vop(lambda e: e.tensor_scalar(TMP[:, 0:N], TMP[:, 0:N],
                                          MAGIC, -MAGIC, AX.add, AX.add))
            tq = vop(lambda e, cch=cch: e.tensor_scalar(
                XQ1[:, cch * 256: cch * 256 + N], TMP[:, 0:N],
                127.0, None, AX.min))
        tBg = bcast_row(R_OS[0:1, 0:N], BCB, N, [t_osg])
        tBu = bcast_row(R_SS[0:1, 0:N], BCA, N, [t_osu, tq])
        for sl in range(32):
            bg, tok_g = wload(wq_o["wg"].ap()[:, sl * 256:(sl + 1) * 256],
                              waits=(ag_toks["wg"],), queue="sync")
            bu, tok_u = wload(wq_o["wu"].ap()[:, sl * 256:(sl + 1) * 256],
                              waits=(ag_toks["wu"],), queue="sync")
            for ml in range(2):
                oc = sl * 2 + ml
                lg_ = None
                lu_ = None
                for kc in range(NCH):
                    lg_ = mm(WB[bg][:, kc * 256 + ml * 128:
                                    kc * 256 + (ml + 1) * 128],
                             XQ1[:, kc * 256:(kc + 1) * 256],
                             pbank[0][0:128, 0:N], kc == 0, kc == NCH - 1,
                             waits=(tok_g, tq, banks[0]) if kc == 0 else (),
                             inc="eT" if kc == NCH - 1 else None)
                for kc in range(NCH):
                    lu_ = mm(WB[bu][:, kc * 256 + ml * 128:
                                    kc * 256 + (ml + 1) * 128],
                             XQ1[:, kc * 256:(kc + 1) * 256],
                             pbank[1][0:128, 0:N], kc == 0, kc == NCH - 1,
                             waits=(tok_u, banks[1]) if kc == 0 else (),
                             inc="eT" if kc == NCH - 1 else None)
                t1 = vop(lambda e: e.tensor_mul(TMP[:, 0:N],
                                                pbank[0][0:128, 0:N], BCB[:, 0:N]),
                         waits=(lg_, tBg))
                banks[0] = t1
                t_sil2 = sop(lambda e: e.activation(TMP[:, 0:N], TMP[:, 0:N],
                                                    ACT.Silu), waits=(t1,))
                t2 = vop(lambda e: e.tensor_mul(TMP2[:, 0:N],
                                                pbank[1][0:128, 0:N], BCA[:, 0:N]),
                         waits=(lu_, tBu))
                banks[1] = t2
                t3 = vop(lambda e, oc=oc: e.tensor_mul(
                    INTER[:, oc * 256: oc * 256 + N], TMP[:, 0:N], TMP2[:, 0:N]),
                    waits=(t_sil2,))
            wb_free[bg] = lg_
            wb_free[bu] = lu_
        tok_inter = ("eV", C.cnt["eV"])
        for sub in range(2):
            st1 = None
            for a in range(64):
                bk = a % 2
                tm = mm(H128[:, :], INTER[:, a * 256 + sub * 128:
                                          a * 256 + (sub + 1) * 128],
                        pbank[bk][0:128, 0:128], True, True,
                        waits=(banks[bk], tok_inter) if a < 2 else (banks[bk],),
                        inc="eT")
                st1 = vop(lambda e, a=a, sub=sub, bk=bk: e.tensor_copy(
                    INTER[:, a * 256 + sub * 128: a * 256 + (sub + 1) * 128],
                    pbank[bk][0:128, 0:128]), waits=(tm,))
                banks[bk] = st1
            t_dump = dma("FW",
                         shufb.ap().rearrange("p (a t) -> p a t", t=128),
                         INTER[:, :].rearrange("p (a t2) -> p a t2", t2=256)
                         [:, :, sub * 128:(sub + 1) * 128], waits=(st1,))
            g_toks = []
            for hh in range(2):
                g_toks.append(dma(
                    "FW",
                    R1[hh * 64:(hh + 1) * 64, :]
                    .rearrange("a (c t) -> a c t", c=64, t=128),
                    shufb.ap().rearrange("(c two) (a t) -> two a c t",
                                         c=64, two=2, a=64, t=128)[hh],
                    waits=(t_dump,)))
            st2 = None
            for cc in range(64):
                bk = cc % 2
                tm = mm(BD64[:, :], R1[:, cc * 128:(cc + 1) * 128],
                        pbank[bk][0:128, 0:128], True, True,
                        waits=(banks[bk],) + tuple(g_toks) if cc < 2
                        else (banks[bk],), inc="eT")
                st2 = vop(lambda e, cc=cc, sub=sub, bk=bk: e.tensor_copy(
                    INTER[:, cc * 256 + sub * 128: cc * 256 + (sub + 1) * 128],
                    pbank[bk][0:128, 0:128]),
                    waits=(tm, t_dump) if cc == 0 else (tm,))
                banks[bk] = st2
        t_amx3 = absmax_to_row([INTER[:, cch * 256: cch * 256 + N]
                                for cch in range(64)], N, [st2])
        vop(lambda e: e.tensor_scalar(R_T[0:1, 0:N], R_MX[0:1, 0:N],
                                      LAM, None, AX.mult), waits=(t_amx3,))
        vop(lambda e: e.tensor_scalar(R_T[0:1, 0:N], R_T[0:1, 0:N],
                                      1e-5, None, AX.max))
        vop(lambda e: e.reciprocal(R_A[0:1, 0:N], R_T[0:1, 0:N]))
        t_cb = vop(lambda e: e.tensor_scalar(R_A[0:1, 0:N], R_A[0:1, 0:N],
                                             128.0 * LAM, None, AX.mult))
        t_osd = vop(lambda e: e.tensor_scalar(R_OS[0:1, 0:N], R_T[0:1, 0:N],
                                              R_CWQ[0:1, 4:5], None, AX.mult),
                    waits=(t_cwq,))
        tA = bcast_row(R_A[0:1, 0:N], BCA, N, [t_cb])
        tq = None
        for cch in range(64):
            ms = INTER[:, cch * 256: cch * 256 + N]
            vop(lambda e, h=ms: e.tensor_mul(TMP[:, 0:N], h, BCA[:, 0:N]),
                waits=(tA,) if cch == 0 else ())
            vop(lambda e: e.tensor_scalar(TMP[:, 0:N], TMP[:, 0:N],
                                          MAGIC, -MAGIC, AX.add, AX.add))
            tq = vop(lambda e, h=ms: e.tensor_scalar(
                h, TMP[:, 0:N], 127.0, None, AX.min))
        tB = bcast_row(R_OS[0:1, 0:N], BCB, N, [t_osd])
        for sl in range(8):
            lastml = [None, None]
            for jcg in range(4):
                b, tok_w = wload(wq_o["wd"].ap()
                                 [jcg * 2048:(jcg + 1) * 2048,
                                  sl * 256:(sl + 1) * 256],
                                 waits=(ag_toks["wd"],), queue="sync")
                for ml in range(2):
                    bk = ml
                    for kc in range(NCH):
                        jc = jcg * 16 + kc
                        lastml[ml] = mm(WB[b][:, kc * 256 + ml * 128:
                                              kc * 256 + (ml + 1) * 128],
                                        INTER[:, jc * 256:(jc + 1) * 256],
                                        pbank[bk][0:128, 0:N],
                                        jcg == 0 and kc == 0,
                                        jcg == 3 and kc == NCH - 1,
                                        waits=(tok_w, tq, banks[bk])
                                        if (jcg == 0 and kc == 0) else (tok_w,)
                                        if kc == 0 else (),
                                        inc="eT" if kc == NCH - 1 else None)
                wb_free[b] = lastml[1]
            for ml in range(2):
                oc = sl * 2 + ml
                bk = ml
                vop(lambda e, bk=bk: e.tensor_mul(
                    TMP2[:, 0:N], pbank[bk][0:128, 0:N], BCB[:, 0:N]),
                    waits=(lastml[ml], tB))
                ev = vop(lambda e, oc=oc, tok0=tok0: e.scalar_tensor_tensor(
                    X[:, oc * 1024 + tok0: oc * 1024 + tok0 + N],
                    TMP2[:, 0:N], MODV[:, 96 + 80 + oc:96 + 81 + oc],
                    X[:, oc * 1024 + tok0: oc * 1024 + tok0 + N],
                    AX.mult, AX.add), waits=(t_modv,))
                banks[bk] = ev
        res_tok[0] = banks[1]
    scopeD.close()
    C.barrier()

    scopeE = contextlib.ExitStack()
    OUTS = scopeE.enter_context(nc.sbuf_tensor("OUTSx", [128, 2048], F32))
    out_toks = []
    prev_store = None
    for tsub in range(8):
        cps = []
        for cch in range(NCH):
            bk = cch % 4
            tt = tpose(X[:, cch * 1024 + tsub * 128: cch * 1024 + (tsub + 1) * 128],
                       IDF[:, :], pbank[bk][0:128, 0:128],
                       waits=(res_tok[0], banks[bk], prev_store))
            cp = vop(lambda e, cch=cch, bk=bk: e.tensor_copy(
                OUTS[:, cch * 128:(cch + 1) * 128], pbank[bk][0:128, 0:128]),
                waits=(tt,))
            banks[bk] = cp
            cps.append(cp)
        tok_o = dma("OS", out_d[tsub * 128:(tsub + 1) * 128, :], OUTS[:, :],
                    waits=(cps[-1],))
        prev_store = tok_o
        out_toks.append(tok_o)
    semh = C.semh
    fin = ("OS", C.cnt["OS"])
    C.q["sync"].append(lambda e, fin=fin: e.wait_ge(semh[fin[0]], fin[1]))
    scopeE.close()

    with nc.Block() as block:
        @block.tensor
        def _(e):
            for fn in C.q["tensor"]:
                fn(e)

        @block.vector
        def _(e):
            for fn in C.q["vector"]:
                fn(e)

        @block.scalar
        def _(e):
            for fn in C.q["scalar"]:
                fn(e)

        @block.gpsimd
        def _(e):
            for fn in C.q["gpsimd"]:
                fn(e)

        @block.sync
        def _(e):
            for fn in C.q["sync"]:
                fn(e)
    st.close()
    return nc


_NC_CACHE = [None]


def kernel(**inputs):
    in_maps = _host_prep(inputs)
    if _NC_CACHE[0] is None:
        _NC_CACHE[0] = build_program()
    nc = _NC_CACHE[0]
    res = run_bass_kernel_spmd(nc, in_maps, core_ids=list(range(8)))
    if os.environ.get("MK_DEBUG") == "1":
        kernel._dbg = res.results
    out = np.zeros((4, 2048, 2048), np.float32)
    for c in range(8):
        pair, par = c // 2, c % 2
        out[pair, par * 1024:(par + 1) * 1024, :] = res.results[c]["out"]
    return out


# version 05
# speedup vs baseline: 1.0419x; 1.0419x over previous
import contextlib
import os
import numpy as np
import ml_dtypes
import concourse.bass as bass
import concourse.mybir as mybir
from concourse.bass_utils import run_bass_kernel_spmd

F32 = mybir.dt.float32
BF16 = mybir.dt.bfloat16
AX = mybir.AluOpType
ACT = mybir.ActivationFunctionType
XL = mybir.AxisListType.X

BF = ml_dtypes.bfloat16
MAGIC = float(np.float32(1.5 * 2 ** 23))
NEG = -1.0e9
D = 2048
NCH = 16
LAM = float(np.float32(1.0 / np.sqrt(8192.0)))
SQD = float(np.sqrt(2048.0))
QPERM = [0, 4, 1, 5, 2, 6, 3, 7, 8, 12, 9, 13, 10, 14, 11, 15]
QMATS = [("w1", 2048, 8192), ("w2", 8192, 2048), ("wg", 2048, 8192),
         ("wu", 2048, 8192), ("wd", 8192, 2048)]
NELEM_Q = float(2048 * 8192)


def _hadamard(n):
    H = np.array([[1.0]], np.float32)
    while H.shape[0] < n:
        H = np.block([[H, H], [H, -H]])
    return H.astype(np.float32)


def _host_prep(inputs):
    f = lambda k: np.ascontiguousarray(np.asarray(inputs[k], np.float32))
    x = f("L0_attn_x")
    xT = np.ascontiguousarray(x.transpose(0, 2, 1))
    w_qkvT = np.ascontiguousarray(f("L0_attn_w_qkv").T)
    w_outT = np.ascontiguousarray(f("L0_attn_w_out").T)
    norm_w = f("L0_attn_norm_w").reshape(1, 2048)
    pm = f("L0_attn_pm")
    w1T = np.ascontiguousarray(f("L0_ffn_w1").T)
    w2T = np.ascontiguousarray(f("L0_ffn_w2").T)
    gT = np.ascontiguousarray(f("L1_ffn_w_gate").T)
    uT = np.ascontiguousarray(f("L1_ffn_w_up").T)
    dT = np.ascontiguousarray(f("L1_ffn_w_down").T)
    cidx = np.arange(64)[:, None, None]
    hidx = np.arange(2)[None, :, None]
    aidx = np.arange(64)[None, None, :]
    jmap = (aidx * 128 + 2 * cidx + hidx).reshape(-1)
    dTP = np.ascontiguousarray(dT[jmap])
    wmod0 = f("L0_wmod"); wmod1 = f("L1_wmod")
    c0 = f("L0_c"); c1 = f("L1_c")
    Wq = f("L1_attn_Wq"); Wk = f("L1_attn_Wk")
    Wv = f("L1_attn_Wv"); Wo = f("L1_attn_Wo")

    idb = np.eye(128, dtype=np.float32).astype(BF)
    idf = np.eye(128, dtype=np.float32)
    onescol = np.ones((128, 1), np.float32).astype(BF)
    onesrow = np.ones((1, 128), np.float32)
    one11 = np.ones((1, 1), np.float32)
    inv0 = 1.0 / (10000.0 ** (np.arange(0, 128, 2, dtype=np.float32) / 128.0))
    fr = np.arange(2048, dtype=np.float32)[:, None] * inv0[None, :]
    fr = np.repeat(fr, 2, axis=-1)
    l0cos_all = np.cos(fr).T.astype(BF)
    l0sin_all = np.sin(fr).T.astype(BF)
    P0 = np.zeros((128, 128), np.float32)
    for i in range(64):
        P0[2 * i, 2 * i + 1] = -1.0
        P0[2 * i + 1, 2 * i] = 1.0
    prot0 = np.ascontiguousarray(P0.T).astype(BF)
    inv1 = 1.0 / (10000.0 ** (np.arange(32, dtype=np.float32) * 2.0 / 64.0))
    ang = np.arange(2048, dtype=np.float32)[:, None] * inv1[None, :]
    c1t = np.cos(ang).T; s1t = np.sin(ang).T
    l1cos = np.zeros((128, 2048), np.float32)
    l1sin = np.zeros((128, 2048), np.float32)
    for hh in range(2):
        for half in range(2):
            rows = slice(hh * 64 + half * 32, hh * 64 + half * 32 + 32)
            l1cos[rows] = c1t
            l1sin[rows] = s1t
    l1cos = l1cos.astype(BF); l1sin = l1sin.astype(BF)
    P1 = np.zeros((128, 128), np.float32)
    for hh in range(2):
        b0 = hh * 64
        for i in range(32):
            P1[b0 + i, b0 + 32 + i] = -1.0
            P1[b0 + 32 + i, b0 + i] = 1.0
    prot1 = np.ascontiguousarray(P1.T).astype(BF)
    mb = np.full((512, 528), NEG, np.float32)
    mb[:, :32] = 0.0
    qi = np.arange(496)[:, None]
    kj = np.arange(496)[None, :]
    mb[16:, 32:] = np.where(kj <= qi, np.float32(0), np.float32(NEG))
    mbT = mb.T
    l0mask = np.ascontiguousarray(np.concatenate(
        [mbT[16 + kc * 128:16 + (kc + 1) * 128, :] for kc in range(4)],
        axis=1)).astype(BF)
    kk = np.arange(128)[:, None]
    qq = np.arange(512)[None, :]
    l1mask = np.concatenate(
        [np.where(kk <= qq - 128 * j, np.float32(0), np.float32(NEG))
         for j in range(4)], axis=1).astype(BF)
    h128 = _hadamard(128).astype(BF)
    h64 = _hadamard(64)
    bd = np.zeros((128, 128), np.float32)
    bd[:64, :64] = h64
    bd[64:, 64:] = h64
    bd64 = bd.astype(BF)
    pmkt_sb = np.zeros((128, 256), np.float32)
    for h in range(16):
        pmkt_sb[:, h * 16:(h + 1) * 16] = pm[0][h].T
    pmv_sb = np.zeros((16, 2048), np.float32)
    for h in range(16):
        pmv_sb[:, h * 128:(h + 1) * 128] = pm[1][h]

    shared = dict(wqkvT=w_qkvT, woutT=w_outT, normw=norm_w, Wo=Wo,
                  idb=idb, idf=idf, onescol=onescol, onesrow=onesrow,
                  one11=one11, prot0=prot0, prot1=prot1, l1cos=l1cos,
                  l1sin=l1sin, l0mask=l0mask, l1mask=l1mask, h128=h128,
                  bd64=bd64, pmkt=pmkt_sb, pmv=pmv_sb)
    in_maps = []
    for c in range(8):
        pair, par = c // 2, c % 2
        m = dict(shared)
        m["xT"] = np.ascontiguousarray(xT[pair][:, par * 1024:(par + 1) * 1024])
        m["l0cos"] = np.ascontiguousarray(l0cos_all[:, par * 1024:(par + 1) * 1024])
        m["l0sin"] = np.ascontiguousarray(l0sin_all[:, par * 1024:(par + 1) * 1024])
        heads = [par * 16 + j for j in QPERM]
        m["Wq_own"] = np.ascontiguousarray(
            np.concatenate([Wq[:, h * 64:(h + 1) * 64] for h in heads], axis=1))
        m["Wk_own"] = np.ascontiguousarray(Wk[:, par * 256:(par + 1) * 256])
        m["Wv_own"] = np.ascontiguousarray(Wv[:, par * 256:(par + 1) * 256])
        for name, I, O in QMATS:
            src = {"w1": w1T, "w2": w2T, "wg": gT, "wu": uT, "wd": dTP}[name]
            rs = I // 8
            m[name + "_sl"] = np.ascontiguousarray(src[c * rs:(c + 1) * rs])
        m["wmod0_sl"] = np.ascontiguousarray(wmod0[c * 256:(c + 1) * 256])
        m["wmod1_sl"] = np.ascontiguousarray(wmod1[c * 256:(c + 1) * 256])
        oh = np.zeros((128, 768), np.float32)
        oh[:, pair::4] = 1.0
        m["ohmask"] = oh
        m["c0s"] = np.ascontiguousarray(c0[:, c * 256:(c + 1) * 256])
        m["c1s"] = np.ascontiguousarray(c1[:, c * 256:(c + 1) * 256])
        in_maps.append(m)
    return in_maps


class Ctx:
    def __init__(self):
        self.q = {e: [] for e in ("tensor", "vector", "scalar", "gpsimd", "sync")}
        self.semh = {}
        self.cnt = {}

    def push(self, eng, fn, waits=(), inc=None, incn=1):
        waits = tuple(w for w in waits if w is not None)
        if eng == "vector":
            if self.cnt.get("eV", 0) > 0:
                waits = waits + (("eV", self.cnt["eV"]),)
            if inc is None:
                inc = "eV"
        elif eng == "scalar":
            if self.cnt.get("eS", 0) > 0:
                waits = waits + (("eS", self.cnt["eS"]),)
            if inc is None:
                inc = "eS"
        tok = None
        if inc is not None:
            self.cnt[inc] += incn
            tok = (inc, self.cnt[inc])
        semh = self.semh

        def thunk(e, fn=fn, waits=waits, inc=inc, incn=incn):
            for (s, v) in waits:
                e.wait_ge(semh[s], v)
            inst = fn(e)
            if inc is not None:
                inst.then_inc(semh[inc], incn)
        self.q[eng].append(thunk)
        return tok

    def barrier(self, exclude=()):
        snap = [(s, v) for s, v in self.cnt.items() if v > 0 and s not in exclude]
        semh = self.semh
        for eng in self.q:
            def thunk(e, snap=tuple(snap)):
                for (s, v) in snap:
                    e.wait_ge(semh[s], v)
            self.q[eng].append(thunk)

    def soft_barrier(self):
        self.barrier(exclude=("cC",))


def build_program():
    nc = bass.Bass("TRN2", target_bir_lowering=False)
    C = Ctx()

    def din(name, shape, dt=F32):
        return nc.dram_tensor(name, shape, dt, kind="ExternalInput").ap()

    xT_d = din("xT", [2048, 1024])
    wqkvT_d = din("wqkvT", [2048, 6144])
    woutT_d = din("woutT", [2048, 2048])
    normw_d = din("normw", [1, 2048])
    pmkt_d = din("pmkt", [128, 256])
    pmv_d = din("pmv", [16, 2048])
    Wo_d = din("Wo", [2048, 2048])
    Wq_d = din("Wq_own", [2048, 1024])
    Wk_d = din("Wk_own", [2048, 256])
    Wv_d = din("Wv_own", [2048, 256])
    idb_d = din("idb", [128, 128], BF16)
    idf_d = din("idf", [128, 128])
    onescol_d = din("onescol", [128, 1], BF16)
    onesrow_d = din("onesrow", [1, 128])
    one11_d = din("one11", [1, 1])
    prot0_d = din("prot0", [128, 128], BF16)
    prot1_d = din("prot1", [128, 128], BF16)
    l0cos_d = din("l0cos", [128, 1024], BF16)
    l0sin_d = din("l0sin", [128, 1024], BF16)
    l1cos_d = din("l1cos", [128, 2048], BF16)
    l1sin_d = din("l1sin", [128, 2048], BF16)
    l0mask_d = din("l0mask", [128, 2048], BF16)
    l1mask_d = din("l1mask", [128, 2048], BF16)
    h128_d = din("h128", [128, 128], BF16)
    bd64_d = din("bd64", [128, 128], BF16)
    wmod_sl_d = {0: din("wmod0_sl", [256, 12288]), 1: din("wmod1_sl", [256, 12288])}
    oh_d = din("ohmask", [128, 768])
    c0s_d = din("c0s", [4, 256])
    c1s_d = din("c1s", [4, 256])
    qsl_d = {name: din(name + "_sl", [I // 8, O]) for name, I, O in QMATS}
    out_d = nc.dram_tensor("out", [1024, 2048], F32, kind="ExternalOutput").ap()
    DBG = os.environ.get("MK_DEBUG") == "1"
    dbg = {}
    if DBG:
        dbg["modv"] = nc.dram_tensor("dbg_modv", [128, 192], F32, kind="ExternalOutput").ap()
        dbg["cw"] = nc.dram_tensor("dbg_cw", [1, 8], F32, kind="ExternalOutput").ap()
        dbg["cwq"] = nc.dram_tensor("dbg_cwq", [1, 8], F32, kind="ExternalOutput").ap()
        dbg["stg"] = nc.dram_tensor("dbg_stg", [1, 8], F32, kind="ExternalOutput").ap()
        dbg["mp"] = nc.dram_tensor("dbg_mp", [128, 768], F32, kind="ExternalOutput").ap()
        dbg["ra"] = nc.dram_tensor("dbg_ra", [1, 512], F32, kind="ExternalOutput").ap()
        dbg["rb"] = nc.dram_tensor("dbg_rb", [1, 512], F32, kind="ExternalOutput").ap()
        dbg["sm"] = nc.dram_tensor("dbg_sm", [1, 512], F32, kind="ExternalOutput").ap()
        dbg["ssr"] = nc.dram_tensor("dbg_ssr", [1, 512], F32, kind="ExternalOutput").ap()
        dbg["flag"] = [0]
        dbg["h"] = nc.dram_tensor("dbg_h", [128, 8192], BF16, kind="ExternalOutput").ap()
        dbg["xn"] = nc.dram_tensor("dbg_xn", [128, 8192], BF16, kind="ExternalOutput").ap()
        dbg["x1"] = nc.dram_tensor("dbg_x1", [128, 16384], F32, kind="ExternalOutput").ap()
        dbg["x2"] = nc.dram_tensor("dbg_x2", [128, 16384], F32, kind="ExternalOutput").ap()
        dbg["x3"] = nc.dram_tensor("dbg_x3", [128, 16384], F32, kind="ExternalOutput").ap()
        dbg["ctx"] = nc.dram_tensor("dbg_ctx", [128, 8192], BF16, kind="ExternalOutput").ap()
        dbg["xq"] = nc.dram_tensor("dbg_xq", [128, 4096], BF16, kind="ExternalOutput").ap()
        dbg["mid"] = nc.dram_tensor("dbg_mid", [128, 16384], BF16, kind="ExternalOutput").ap()
        dbg["mint"] = nc.dram_tensor("dbg_mint", [128, 16384], BF16, kind="ExternalOutput").ap()
        dbg["q1ss"] = nc.dram_tensor("dbg_q1ss", [1, 256], F32, kind="ExternalOutput").ap()
        dbg["q1mx"] = nc.dram_tensor("dbg_q1mx", [1, 256], F32, kind="ExternalOutput").ap()
        dbg["q1cb"] = nc.dram_tensor("dbg_q1cb", [1, 256], F32, kind="ExternalOutput").ap()
        dbg["q1os"] = nc.dram_tensor("dbg_q1os", [1, 256], F32, kind="ExternalOutput").ap()
        dbg["q2cb"] = nc.dram_tensor("dbg_q2cb", [1, 256], F32, kind="ExternalOutput").ap()
        dbg["q2os"] = nc.dram_tensor("dbg_q2os", [1, 256], F32, kind="ExternalOutput").ap()
        dbg["hffn"] = nc.dram_tensor("dbg_hffn", [128, 8192], BF16, kind="ExternalOutput").ap()
        dbg["w1q"] = nc.dram_tensor("dbg_w1q", [2048, 8192], BF16, kind="ExternalOutput").ap()
        dbg["w2q"] = nc.dram_tensor("dbg_w2q", [8192, 2048], BF16, kind="ExternalOutput").ap()

    modb_i = nc.dram_tensor("modb_i", [128, 768], F32)
    modb_o = nc.dram_tensor("modb_o", [128, 768], F32)
    absb_i = nc.dram_tensor("absb_i", [1, 8], F32)
    absb_o = nc.dram_tensor("absb_o", [1, 8], F32)
    wq_i = {name: nc.dram_tensor("wqi_" + name, [I // 8, O], BF16)
            for name, I, O in QMATS}
    wq_o = {name: nc.dram_tensor("wqo_" + name, [I, O], BF16,
                                 addr_space="Shared")
            for name, I, O in QMATS}
    wbf = {nm: nc.dram_tensor("wbf_" + nm, shp, BF16)
           for nm, shp in [("wqkv", [2048, 6144]), ("wout", [2048, 2048]),
                           ("Wq", [2048, 1024]), ("Wk", [2048, 256]),
                           ("Wv", [2048, 256]), ("Wo", [2048, 2048])]}
    hb_i = nc.dram_tensor("hb_i", [2048, 1024], BF16)
    hb_o = nc.dram_tensor("hb_o", [4096, 1024], BF16)
    shufb = nc.dram_tensor("shufb", [128, 8192], BF16)
    ctx2b = nc.dram_tensor("ctx2b", [2048, 1024], BF16)

    st = contextlib.ExitStack()
    sb = lambda n, s, d: st.enter_context(nc.sbuf_tensor(n, s, d))
    psa = lambda n, s, d: st.enter_context(nc.psum_tensor(n, s, d))

    X = sb("X", [128, NCH * 1024], F32)
    H = sb("H", [128, NCH * 512], BF16)
    WB = [sb("WBa", [128, 4096], BF16), sb("WBb", [128, 4096], BF16)]
    TMP = sb("TMPa", [128, 512], F32)
    TMP2 = sb("TMPb", [128, 512], F32)
    TMP3 = sb("TMPc", [128, 4], F32)
    BCA = sb("BCAx", [128, 512], F32)
    BCB = sb("BCBx", [128, 512], F32)
    XB = sb("XBx", [128, 1024], BF16)
    XB2 = sb("XB2x", [128, 1024], BF16)
    IDB = sb("IDBx", [128, 128], BF16)
    IDF = sb("IDFx", [128, 128], F32)
    ONESC = sb("ONESCx", [128, 1], BF16)
    ONESR = sb("ONESRx", [1, 128], F32)
    ONE11 = sb("ONE11x", [1, 1], F32)
    PROT1 = sb("PROT1x", [128, 128], BF16)
    H128 = sb("H128x", [128, 128], BF16)
    BD64 = sb("BD64x", [128, 128], BF16)
    NORMW = sb("NORMWx", [128, 16], F32)
    MODV = sb("MODVx", [128, 192], F32)
    R_SM = sb("R_SMx", [1, 512], F32)
    R_SS = sb("R_SSx", [1, 512], F32)
    R_A = sb("R_Ax", [1, 512], F32)
    R_B = sb("R_Bx", [1, 512], F32)
    R_MX = sb("R_MXx", [1, 512], F32)
    R_T = sb("R_Tx", [1, 512], F32)
    R_OS = sb("R_OSx", [1, 512], F32)
    R_CW = sb("R_CWx", [1, 8], F32)
    R_SQ = sb("R_SQx", [1, 8], F32)
    R_CWQ = sb("R_CWQx", [1, 8], F32)
    MCOL = sb("MCOLx", [128, 20], F32)
    SCL = sb("SCLx", [128, 2], F32)
    DCOL = sb("DCOLx", [128, 1], F32)
    EPSC = sb("EPSCx", [1, 2], F32)

    pbank = [psa("pb%d" % i, [128, 512], F32) for i in range(4)]
    pSTAT = psa("pSTATx", [128, 512], F32)
    pSTAT2 = psa("pSTAT2x", [128, 512], F32)
    pBF = psa("pBFx", [128, 1024], BF16)
    pSM = psa("pSMx", [128, 512], F32)

    DSTREAMS = ("CS", "WB0", "WB1", "QS", "WQ", "MB", "HS", "HB", "CX",
                "CXL", "OS", "FW", "XS", "PC")
    for s in ("cC", "eT", "eV", "eS") + DSTREAMS:
        C.semh[s] = st.enter_context(nc.semaphore(s))
        C.cnt[s] = 0

    QMAP = {"CS": "sync", "WB0": "sync", "WB1": "scalar", "QS": "sync",
            "WQ": "sync", "MB": "sync", "HS": "sync", "HB": "sync",
            "CX": "sync", "CXL": "sync", "OS": "sync", "FW": "gpsimd",
            "XS": "sync", "PC": "gpsimd"}

    SERIAL = {"CS", "MB", "HB", "CX", "OS", "WQ", "QS", "HS", "XS"}

    def dma(stream, out, in_, waits=(), queue=None):
        q = queue or QMAP[stream]
        w = tuple(waits)
        if stream in SERIAL and C.cnt[stream] > 0:
            w = w + ((stream, C.cnt[stream]),)
        return C.push(q, lambda e, o=out, i=in_: e.dma_start(out=o, in_=i),
                      waits=w, inc=stream, incn=16)

    def scur(stream):
        return (stream, C.cnt[stream])

    def mm(lhsT, rhs, out, start, stop, waits=(), inc=None):
        return C.push("tensor",
                      lambda e, l=lhsT, r=rhs, o=out, s1=start, s2=stop:
                      e.matmul(o, l, r, start=s1, stop=s2),
                      waits=waits, inc=inc)

    def tpose(in_, ident, out, waits=(), inc="eT"):
        return C.push("tensor", lambda e, i=in_, d=ident, o=out:
                      e.transpose(o, i, d), waits=waits, inc=inc)

    def vop(fn, waits=(), inc="eV"):
        return C.push("vector", fn, waits=waits, inc=inc)

    def sop(fn, waits=(), inc="eS"):
        return C.push("scalar", fn, waits=waits, inc=inc)

    def wsync(eng, waits):
        waits = [w for w in waits if w is not None]
        semh = C.semh

        def thunk(e, waits=tuple(waits)):
            for (sname, v) in waits:
                e.wait_ge(semh[sname], v)
        C.q[eng].append(thunk)

    banks = [None, None, None, None]
    pSM_free = [None]
    pBF_free = [None]
    stat_free = [None, None]
    res_tok = [None]
    wb_free = [None, None]
    wb_i = [0]

    def bcast_row(row_ap, out_sb, n, waits):
        t1 = mm(ONESR[:, :], row_ap, pSM[0:128, 0:n], True, True,
                waits=tuple(waits) + (pSM_free[0],), inc="eT")
        t2 = vop(lambda e, o=out_sb, n=n: e.tensor_copy(o[:, 0:n], pSM[0:128, 0:n]),
                 waits=(t1,))
        pSM_free[0] = t2
        return t2

    def negbias_from_col(col_ap, scale, outcol, waits):
        t1 = tpose(col_ap, IDF[:, :], pSM[0:1, 0:128],
                   waits=tuple(waits) + (pSM_free[0],))
        t2 = vop(lambda e: e.reduce_max(R_T[0:1, 0:1], pSM[0:1, 0:128], XL),
                 waits=(t1,))
        t3 = mm(ONESR[:, :], R_T[0:1, 0:1], pSM[0:128, 0:1], True, True,
                waits=(t2,), inc="eT")
        t4 = vop(lambda e, oc=outcol, sc=scale: e.tensor_scalar(
            MCOL[:, oc:oc + 1], pSM[0:128, 0:1], -sc, None, AX.mult), waits=(t3,))
        pSM_free[0] = t4
        return t4

    def absmax_to_row(tiles, N, waits):
        t = vop(lambda e, a=tiles[0], N=N: e.tensor_mul(TMP2[:, 0:N], a, a),
                waits=tuple(waits))
        for ap_ in tiles[1:]:
            vop(lambda e, a=ap_, N=N: e.tensor_mul(TMP[:, 0:N], a, a))
            t = vop(lambda e, N=N: e.tensor_max(TMP2[:, 0:N], TMP2[:, 0:N],
                                                TMP[:, 0:N]))
        last = None
        for j in range(N // 128):
            tt = tpose(TMP2[:, j * 128:(j + 1) * 128], IDF[:, :],
                       pSM[0:128, 0:128], waits=(t, pSM_free[0]))
            tr = vop(lambda e: e.reduce_max(TMP3[:, 0:1], pSM[0:128, 0:128], XL),
                     waits=(tt,))
            pSM_free[0] = tr
            tt2 = tpose(TMP3[:, 0:1], IDF[:, :], pSM[0:1, 0:128],
                        waits=(tr, pSM_free[0]))
            last = vop(lambda e, j=j: e.tensor_copy(
                R_MX[0:1, j * 128:(j + 1) * 128], pSM[0:1, 0:128]), waits=(tt2,))
            pSM_free[0] = last
        sop(lambda e, N=N: e.activation(R_MX[0:1, 0:N], R_MX[0:1, 0:N], ACT.Ln),
            waits=(last,))
        last = sop(lambda e, N=N: e.activation(R_MX[0:1, 0:N], R_MX[0:1, 0:N],
                                               ACT.Exp, scale=0.5))
        return last

    def ln_modulated(l, modofs, x_tok0, N, t_modv):
        base = l * 96 + modofs
        m1 = m2 = None
        m2s = [None] * NCH
        for cch in range(NCH):
            sl_ = (cch % 2) * 512
            xc = X[:, cch * 1024 + x_tok0: cch * 1024 + x_tok0 + N]
            t_cast = sop(lambda e, xc=xc, N=N, sl_=sl_: e.copy(
                XB[:, sl_:sl_ + N], xc),
                waits=(res_tok[0], m2s[cch - 2] if cch >= 2 else None))
            t_sq = vop(lambda e, N=N, sl_=sl_: e.tensor_mul(
                XB2[:, sl_:sl_ + N], XB[:, sl_:sl_ + N], XB[:, sl_:sl_ + N]),
                waits=(t_cast,))
            m1 = mm(ONESC[:, :], XB[:, sl_:sl_ + N], pSTAT[0:1, 0:N],
                    cch == 0, cch == NCH - 1,
                    waits=(t_cast, stat_free[0]) if cch == 0 else (t_cast,),
                    inc="eT")
            m2 = mm(ONESC[:, :], XB2[:, sl_:sl_ + N], pSTAT2[0:1, 0:N],
                    cch == 0, cch == NCH - 1,
                    waits=(t_sq, stat_free[1]) if cch == 0 else (t_sq,),
                    inc="eT")
            m2s[cch] = m2
        t_mean = vop(lambda e, N=N: e.tensor_scalar(
            R_SM[0:1, 0:N], pSTAT[0:1, 0:N], 1.0 / D, None, AX.mult), waits=(m1,))
        t_ss = vop(lambda e, N=N: e.tensor_scalar(
            R_SS[0:1, 0:N], pSTAT2[0:1, 0:N], 1.0 / D, None, AX.mult), waits=(m2,))
        stat_free[0] = t_mean
        stat_free[1] = t_ss
        vop(lambda e, N=N: e.tensor_mul(R_T[0:1, 0:N], R_SM[0:1, 0:N], R_SM[0:1, 0:N]))
        t_var = vop(lambda e, N=N: e.tensor_sub(R_SS[0:1, 0:N], R_SS[0:1, 0:N],
                                                R_T[0:1, 0:N]))
        t_ln = sop(lambda e, N=N: e.activation(R_A[0:1, 0:N], R_SS[0:1, 0:N],
                                               ACT.Ln, bias=EPSC[0:1, 0:1]),
                   waits=(t_var,))
        t_a = sop(lambda e, N=N: e.activation(R_A[0:1, 0:N], R_A[0:1, 0:N],
                                              ACT.Exp, scale=-0.5), waits=(t_ln,))
        t_bm = vop(lambda e, N=N: e.tensor_mul(R_B[0:1, 0:N], R_SM[0:1, 0:N],
                                               R_A[0:1, 0:N]), waits=(t_a,))
        t_bn = vop(lambda e, N=N: e.tensor_scalar(R_B[0:1, 0:N], R_B[0:1, 0:N],
                                                  -1.0, None, AX.mult))
        if DBG and not dbg["flag"][0]:
            dbg["flag"][0] = 1
            dma("CS", dbg["ra"], R_A[0:1, :], waits=(t_a,))
            dma("CS", dbg["rb"], R_B[0:1, :], waits=(t_bn,))
            dma("CS", dbg["sm"], R_SM[0:1, :], waits=(t_mean,))
            dma("CS", dbg["ssr"], R_SS[0:1, :], waits=(t_var,))
        tA = bcast_row(R_A[0:1, 0:N], BCA, N, [t_a])
        tB = bcast_row(R_B[0:1, 0:N], BCB, N, [t_bn])
        done = None
        for cch in range(NCH):
            xc = X[:, cch * 1024 + x_tok0: cch * 1024 + x_tok0 + N]
            hs = H[:, cch * 512: cch * 512 + N]
            vop(lambda e, xc=xc, N=N: e.tensor_mul(TMP[:, 0:N], xc, BCA[:, 0:N]),
                waits=(tA, tB) if cch == 0 else ())
            vop(lambda e, N=N: e.tensor_add(TMP[:, 0:N], TMP[:, 0:N], BCB[:, 0:N]))
            done = vop(lambda e, h=hs, cch=cch, N=N, base=base: e.tensor_scalar(
                h, TMP[:, 0:N], MODV[:, base + 16 + cch:base + 17 + cch],
                MODV[:, base + cch:base + 1 + cch], AX.mult, AX.add),
                waits=(t_modv,) if cch == 0 else ())
        return done

    def wload(src_slice_ap, waits=(), queue=None):
        b = wb_i[0]
        wb_i[0] = 1 - b
        tok = dma("WB%d" % b, WB[b][:, :].rearrange("p (kc m) -> p kc m", m=256),
                  src_slice_ap.rearrange("(kc p) m -> p kc m", p=128),
                  waits=tuple(waits) + (wb_free[b],), queue=queue)
        return b, tok

    for dst, src in [(IDB, idb_d), (IDF, idf_d), (ONESC, onescol_d),
                     (ONESR, onesrow_d), (ONE11, one11_d), (PROT1, prot1_d),
                     (H128, h128_d), (BD64, bd64_d)]:
        tok_consts = dma("CS", dst[:, :], src)
    t_x = dma("XS", X[:, :].rearrange("p (c t) -> p c t", t=1024),
              xT_d.rearrange("(c p) t -> p c t", p=128))
    res_tok[0] = t_x
    vop(lambda e: e.memset(EPSC[0:1, 0:1], 1e-5))
    vop(lambda e: e.memset(EPSC[0:1, 1:2], 1e-6))

    scope0 = contextlib.ExitStack()
    sb0 = lambda n, s, d: scope0.enter_context(nc.sbuf_tensor(n, s, d))
    NWROW = sb0("NWROWx", [1, 2048], F32)
    CSL = sb0("CSLx", [4, 512], F32)
    CSLB = sb0("CSLBx", [4, 512], BF16)
    SILC = sb0("SILCx", [128, 16], BF16)
    MP = sb0("MPx", [128, 768], F32)
    OH = sb0("OHx", [128, 768], F32)
    QSL = sb0("QSLx", [128, 8192], F32)
    QSL2 = sb0("QSL2x", [128, 8192], F32)
    QBW = H
    STG = sb0("STGx", [1, 8], F32)

    t_nwr = dma("CS", NWROW[:, :], normw_d)
    nw_last = None
    for cch in range(NCH):
        nw_last = mm(NWROW[0:1, cch * 128:(cch + 1) * 128], ONE11[:, :],
                     pSM[0:128, cch:cch + 1], True, True,
                     waits=(t_nwr, tok_consts, pSM_free[0]) if cch == 0 else (),
                     inc="eT" if cch == NCH - 1 else None)
    t_nw = vop(lambda e: e.tensor_copy(NORMW[:, :], pSM[0:128, 0:16]),
               waits=(nw_last,))
    pSM_free[0] = t_nw

    t_c0 = dma("CS", CSL[:, 0:256], c0s_d)
    t_c1 = dma("CS", CSL[:, 256:512], c1s_d)
    t_sil = sop(lambda e: e.activation(CSL[:, :], CSL[:, :], ACT.Silu),
                waits=(t_c0, t_c1))
    t_silb = sop(lambda e: e.copy(CSLB[:, :], CSL[:, :]))
    tp_last = None
    for l in range(2):
        for kc in range(2):
            tp_last = tpose(CSLB[0:4, l * 256 + kc * 128: l * 256 + (kc + 1) * 128],
                            IDB[0:4, 0:4],
                            pBF[:, (l * 2 + kc) * 4:(l * 2 + kc) * 4 + 4],
                            waits=(t_silb, tok_consts, pBF_free[0]))
    t_silc = vop(lambda e: e.tensor_copy(SILC[:, :], pBF[0:128, 0:16]),
                 waits=(tp_last,))
    pBF_free[0] = t_silc

    for l in range(2):
        for sl in range(6):
            wsl = wmod_sl_d[l][:, sl * 2048:(sl + 1) * 2048]
            b = wb_i[0]
            wb_i[0] = 1 - b
            tok_w = dma("WB%d" % b,
                        WB[b][:, :].rearrange("p (kc m) -> p kc m", m=2048),
                        wsl.rearrange("(kc p) m -> p kc m", p=128),
                        waits=(wb_free[b],), queue="gpsimd")
            last_mm = None
            for ml in range(16):
                mc = sl * 16 + ml
                bk = mc % 2
                mm(WB[b][:, ml * 128: (ml + 1) * 128], SILC[0:128, l * 8: l * 8 + 4],
                   pbank[bk][0:128, 0:4], True, False,
                   waits=(tok_w, t_silc, banks[bk]))
                last_mm = mm(WB[b][:, 2048 + ml * 128: 2048 + (ml + 1) * 128],
                             SILC[0:128, l * 8 + 4: l * 8 + 8],
                             pbank[bk][0:128, 0:4], False, True, inc="eT")
                ev = vop(lambda e, mc=mc, l=l, bk=bk: e.tensor_copy(
                    MP[:, l * 384 + mc * 4: l * 384 + mc * 4 + 4],
                    pbank[bk][0:128, 0:4]), waits=(last_mm,))
                banks[bk] = ev
                if ml == 15:
                    wb_free[b] = last_mm
    t_mpd = dma("MB", modb_i.ap(), MP[:, :], waits=(banks[0], banks[1]))
    t_ar_mod = C.push("gpsimd", lambda e: e.collective_compute(
        "AllReduce", AX.add, replica_groups=[list(range(8))],
        ins=[modb_i.ap()], outs=[modb_o.ap()]), waits=(t_mpd,), inc="cC")
    pc_tok = {}
    for nm, src_d in [("wqkv", wqkvT_d), ("wout", woutT_d)]:
        pc_tok[nm] = dma("PC", wbf[nm].ap(), src_d)

    QBUFS = [QSL, QSL2]
    qb_last = [None, None]
    qb_i = [0]
    prev = None
    for mi, (name, I, O) in enumerate(QMATS):
        na = I // 8 // 128
        for hf in range(2):
            qb = qb_i[0]
            qb_i[0] = 1 - qb
            tok_l = dma("QS", QBUFS[qb][:, :].rearrange("p (a o) -> p a o", o=O),
                        qsl_d[name].rearrange("(a p) o -> p a o", p=128)
                        [:, hf * na // 2:(hf + 1) * na // 2, :],
                        waits=(qb_last[qb],))
            prev = vop(lambda e, hf=hf, qb=qb: e.reduce_sum(
                TMP3[:, hf + 1:hf + 2], QBUFS[qb][:, :], XL,
                apply_absolute_value=True), waits=(tok_l,))
            qb_last[qb] = prev
        t_add = vop(lambda e: e.tensor_add(TMP3[:, 1:2], TMP3[:, 1:2], TMP3[:, 2:3]))
        tt = tpose(TMP3[:, 1:2], IDF[:, :], pSM[0:1, 0:128],
                   waits=(t_add, pSM_free[0]))
        ts_ = vop(lambda e, mi=mi: e.reduce_sum(
            STG[0:1, mi:mi + 1], pSM[0:1, 0:128], XL), waits=(tt,))
        pSM_free[0] = ts_
        prev = ts_
    if DBG:
        dma("CS", dbg["stg"], STG[0:1, :], waits=(prev,))
    t_absd = dma("HS", absb_i.ap(), STG[0:1, :], waits=(prev,))
    t_ar_abs = C.push("gpsimd", lambda e: e.collective_compute(
        "AllReduce", AX.add, replica_groups=[list(range(8))],
        ins=[absb_i.ap()], outs=[absb_o.ap()]), waits=(t_absd,), inc="cC")

    t_oh = dma("CS", OH[:, :], oh_d)
    t_mpb = dma("MB", MP[:, :], modb_o.ap(), waits=(t_ar_mod,))
    vop(lambda e: e.tensor_mul(MP[:, :], MP[:, :], OH[:, :]), waits=(t_mpb, t_oh))
    t_sel = vop(lambda e: e.tensor_reduce(
        MODV[:, :], MP[:, :].rearrange("p (m r) -> p m r", r=4), XL, AX.add))
    if DBG:
        dma("CS", dbg["mp"], MP[:, :], waits=(t_sel,))
    t_modv = t_sel
    prev_fix = t_sel
    for l in range(2):
        base = l * 96
        for gofs in (32, 80):
            cols = slice(base + gofs, base + gofs + 16)
            vop(lambda e, c=cols: e.tensor_scalar(
                TMP[:, 0:16], MODV[:, c], 0.0, None, AX.is_ge), waits=(t_sel,))
            tb = vop(lambda e: e.tensor_scalar(
                TMP[:, 0:16], TMP[:, 0:16], 2.0, -1.0, AX.mult, AX.add))
            tc2 = sop(lambda e, c=cols: e.activation(
                TMP2[:, 0:16], MODV[:, c], ACT.Abs), waits=(t_sel, prev_fix))
            td = vop(lambda e: e.tensor_scalar(
                TMP2[:, 0:16], TMP2[:, 0:16], 0.25, None, AX.max), waits=(tc2,))
            t_modv = vop(lambda e, c=cols: e.tensor_mul(
                MODV[:, c], TMP[:, 0:16], TMP2[:, 0:16]), waits=(tb, td))
            prev_fix = t_modv
        for cofs in (16, 64):
            cols = slice(base + cofs, base + cofs + 16)
            t_modv = vop(lambda e, c=cols: e.tensor_scalar(
                MODV[:, c], MODV[:, c], 1.0, None, AX.add), waits=(t_sel,))

    t_absb = dma("HS", R_CW[0:1, 0:8], absb_o.ap(), waits=(t_ar_abs,))
    vop(lambda e: e.tensor_scalar(R_CW[0:1, 0:8], R_CW[0:1, 0:8],
                                  1.0 / NELEM_Q, None, AX.mult), waits=(t_absb,))
    vop(lambda e: e.tensor_scalar(R_CW[0:1, 0:8], R_CW[0:1, 0:8],
                                  1e-5, None, AX.max))
    t_sq_r = vop(lambda e: e.reciprocal(R_SQ[0:1, 0:8], R_CW[0:1, 0:8]))
    vop(lambda e: e.tensor_scalar(R_CWQ[0:1, 0:2], R_CW[0:1, 0:2],
                                  1.0 / 127.0, None, AX.mult))
    t_cwq = vop(lambda e: e.tensor_scalar(R_CWQ[0:1, 2:5], R_CW[0:1, 2:5],
                                          1.0 / 128.0, None, AX.mult))
    if DBG:
        dma("CS", dbg["modv"], MODV[:, :], waits=(t_modv,))
        dma("CS", dbg["cw"], R_CW[0:1, :], waits=(t_cwq,))
        dma("CS", dbg["cwq"], R_CWQ[0:1, :], waits=(t_cwq,))
    ag_toks = {}
    last = None
    for mi, (name, I, O) in enumerate(QMATS):
        tb = mm(ONESR[:, :], R_SQ[0:1, mi:mi + 1], pSM[0:128, 0:1], True, True,
                waits=(t_sq_r, pSM_free[0]), inc="eT")
        tsc = vop(lambda e: e.tensor_copy(SCL[:, 0:1], pSM[0:128, 0:1]), waits=(tb,))
        pSM_free[0] = tsc
        na = I // 8 // 128
        for hf in range(2):
            qb = qb_i[0]
            qb_i[0] = 1 - qb
            tok_l = dma("QS", QBUFS[qb][:, :].rearrange("p (a o) -> p a o", o=O),
                        qsl_d[name].rearrange("(a p) o -> p a o", p=128)
                        [:, hf * na // 2:(hf + 1) * na // 2, :],
                        waits=(qb_last[qb],))
            sop(lambda e, qb=qb: e.activation(
                QBUFS[qb][:, :], QBUFS[qb][:, :], ACT.Copy, bias=MAGIC,
                scale=SCL[:, 0:1]), waits=(tok_l, tsc))
            s2 = sop(lambda e, qb=qb: e.activation(
                QBUFS[qb][:, :], QBUFS[qb][:, :], ACT.Copy, bias=-MAGIC))
            t3 = vop(lambda e, qb=qb: e.tensor_scalar(
                QBW[:, :], QBUFS[qb][:, :], -1.0, 1.0, AX.max, AX.min),
                waits=(s2, last))
            qb_last[qb] = t3
            last = dma("WQ", wq_i[name].ap().rearrange("(a p) o -> p a o", p=128)
                       [:, hf * na // 2:(hf + 1) * na // 2, :],
                       QBW[:, :].rearrange("p (a o) -> p a o", o=O),
                       waits=(t3,))
        ag_toks[name] = C.push("gpsimd", lambda e, n=name: e.collective_compute(
            "AllGather", AX.bypass, replica_groups=[list(range(8))],
            ins=[wq_i[n].ap()], outs=[wq_o[n].ap()]), waits=(scur("WQ"),), inc="cC")
    for nm, src_d in [("Wq", Wq_d), ("Wk", Wk_d), ("Wv", Wv_d), ("Wo", Wo_d)]:
        pc_tok[nm] = dma("PC", wbf[nm].ap(), src_d)
    scope0.close()
    C.soft_barrier()

    scopeA = contextlib.ExitStack()
    sbA = lambda n, s, d: scopeA.enter_context(nc.sbuf_tensor(n, s, d))
    L0COS = sbA("L0COSx", [128, 1024], BF16)
    L0SIN = sbA("L0SINx", [128, 1024], BF16)
    PROT0 = sbA("PROT0x", [128, 128], BF16)
    L0MASK = sbA("L0MASKx", [128, 2048], BF16)
    PMKT = sbA("PMKTx", [128, 256], BF16)
    PMV = sbA("PMVx", [16, 2048], BF16)
    QT = sbA("QTx", [128, 2 * 512], BF16)
    KT = sbA("KTx", [128, 2 * 512], BF16)
    VV = sbA("VVx", [128, 4 * 2048], BF16)
    PP = sbA("PPx", [128, 4 * 512], BF16)
    PPP = sbA("PPPx", [16, 512], BF16)
    CTX = sbA("CTXx", [128, 16 * 512], BF16)
    DNB = sbA("DNBx", [128, 512], F32)
    dma("CS", L0COS[:, :], l0cos_d)
    dma("CS", L0SIN[:, :], l0sin_d)
    tok_prot0 = dma("CS", PROT0[:, :], prot0_d)
    t_msk0 = dma("CS", L0MASK[:, :], l0mask_d)
    tok_ropetab = t_msk0
    PMKF = sbA("PMKFx", [128, 256], F32)
    PMVF = sbA("PMVFx", [16, 2048], F32)
    t_pmkf = dma("CS", PMKF[:, :], pmkt_d)
    t_pmvf = dma("CS", PMVF[:, :], pmv_d)
    t_pmk = vop(lambda e: e.tensor_copy(PMKT[:, :], PMKF[:, :]),
                waits=(t_pmkf,))
    t_pmv = vop(lambda e: e.tensor_copy(PMV[:, :], PMVF[:, :]),
                waits=(t_pmvf,))
    SC0 = float(1.0 / np.sqrt(128.0))

    def rope_apply(dst, src_raw_bf, rot_psum_bank, costab, sintab, tok_mm):
        vop(lambda e: e.tensor_mul(TMP[:, 0:512], src_raw_bf, costab),
            waits=(tok_ropetab,))
        vop(lambda e: e.tensor_mul(TMP2[:, 0:512], rot_psum_bank[0:128, 0:512],
                                   sintab), waits=(tok_mm,))
        return vop(lambda e, d=dst: e.tensor_add(d, TMP[:, 0:512], TMP2[:, 0:512]))

    for w in range(2):
        tok0 = w * 512
        wsync("vector", [("eT", C.cnt["eT"])])
        t_h = ln_modulated(0, 0, tok0, 512, t_modv)
        m2 = None
        m2s = [None] * NCH
        for cch in range(NCH):
            sl_ = (cch % 2) * 512
            hs = H[:, cch * 512:(cch + 1) * 512]
            t_sq = vop(lambda e, h=hs, sl_=sl_: e.tensor_mul(
                XB2[:, sl_:sl_ + 512], h, h),
                waits=((t_h,) if cch == 0 else ()) +
                ((m2s[cch - 2],) if cch >= 2 else ()))
            m2 = mm(ONESC[:, :], XB2[:, sl_:sl_ + 512], pSTAT[0:1, 0:512],
                    cch == 0, cch == NCH - 1,
                    waits=(t_sq, stat_free[0]) if cch == 0 else (t_sq,),
                    inc="eT")
            m2s[cch] = m2
        t_ss = vop(lambda e: e.tensor_scalar(R_SS[0:1, 0:512], pSTAT[0:1, 0:512],
                                             1.0 / D, None, AX.mult), waits=(m2,))
        stat_free[0] = t_ss
        sop(lambda e: e.activation(R_A[0:1, 0:512], R_SS[0:1, 0:512], ACT.Ln,
                                   bias=EPSC[0:1, 1:2]), waits=(t_ss,))
        t_a = sop(lambda e: e.activation(R_A[0:1, 0:512], R_A[0:1, 0:512],
                                         ACT.Exp, scale=-0.5))
        if DBG and w == 0:
            dma("CS", dbg["h"], H[:, :], waits=(t_h,))
        tA = bcast_row(R_A[0:1, 0:512], BCA, 512, [t_a])
        t_xn = None
        for cch in range(NCH):
            hs = H[:, cch * 512:(cch + 1) * 512]
            vop(lambda e, h=hs: e.tensor_mul(h, h, BCA[:, 0:512]),
                waits=(tA, scur("CS")) if cch == 0 else ())
            t_xn = vop(lambda e, h=hs, cch=cch: e.tensor_scalar(
                h, h, NORMW[:, cch:cch + 1], None, AX.mult),
                waits=(t_nw,) if cch == 0 else ())
        if DBG and w == 0:
            dma("CS", dbg["xn"], H[:, :], waits=(t_xn,))
        for sl in range(8):
            b, tok_w = wload(wbf["wqkv"].ap()[:, 4096 + sl * 256: 4096 + (sl + 1) * 256], waits=(pc_tok["wqkv"],))
            lastm = None
            for ts4 in range(4):
                bk = ts4 % 2
                for kc in range(NCH):
                    lastm = mm(H[:, kc * 512 + ts4 * 128: kc * 512 + (ts4 + 1) * 128],
                               WB[b][:, kc * 256:(kc + 1) * 256],
                               pbank[bk][0:128, 0:256], kc == 0, kc == NCH - 1,
                               waits=(tok_w, t_xn, banks[bk]) if kc == 0 else (),
                               inc="eT" if kc == NCH - 1 else None)
                ev = vop(lambda e, ts4=ts4, sl=sl, bk=bk: e.tensor_copy(
                    VV[:, ts4 * 2048 + sl * 256: ts4 * 2048 + (sl + 1) * 256],
                    pbank[bk][0:128, 0:256]), waits=(lastm,))
                banks[bk] = ev
            wb_free[b] = lastm
        tok_vv = ("eV", C.cnt["eV"])
        for hp in range(8):
            heads = (2 * hp, 2 * hp + 1)
            b_q, tok_wq = wload(wbf["wqkv"].ap()[:, hp * 256:(hp + 1) * 256], waits=(pc_tok["wqkv"],))
            b_k, tok_wk = wload(wbf["wqkv"].ap()[:, 2048 + hp * 256: 2048 + (hp + 1) * 256], waits=(pc_tok["wqkv"],))
            for which, (bw, tok_w, buf) in enumerate(
                    [(b_q, tok_wq, QT), (b_k, tok_wk, KT)]):
                for ml in range(2):
                    bk = 2 + ml
                    lastm = None
                    for kc in range(NCH):
                        lastm = mm(WB[bw][:, kc * 256 + ml * 128:
                                          kc * 256 + (ml + 1) * 128],
                                   H[:, kc * 512:(kc + 1) * 512],
                                   pbank[bk][0:128, 0:512], kc == 0, kc == NCH - 1,
                                   waits=(tok_w, t_xn, banks[bk]) if kc == 0 else (),
                                   inc="eT" if kc == NCH - 1 else None)
                    t_raw = vop(lambda e, bk=bk, ml=ml: e.tensor_copy(
                        XB[:, ml * 512:(ml + 1) * 512], pbank[bk][0:128, 0:512]),
                        waits=(lastm,))
                    banks[bk] = t_raw
                    t_rot = mm(PROT0[:, :], XB[:, ml * 512:(ml + 1) * 512],
                               pbank[bk][0:128, 0:512], True, True,
                               waits=(t_raw, tok_prot0), inc="eT")
                    tr = rope_apply(buf[:, ml * 512:(ml + 1) * 512],
                                    XB[:, ml * 512:(ml + 1) * 512],
                                    pbank[bk],
                                    L0COS[:, tok0:tok0 + 512],
                                    L0SIN[:, tok0:tok0 + 512], t_rot)
                    banks[bk] = tr
            wb_free[b_q] = ("eT", C.cnt["eT"])
            wb_free[b_k] = ("eT", C.cnt["eT"])
            tok_rope = ("eV", C.cnt["eV"])
            for hh_i, hh in enumerate(heads):
                qtile = QT[:, hh_i * 512:(hh_i + 1) * 512]
                ktile = KT[:, hh_i * 512:(hh_i + 1) * 512]
                sc_toks = []
                for kc in range(4):
                    tkk = mm(ktile[:, kc * 128:(kc + 1) * 128], qtile,
                             pbank[kc][0:128, 0:512], True, True,
                             waits=(banks[kc], tok_rope), inc="eT")
                    sc_toks.append(tkk)
                tpp = mm(PMKT[:, hh * 16:(hh + 1) * 16], qtile,
                         pSTAT2[0:16, 0:512], True, True,
                         waits=(t_pmk, stat_free[1], tok_rope), inc="eT")
                ev_toks = []
                for kc in range(4):
                    ev = vop(lambda e, kc=kc: e.tensor_tensor(
                        PP[:, kc * 512:(kc + 1) * 512], pbank[kc][0:128, 0:512],
                        L0MASK[:, kc * 512:(kc + 1) * 512], AX.add),
                        waits=(sc_toks[kc], t_msk0))
                    banks[kc] = ev
                    ev_toks.append(ev)
                evp = vop(lambda e: e.tensor_copy(PPP[0:16, 0:512],
                                                  pSTAT2[0:16, 0:512]), waits=(tpp,))
                stat_free[1] = evp
                for kc in range(4):
                    vop(lambda e, kc=kc: e.reduce_max(
                        MCOL[:, kc:kc + 1], PP[:, kc * 512:(kc + 1) * 512], XL),
                        waits=(ev_toks[kc],))
                vop(lambda e: e.reduce_max(MCOL[0:16, 4:5], PPP[0:16, 0:512], XL),
                    waits=(evp,))
                vop(lambda e: e.tensor_max(MCOL[:, 5:6], MCOL[:, 0:1], MCOL[:, 1:2]))
                vop(lambda e: e.tensor_max(MCOL[:, 6:7], MCOL[:, 2:3], MCOL[:, 3:4]))
                vop(lambda e: e.tensor_max(MCOL[:, 7:8], MCOL[:, 5:6], MCOL[:, 6:7]))
                t_mx = vop(lambda e: e.tensor_max(MCOL[0:16, 7:8], MCOL[0:16, 7:8],
                                                  MCOL[0:16, 4:5]))
                t_nb = negbias_from_col(MCOL[:, 7:8], SC0, 8, [t_mx])
                ex_toks = []
                for kc in range(4):
                    ex = sop(lambda e, kc=kc: e.activation(
                        PP[:, kc * 512:(kc + 1) * 512],
                        PP[:, kc * 512:(kc + 1) * 512],
                        ACT.Exp, bias=MCOL[:, 8:9], scale=SC0), waits=(t_nb,))
                    ex_toks.append(ex)
                exp_p = sop(lambda e: e.activation(
                    PPP[0:16, 0:512], PPP[0:16, 0:512], ACT.Exp,
                    bias=MCOL[0:16, 8:9], scale=SC0), waits=(t_nb,))
                dn = None
                for kc in range(4):
                    dn = mm(ONESC[:, :], PP[:, kc * 512:(kc + 1) * 512],
                            pSTAT[0:1, 0:512], kc == 0, False,
                            waits=(ex_toks[kc], stat_free[0] if kc == 0 else None))
                dn = mm(ONESC[0:16, :], PPP[0:16, 0:512], pSTAT[0:1, 0:512],
                        False, True, waits=(exp_p,), inc="eT")
                t_rc = vop(lambda e: e.reciprocal(R_T[0:1, 0:512],
                                                  pSTAT[0:1, 0:512]), waits=(dn,))
                stat_free[0] = t_rc
                t_dnb = bcast_row(R_T[0:1, 0:512], DNB, 512, [t_rc])
                av = None
                for kc in range(4):
                    av = mm(VV[:, kc * 2048 + hh * 128: kc * 2048 + (hh + 1) * 128],
                            PP[:, kc * 512:(kc + 1) * 512],
                            pbank[0][0:128, 0:512], kc == 0, False,
                            waits=(ex_toks[kc], banks[0], tok_vv) if kc == 0
                            else (ex_toks[kc],))
                av = mm(PMV[0:16, hh * 128:(hh + 1) * 128], PPP[0:16, 0:512],
                        pbank[0][0:128, 0:512], False, True,
                        waits=(exp_p, t_pmv), inc="eT")
                t_ctx = vop(lambda e, hh=hh: e.tensor_mul(
                    CTX[:, hh * 512:(hh + 1) * 512], pbank[0][0:128, 0:512],
                    DNB[:, 0:512]), waits=(av, t_dnb))
                banks[0] = t_ctx
        if DBG and w == 0:
            dma("CS", dbg["ctx"], CTX[:, :], waits=(("eV", C.cnt["eV"]),))
        tok_ctx = ("eV", C.cnt["eV"])
        for sl in range(8):
            b, tok_w = wload(wbf["wout"].ap()[:, sl * 256:(sl + 1) * 256], waits=(pc_tok["wout"],))
            for ml in range(2):
                oc = sl * 2 + ml
                bk = ml
                lastm = None
                for kc in range(NCH):
                    lastm = mm(WB[b][:, kc * 256 + ml * 128:
                                     kc * 256 + (ml + 1) * 128],
                               CTX[:, kc * 512:(kc + 1) * 512],
                               pbank[bk][0:128, 0:512], kc == 0, kc == NCH - 1,
                               waits=(tok_w, banks[bk], tok_ctx) if kc == 0 else (),
                               inc="eT" if kc == NCH - 1 else None)
                ev = vop(lambda e, oc=oc, bk=bk, tok0=tok0: e.scalar_tensor_tensor(
                    X[:, oc * 1024 + tok0: oc * 1024 + tok0 + 512],
                    pbank[bk][0:128, 0:512], MODV[:, 32 + oc:33 + oc],
                    X[:, oc * 1024 + tok0: oc * 1024 + tok0 + 512],
                    AX.mult, AX.add), waits=(lastm, t_modv))
                banks[bk] = ev
            wb_free[b] = lastm
        res_tok[0] = banks[1]
    if DBG:
        dma("CS", dbg["x1"], X[:, :], waits=(res_tok[0],))
    scopeA.close()
    C.soft_barrier()

    scopeB = contextlib.ExitStack()
    sbB = lambda n, s, d: scopeB.enter_context(nc.sbuf_tensor(n, s, d))
    XQ = sbB("XQx", [128, 16 * 256], BF16)
    MID = sbB("MIDx", [128, 64 * 256], BF16)

    def quant_rows_rms(N, cw_col, waits=(), sqd=SQD):
        sop(lambda e, N=N: e.activation(R_A[0:1, 0:N], R_SS[0:1, 0:N], ACT.Ln),
            waits=tuple(waits))
        t_exp = sop(lambda e, N=N: e.activation(R_A[0:1, 0:N], R_A[0:1, 0:N],
                                                ACT.Exp, scale=-0.5))
        vop(lambda e, N=N: e.tensor_scalar(R_A[0:1, 0:N], R_A[0:1, 0:N],
                                           1e12, None, AX.min), waits=(t_exp,))
        t_sr = vop(lambda e, N=N, sqd=sqd: e.tensor_scalar(
            R_A[0:1, 0:N], R_A[0:1, 0:N], sqd, None, AX.mult))
        vop(lambda e, N=N: e.tensor_mul(R_T[0:1, 0:N], R_A[0:1, 0:N],
                                        R_MX[0:1, 0:N]))
        vop(lambda e, N=N: e.tensor_scalar(R_T[0:1, 0:N], R_T[0:1, 0:N],
                                           1e-5, None, AX.max))
        vop(lambda e, N=N: e.reciprocal(R_B[0:1, 0:N], R_T[0:1, 0:N]))
        vop(lambda e, N=N: e.tensor_scalar(R_B[0:1, 0:N], R_B[0:1, 0:N],
                                           127.0, None, AX.mult))
        t_cb = vop(lambda e, N=N: e.tensor_mul(R_A[0:1, 0:N], R_A[0:1, 0:N],
                                               R_B[0:1, 0:N]))
        t_os = vop(lambda e, N=N, cw=cw_col: e.tensor_scalar(
            R_OS[0:1, 0:N], R_T[0:1, 0:N], R_CWQ[0:1, cw:cw + 1], None, AX.mult),
            waits=(t_cwq,))
        return t_cb, t_os

    if DBG:
        dma("CS", dbg["w1q"], wq_o["w1"].ap(), waits=(ag_toks["w1"],))
        dma("CS", dbg["w2q"], wq_o["w2"].ap(), waits=(ag_toks["w2"],))
    for g in range(4):
        tok0 = g * 256
        N = 256
        wsync("vector", [("eT", C.cnt["eT"])])
        t_h = ln_modulated(0, 48, tok0, N, t_modv)
        if DBG and g == 0:
            dma("CS", dbg["hffn"], H[:, :], waits=(t_h,))
        m2 = None
        m2s = [None] * NCH
        for cch in range(NCH):
            sl_ = (cch % 2) * 512
            hs = H[:, cch * 512: cch * 512 + N]
            t_sq = vop(lambda e, h=hs, sl_=sl_: e.tensor_mul(
                XB2[:, sl_:sl_ + N], h, h),
                waits=((t_h,) if cch == 0 else ()) +
                ((m2s[cch - 2],) if cch >= 2 else ()))
            m2 = mm(ONESC[:, :], XB2[:, sl_:sl_ + N], pSTAT[0:1, 0:N],
                    cch == 0, cch == NCH - 1,
                    waits=(t_sq, stat_free[0]) if cch == 0 else (t_sq,),
                    inc="eT")
            m2s[cch] = m2
        t_ssr = vop(lambda e: e.tensor_copy(R_SS[0:1, 0:N], pSTAT[0:1, 0:N]),
                    waits=(m2,))
        stat_free[0] = t_ssr
        t_amx = absmax_to_row([H[:, cch * 512: cch * 512 + N]
                               for cch in range(NCH)], N, [t_h])
        t_cb, t_os = quant_rows_rms(N, 0, waits=(t_ssr, t_amx))
        tA = bcast_row(R_A[0:1, 0:N], BCA, N, [t_cb])
        tq = None
        for cch in range(NCH):
            hs = H[:, cch * 512: cch * 512 + N]
            vop(lambda e, h=hs: e.tensor_mul(TMP[:, 0:N], h, BCA[:, 0:N]),
                waits=(tA,) if cch == 0 else ())
            tq = vop(lambda e, cch=cch: e.tensor_scalar(
                XQ[:, cch * 256: cch * 256 + N], TMP[:, 0:N], MAGIC, -MAGIC,
                AX.add, AX.add))
        tB = bcast_row(R_OS[0:1, 0:N], BCB, N, [t_os])
        if DBG and g == 0:
            dma("CS", dbg["q1ss"], R_SS[0:1, 0:256], waits=(t_ssr,))
            dma("CS", dbg["q1mx"], R_MX[0:1, 0:256], waits=(t_amx,))
            dma("CS", dbg["q1cb"], R_A[0:1, 0:256], waits=(t_cb,))
            dma("CS", dbg["q1os"], R_OS[0:1, 0:256], waits=(t_os,))
            dma("CS", dbg["xq"], XQ[:, :], waits=(tq,))
            wsync("vector", [scur("CS")])
            wsync("scalar", [scur("CS")])
        for sl in range(32):
            b, tok_w = wload(wq_o["w1"].ap()[:, sl * 256:(sl + 1) * 256],
                             waits=(ag_toks["w1"],), queue="sync")
            for ml in range(2):
                oc = sl * 2 + ml
                bk = ml
                lastm = None
                for kc in range(NCH):
                    lastm = mm(WB[b][:, kc * 256 + ml * 128:
                                     kc * 256 + (ml + 1) * 128],
                               XQ[:, kc * 256:(kc + 1) * 256],
                               pbank[bk][0:128, 0:N], kc == 0, kc == NCH - 1,
                               waits=(tok_w, tq, banks[bk]) if kc == 0 else (),
                               inc="eT" if kc == NCH - 1 else None)
                t1 = vop(lambda e, bk=bk: e.tensor_mul(
                    TMP2[:, 0:N], pbank[bk][0:128, 0:N], BCB[:, 0:N]),
                    waits=(lastm, tB))
                banks[bk] = t1
                sop(lambda e: e.activation(TMP2[:, 0:N], TMP2[:, 0:N], ACT.Gelu),
                    waits=(t1,))
                g2 = sop(lambda e, oc=oc: e.activation(
                    MID[:, oc * 256: oc * 256 + N], TMP2[:, 0:N], ACT.Gelu))
                vop(lambda e: e.tensor_copy(TMP3[:, 3:4], TMP3[:, 3:4]),
                    waits=(g2,))
            wb_free[b] = lastm
        tok_gelu = ("eS", C.cnt["eS"])
        if DBG and g == 0:
            dma("CS", dbg["mid"], MID[:, :], waits=(tok_gelu,))
            wsync("vector", [scur("CS")])
        m2 = None
        m2s = [None] * 64
        for cch in range(64):
            sl_ = (cch % 2) * 512
            ms = MID[:, cch * 256: cch * 256 + N]
            t_sq = vop(lambda e, h=ms, sl_=sl_: e.tensor_mul(
                XB2[:, sl_:sl_ + N], h, h),
                waits=((tok_gelu,) if cch == 0 else ()) +
                ((m2s[cch - 2],) if cch >= 2 else ()))
            m2 = mm(ONESC[:, :], XB2[:, sl_:sl_ + N], pSTAT[0:1, 0:N],
                    cch == 0, cch == 63,
                    waits=(t_sq, stat_free[0]) if cch == 0 else (t_sq,),
                    inc="eT")
            m2s[cch] = m2
        t_ssr = vop(lambda e: e.tensor_copy(R_SS[0:1, 0:N], pSTAT[0:1, 0:N]),
                    waits=(m2,))
        stat_free[0] = t_ssr
        t_amx2 = absmax_to_row([MID[:, cch * 256: cch * 256 + N]
                                for cch in range(64)], N, [tok_gelu])
        t_cb, t_os = quant_rows_rms(N, 1, waits=(t_ssr, t_amx2),
                                    sqd=float(np.sqrt(8192.0)))
        tA = bcast_row(R_A[0:1, 0:N], BCA, N, [t_cb])
        tq = None
        for cch in range(64):
            ms = MID[:, cch * 256: cch * 256 + N]
            vop(lambda e, h=ms: e.tensor_mul(TMP[:, 0:N], h, BCA[:, 0:N]),
                waits=(tA,) if cch == 0 else ())
            tq = vop(lambda e, h=ms: e.tensor_scalar(
                h, TMP[:, 0:N], MAGIC, -MAGIC, AX.add, AX.add))
        tB = bcast_row(R_OS[0:1, 0:N], BCB, N, [t_os])
        if DBG and g == 0:
            dma("CS", dbg["q2cb"], R_A[0:1, 0:256], waits=(t_cb,))
            dma("CS", dbg["q2os"], R_OS[0:1, 0:256], waits=(t_os,))
            dma("CS", dbg["mint"], MID[:, :], waits=(tq,))
            wsync("vector", [scur("CS")])
            wsync("scalar", [scur("CS")])
        for sl in range(8):
            lastml = [None, None]
            for jcg in range(4):
                b, tok_w = wload(wq_o["w2"].ap()
                                 [jcg * 2048:(jcg + 1) * 2048,
                                  sl * 256:(sl + 1) * 256],
                                 waits=(ag_toks["w2"],), queue="sync")
                for ml in range(2):
                    bk = ml
                    for kc in range(NCH):
                        jc = jcg * 16 + kc
                        lastml[ml] = mm(WB[b][:, kc * 256 + ml * 128:
                                              kc * 256 + (ml + 1) * 128],
                                        MID[:, jc * 256:(jc + 1) * 256],
                                        pbank[bk][0:128, 0:N],
                                        jcg == 0 and kc == 0,
                                        jcg == 3 and kc == NCH - 1,
                                        waits=(tok_w, tq, banks[bk])
                                        if (jcg == 0 and kc == 0) else (tok_w,)
                                        if kc == 0 else (),
                                        inc="eT" if kc == NCH - 1 else None)
                wb_free[b] = lastml[1]
            for ml in range(2):
                oc = sl * 2 + ml
                bk = ml
                t1 = vop(lambda e, bk=bk: e.tensor_mul(
                    TMP2[:, 0:N], pbank[bk][0:128, 0:N], BCB[:, 0:N]),
                    waits=(lastml[ml], tB))
                ev = vop(lambda e, oc=oc, tok0=tok0: e.scalar_tensor_tensor(
                    X[:, oc * 1024 + tok0: oc * 1024 + tok0 + N],
                    TMP2[:, 0:N], MODV[:, 80 + oc:81 + oc],
                    X[:, oc * 1024 + tok0: oc * 1024 + tok0 + N],
                    AX.mult, AX.add), waits=(t_modv,))
                banks[bk] = ev
        res_tok[0] = banks[1]
    if DBG:
        dma("CS", dbg["x2"], X[:, :], waits=(res_tok[0],))
    scopeB.close()
    C.soft_barrier()

    for qt_own in range(2):
        wsync("vector", [("eT", C.cnt["eT"]), scur("HB")])
        t_h = ln_modulated(1, 0, qt_own * 512, 512, t_modv)
        dma("HB",
            hb_i.ap().rearrange("(c p) t -> p c t", p=128)
            [:, :, qt_own * 512:(qt_own + 1) * 512],
            H[:, :].rearrange("p (c t) -> p c t", t=512), waits=(t_h,))
    t_hag = C.push("gpsimd", lambda e: e.collective_compute(
        "AllGather", AX.bypass, replica_groups=[[0, 1], [2, 3], [4, 5], [6, 7]],
        ins=[hb_i.ap()], outs=[hb_o.ap()]),
        waits=(scur("HB"),), inc="cC")
    C.soft_barrier()

    scopeC = contextlib.ExitStack()
    sbC = lambda n, s, d: scopeC.enter_context(nc.sbuf_tensor(n, s, d))
    L1COS = sbC("L1COSx", [128, 2048], BF16)
    L1SIN = sbC("L1SINx", [128, 2048], BF16)
    L1MASK = sbC("L1MASKx", [128, 2048], BF16)
    KT1 = sbC("KT1x", [128, 2 * 2048], BF16)
    VV1 = sbC("VV1x", [128, 16 * 260], BF16)
    QT1 = sbC("QT1x", [128, 8 * 512], BF16)
    PP1 = sbC("PP1x", [128, 16 * 512], BF16)
    C2Q = sbC("C2Qx", [128, 4 * 1024], BF16)
    dma("CS", L1COS[:, :], l1cos_d)
    tok_rt1 = dma("CS", L1SIN[:, :], l1sin_d)
    t_msk1 = dma("CS", L1MASK[:, :], l1mask_d)
    t_vones = vop(lambda e: e.memset(VV1[:, :], 1.0))
    SC1 = 0.125

    def rope512(dst, raw_bf, bank, cos_ap, sin_ap, tok_mm):
        vop(lambda e: e.tensor_mul(TMP[:, 0:512], raw_bf, cos_ap),
            waits=(tok_rt1,))
        vop(lambda e: e.tensor_mul(TMP2[:, 0:512], bank[0:128, 0:512], sin_ap),
            waits=(tok_mm,))
        return vop(lambda e, d=dst: e.tensor_add(d, TMP[:, 0:512], TMP2[:, 0:512]))

    ctx_dma_toks = []
    for qt in range(4):
        nkc = 4 * (qt + 1)
        rb = (qt // 2) * 2048
        c0c = (qt % 2) * 512
        t_hs = dma("HS", H[:, :].rearrange("p (c t) -> p c t", t=512),
                   hb_o.ap()[rb:rb + 2048, c0c:c0c + 512]
                   .rearrange("(c p) t -> p c t", p=128),
                   waits=(t_hag, ("eT", C.cnt["eT"])))
        b_k, tok_wk = wload(wbf["Wk"].ap()[:, :], waits=(pc_tok["Wk"],))
        for oc in range(2):
            bk = 2
            lastm = None
            for kc in range(NCH):
                lastm = mm(WB[b_k][:, kc * 256 + oc * 128:
                                   kc * 256 + (oc + 1) * 128],
                           H[:, kc * 512:(kc + 1) * 512],
                           pbank[bk][0:128, 0:512], kc == 0, kc == NCH - 1,
                           waits=(tok_wk, t_hs, banks[bk]) if kc == 0 else (),
                           inc="eT" if kc == NCH - 1 else None)
            t_raw = vop(lambda e, bk=bk: e.tensor_copy(XB[:, 0:512],
                                                       pbank[bk][0:128, 0:512]),
                        waits=(lastm,))
            banks[bk] = t_raw
            t_rot = mm(PROT1[:, :], XB[:, 0:512], pbank[bk][0:128, 0:512],
                       True, True, waits=(t_raw,), inc="eT")
            tr = rope512(KT1[:, oc * 2048 + qt * 512: oc * 2048 + (qt + 1) * 512],
                         XB[:, 0:512], pbank[bk],
                         L1COS[:, qt * 512:(qt + 1) * 512],
                         L1SIN[:, qt * 512:(qt + 1) * 512], t_rot)
            banks[bk] = tr
        wb_free[b_k] = lastm
        b_v, tok_wv = wload(wbf["Wv"].ap()[:, :], waits=(pc_tok["Wv"],))
        for ts4 in range(4):
            gkc = qt * 4 + ts4
            bk = 3
            lastm = None
            for kc in range(NCH):
                lastm = mm(H[:, kc * 512 + ts4 * 128: kc * 512 + (ts4 + 1) * 128],
                           WB[b_v][:, kc * 256:(kc + 1) * 256],
                           pbank[bk][0:128, 0:256], kc == 0, kc == NCH - 1,
                           waits=(tok_wv, t_hs, banks[bk]) if kc == 0 else (),
                           inc="eT" if kc == NCH - 1 else None)
            ev = None
            for lg in range(4):
                ev = vop(lambda e, gkc=gkc, lg=lg, bk=bk: e.tensor_copy(
                    VV1[:, gkc * 260 + lg * 65: gkc * 260 + lg * 65 + 64],
                    pbank[bk][0:128, lg * 64:(lg + 1) * 64]),
                    waits=(lastm, t_vones) if lg == 0 else ())
            banks[bk] = ev
        wb_free[b_v] = lastm
        for sl in range(4):
            b_q, tok_wq = wload(wbf["Wq"].ap()[:, sl * 256:(sl + 1) * 256], waits=(pc_tok["Wq"],))
            for ml in range(2):
                pc = sl * 2 + ml
                bk = 2
                lastm = None
                for kc in range(NCH):
                    lastm = mm(WB[b_q][:, kc * 256 + ml * 128:
                                       kc * 256 + (ml + 1) * 128],
                               H[:, kc * 512:(kc + 1) * 512],
                               pbank[bk][0:128, 0:512], kc == 0, kc == NCH - 1,
                               waits=(tok_wq, t_hs, banks[bk]) if kc == 0 else (),
                               inc="eT" if kc == NCH - 1 else None)
                t_raw = vop(lambda e, bk=bk: e.tensor_copy(
                    XB[:, 0:512], pbank[bk][0:128, 0:512]), waits=(lastm,))
                banks[bk] = t_raw
                t_rot = mm(PROT1[:, :], XB[:, 0:512], pbank[bk][0:128, 0:512],
                           True, True, waits=(t_raw,), inc="eT")
                tr = rope512(QT1[:, pc * 512:(pc + 1) * 512], XB[:, 0:512],
                             pbank[bk], L1COS[:, qt * 512:(qt + 1) * 512],
                             L1SIN[:, qt * 512:(qt + 1) * 512], t_rot)
                banks[bk] = tr
            wb_free[b_q] = lastm
        tok_rope1 = ("eV", C.cnt["eV"])
        tok_c2free = scur("CX")
        for pc in range(8):
            for half in range(2):
                hloc = QPERM[2 * pc + half]
                lg = hloc // 4
                kchunk = lg // 2
                krow = (lg % 2) * 64
                qv = QT1[half * 64:half * 64 + 64, pc * 512:(pc + 1) * 512]
                ev_toks = []
                for kc in range(nkc):
                    bk = kc % 2
                    tkk = mm(KT1[krow:krow + 64,
                                 kchunk * 2048 + kc * 128:
                                 kchunk * 2048 + (kc + 1) * 128],
                             qv, pbank[bk][0:128, 0:512], True, True,
                             waits=(banks[bk], tok_rope1), inc="eT")
                    j = kc - 4 * qt
                    if 0 <= j < 4:
                        ev = vop(lambda e, kc=kc, j=j, bk=bk: e.tensor_tensor(
                            PP1[:, kc * 512:(kc + 1) * 512],
                            pbank[bk][0:128, 0:512],
                            L1MASK[:, j * 512:(j + 1) * 512], AX.add),
                            waits=(tkk, t_msk1))
                    else:
                        ev = vop(lambda e, kc=kc, bk=bk: e.tensor_copy(
                            PP1[:, kc * 512:(kc + 1) * 512],
                            pbank[bk][0:128, 0:512]), waits=(tkk,))
                    banks[bk] = ev
                    ev_toks.append(ev)
                for kc in range(nkc):
                    vop(lambda e, kc=kc: e.reduce_max(
                        MCOL[:, kc:kc + 1], PP1[:, kc * 512:(kc + 1) * 512], XL),
                        waits=(ev_toks[kc],))
                t_mx = vop(lambda e, nkc=nkc: e.tensor_reduce(
                    MCOL[:, 16:17], MCOL[:, 0:nkc], XL, AX.max))
                t_nb = negbias_from_col(MCOL[:, 16:17], SC1, 17, [t_mx])
                ex_toks = []
                for kc in range(nkc):
                    ex = sop(lambda e, kc=kc: e.activation(
                        PP1[:, kc * 512:(kc + 1) * 512],
                        PP1[:, kc * 512:(kc + 1) * 512], ACT.Exp,
                        bias=MCOL[:, 17:18], scale=SC1), waits=(t_nb,))
                    ex_toks.append(ex)
                for qsub in range(4):
                    av = None
                    for kc in range(nkc):
                        av = mm(PP1[:, kc * 512 + qsub * 128:
                                    kc * 512 + (qsub + 1) * 128],
                                VV1[:, kc * 260 + lg * 65: kc * 260 + lg * 65 + 65],
                                pbank[3][0:128, 0:65], kc == 0, kc == nkc - 1,
                                waits=(ex_toks[kc], banks[3], tok_rope1) if kc == 0
                                else (ex_toks[kc],),
                                inc="eT" if kc == nkc - 1 else None)
                    t_rc = vop(lambda e: e.reciprocal(DCOL[:, 0:1],
                                                      pbank[3][0:128, 64:65]),
                               waits=(av,))
                    t_ev = vop(lambda e, qsub=qsub, hloc=hloc: e.tensor_scalar(
                        C2Q[:, qsub * 1024 + hloc * 64:
                            qsub * 1024 + hloc * 64 + 64],
                        pbank[3][0:128, 0:64], DCOL[:, 0:1], None, AX.mult),
                        waits=(tok_c2free,) if (pc == 0 and half == 0 and
                                                qsub == 0) else ())
                    banks[3] = t_ev
        tok_c2 = dma("CX",
                     ctx2b.ap()[qt * 512:(qt + 1) * 512, :]
                     .rearrange("(qc p) hd -> p qc hd", p=128),
                     C2Q[:, :].rearrange("p (qc hd) -> p qc hd", hd=1024),
                     waits=(banks[3],))
        ctx_dma_toks.append(tok_c2)
    CTXR = PP1
    tok_pp1_free = ("eT", C.cnt["eT"])
    tok_pp1_s = ("eS", C.cnt["eS"])
    for ng in range(2):
        ld = None
        for qc in range(16):
            ld = dma("CXL", CTXR[:, qc * 512:(qc + 1) * 512],
                     ctx2b.ap()[qc * 128:(qc + 1) * 128,
                                ng * 512:(ng + 1) * 512],
                     waits=(scur("CX"), tok_pp1_free, tok_pp1_s,
                            ("eT", C.cnt["eT"]))
                     if qc == 0 else ())
        for sl in range(8):
            b, tok_w = wload(wbf["Wo"].ap()[:, sl * 256:(sl + 1) * 256], waits=(pc_tok["Wo"],))
            for ml in range(2):
                oc = sl * 2 + ml
                bk = ml
                lastm = None
                for qc in range(16):
                    lastm = mm(WB[b][:, qc * 256 + ml * 128:
                                     qc * 256 + (ml + 1) * 128],
                               CTXR[:, qc * 512:(qc + 1) * 512],
                               pbank[bk][0:128, 0:512], qc == 0, qc == 15,
                               waits=(tok_w, scur("CXL"), banks[bk])
                               if qc == 0 else (),
                               inc="eT" if qc == 15 else None)
                ev = vop(lambda e, oc=oc, ng=ng, bk=bk: e.scalar_tensor_tensor(
                    X[:, oc * 1024 + ng * 512: oc * 1024 + (ng + 1) * 512],
                    pbank[bk][0:128, 0:512], MODV[:, 96 + 32 + oc:96 + 33 + oc],
                    X[:, oc * 1024 + ng * 512: oc * 1024 + (ng + 1) * 512],
                    AX.mult, AX.add), waits=(lastm, t_modv))
                banks[bk] = ev
            wb_free[b] = lastm
    res_tok[0] = banks[1]
    if DBG:
        dma("CS", dbg["x3"], X[:, :], waits=(res_tok[0],))
    scopeC.close()
    C.soft_barrier()

    scopeD = contextlib.ExitStack()
    sbD = lambda n, s, d: scopeD.enter_context(nc.sbuf_tensor(n, s, d))
    XQ1 = sbD("XQ1x", [128, 16 * 256], BF16)
    INTER = sbD("INTERx", [128, 64 * 256], BF16)
    R1 = sbD("R1x", [128, 64 * 128], BF16)

    for g in range(4):
        tok0 = g * 256
        N = 256
        wsync("vector", [("eT", C.cnt["eT"])])
        t_h = ln_modulated(1, 48, tok0, N, t_modv)
        t_amx = absmax_to_row([H[:, cch * 512: cch * 512 + N]
                               for cch in range(NCH)], N, [t_h])
        vop(lambda e: e.tensor_scalar(R_T[0:1, 0:N], R_MX[0:1, 0:N],
                                      1e-5, None, AX.max), waits=(t_amx,))
        vop(lambda e: e.reciprocal(R_A[0:1, 0:N], R_T[0:1, 0:N]))
        t_cb = vop(lambda e: e.tensor_scalar(R_A[0:1, 0:N], R_A[0:1, 0:N],
                                             128.0, None, AX.mult))
        t_osg = vop(lambda e: e.tensor_scalar(R_OS[0:1, 0:N], R_T[0:1, 0:N],
                                              R_CWQ[0:1, 2:3], None, AX.mult),
                    waits=(t_cwq,))
        t_osu = vop(lambda e: e.tensor_scalar(R_SS[0:1, 0:N], R_T[0:1, 0:N],
                                              R_CWQ[0:1, 3:4], None, AX.mult))
        tA = bcast_row(R_A[0:1, 0:N], BCA, N, [t_cb])
        tq = None
        for cch in range(NCH):
            hs = H[:, cch * 512: cch * 512 + N]
            vop(lambda e, h=hs: e.tensor_mul(TMP[:, 0:N], h, BCA[:, 0:N]),
                waits=(tA,) if cch == 0 else ())
            vop(lambda e: e.tensor_scalar(TMP[:, 0:N], TMP[:, 0:N],
                                          MAGIC, -MAGIC, AX.add, AX.add))
            tq = vop(lambda e, cch=cch: e.tensor_scalar(
                XQ1[:, cch * 256: cch * 256 + N], TMP[:, 0:N],
                127.0, None, AX.min))
        tBg = bcast_row(R_OS[0:1, 0:N], BCB, N, [t_osg])
        tBu = bcast_row(R_SS[0:1, 0:N], BCA, N, [t_osu, tq])
        for sl in range(32):
            bg, tok_g = wload(wq_o["wg"].ap()[:, sl * 256:(sl + 1) * 256],
                              waits=(ag_toks["wg"],), queue="sync")
            bu, tok_u = wload(wq_o["wu"].ap()[:, sl * 256:(sl + 1) * 256],
                              waits=(ag_toks["wu"],), queue="sync")
            for ml in range(2):
                oc = sl * 2 + ml
                lg_ = None
                lu_ = None
                for kc in range(NCH):
                    lg_ = mm(WB[bg][:, kc * 256 + ml * 128:
                                    kc * 256 + (ml + 1) * 128],
                             XQ1[:, kc * 256:(kc + 1) * 256],
                             pbank[0][0:128, 0:N], kc == 0, kc == NCH - 1,
                             waits=(tok_g, tq, banks[0]) if kc == 0 else (),
                             inc="eT" if kc == NCH - 1 else None)
                for kc in range(NCH):
                    lu_ = mm(WB[bu][:, kc * 256 + ml * 128:
                                    kc * 256 + (ml + 1) * 128],
                             XQ1[:, kc * 256:(kc + 1) * 256],
                             pbank[1][0:128, 0:N], kc == 0, kc == NCH - 1,
                             waits=(tok_u, banks[1]) if kc == 0 else (),
                             inc="eT" if kc == NCH - 1 else None)
                t1 = vop(lambda e: e.tensor_mul(TMP[:, 0:N],
                                                pbank[0][0:128, 0:N], BCB[:, 0:N]),
                         waits=(lg_, tBg))
                banks[0] = t1
                t_sil2 = sop(lambda e: e.activation(TMP[:, 0:N], TMP[:, 0:N],
                                                    ACT.Silu), waits=(t1,))
                t2 = vop(lambda e: e.tensor_mul(TMP2[:, 0:N],
                                                pbank[1][0:128, 0:N], BCA[:, 0:N]),
                         waits=(lu_, tBu))
                banks[1] = t2
                t3 = vop(lambda e, oc=oc: e.tensor_mul(
                    INTER[:, oc * 256: oc * 256 + N], TMP[:, 0:N], TMP2[:, 0:N]),
                    waits=(t_sil2,))
            wb_free[bg] = lg_
            wb_free[bu] = lu_
        tok_inter = ("eV", C.cnt["eV"])
        for sub in range(2):
            st1 = None
            for a in range(64):
                bk = a % 2
                tm = mm(H128[:, :], INTER[:, a * 256 + sub * 128:
                                          a * 256 + (sub + 1) * 128],
                        pbank[bk][0:128, 0:128], True, True,
                        waits=(banks[bk], tok_inter) if a < 2 else (banks[bk],),
                        inc="eT")
                st1 = vop(lambda e, a=a, sub=sub, bk=bk: e.tensor_copy(
                    INTER[:, a * 256 + sub * 128: a * 256 + (sub + 1) * 128],
                    pbank[bk][0:128, 0:128]), waits=(tm,))
                banks[bk] = st1
            t_dump = dma("FW",
                         shufb.ap().rearrange("p (a t) -> p a t", t=128),
                         INTER[:, :].rearrange("p (a t2) -> p a t2", t2=256)
                         [:, :, sub * 128:(sub + 1) * 128], waits=(st1,))
            g_toks = []
            for hh in range(2):
                g_toks.append(dma(
                    "FW",
                    R1[hh * 64:(hh + 1) * 64, :]
                    .rearrange("a (c t) -> a c t", c=64, t=128),
                    shufb.ap().rearrange("(c two) (a t) -> two a c t",
                                         c=64, two=2, a=64, t=128)[hh],
                    waits=(t_dump,)))
            st2 = None
            for cc in range(64):
                bk = cc % 2
                tm = mm(BD64[:, :], R1[:, cc * 128:(cc + 1) * 128],
                        pbank[bk][0:128, 0:128], True, True,
                        waits=(banks[bk],) + tuple(g_toks) if cc < 2
                        else (banks[bk],), inc="eT")
                st2 = vop(lambda e, cc=cc, sub=sub, bk=bk: e.tensor_copy(
                    INTER[:, cc * 256 + sub * 128: cc * 256 + (sub + 1) * 128],
                    pbank[bk][0:128, 0:128]),
                    waits=(tm, t_dump) if cc == 0 else (tm,))
                banks[bk] = st2
        t_amx3 = absmax_to_row([INTER[:, cch * 256: cch * 256 + N]
                                for cch in range(64)], N, [st2])
        vop(lambda e: e.tensor_scalar(R_T[0:1, 0:N], R_MX[0:1, 0:N],
                                      LAM, None, AX.mult), waits=(t_amx3,))
        vop(lambda e: e.tensor_scalar(R_T[0:1, 0:N], R_T[0:1, 0:N],
                                      1e-5, None, AX.max))
        vop(lambda e: e.reciprocal(R_A[0:1, 0:N], R_T[0:1, 0:N]))
        t_cb = vop(lambda e: e.tensor_scalar(R_A[0:1, 0:N], R_A[0:1, 0:N],
                                             128.0 * LAM, None, AX.mult))
        t_osd = vop(lambda e: e.tensor_scalar(R_OS[0:1, 0:N], R_T[0:1, 0:N],
                                              R_CWQ[0:1, 4:5], None, AX.mult),
                    waits=(t_cwq,))
        tA = bcast_row(R_A[0:1, 0:N], BCA, N, [t_cb])
        tq = None
        for cch in range(64):
            ms = INTER[:, cch * 256: cch * 256 + N]
            vop(lambda e, h=ms: e.tensor_mul(TMP[:, 0:N], h, BCA[:, 0:N]),
                waits=(tA,) if cch == 0 else ())
            vop(lambda e: e.tensor_scalar(TMP[:, 0:N], TMP[:, 0:N],
                                          MAGIC, -MAGIC, AX.add, AX.add))
            tq = vop(lambda e, h=ms: e.tensor_scalar(
                h, TMP[:, 0:N], 127.0, None, AX.min))
        tB = bcast_row(R_OS[0:1, 0:N], BCB, N, [t_osd])
        for sl in range(8):
            lastml = [None, None]
            for jcg in range(4):
                b, tok_w = wload(wq_o["wd"].ap()
                                 [jcg * 2048:(jcg + 1) * 2048,
                                  sl * 256:(sl + 1) * 256],
                                 waits=(ag_toks["wd"],), queue="sync")
                for ml in range(2):
                    bk = ml
                    for kc in range(NCH):
                        jc = jcg * 16 + kc
                        lastml[ml] = mm(WB[b][:, kc * 256 + ml * 128:
                                              kc * 256 + (ml + 1) * 128],
                                        INTER[:, jc * 256:(jc + 1) * 256],
                                        pbank[bk][0:128, 0:N],
                                        jcg == 0 and kc == 0,
                                        jcg == 3 and kc == NCH - 1,
                                        waits=(tok_w, tq, banks[bk])
                                        if (jcg == 0 and kc == 0) else (tok_w,)
                                        if kc == 0 else (),
                                        inc="eT" if kc == NCH - 1 else None)
                wb_free[b] = lastml[1]
            for ml in range(2):
                oc = sl * 2 + ml
                bk = ml
                vop(lambda e, bk=bk: e.tensor_mul(
                    TMP2[:, 0:N], pbank[bk][0:128, 0:N], BCB[:, 0:N]),
                    waits=(lastml[ml], tB))
                ev = vop(lambda e, oc=oc, tok0=tok0: e.scalar_tensor_tensor(
                    X[:, oc * 1024 + tok0: oc * 1024 + tok0 + N],
                    TMP2[:, 0:N], MODV[:, 96 + 80 + oc:96 + 81 + oc],
                    X[:, oc * 1024 + tok0: oc * 1024 + tok0 + N],
                    AX.mult, AX.add), waits=(t_modv,))
                banks[bk] = ev
        res_tok[0] = banks[1]
    scopeD.close()
    C.soft_barrier()

    scopeE = contextlib.ExitStack()
    OUTS = scopeE.enter_context(nc.sbuf_tensor("OUTSx", [128, 2048], F32))
    out_toks = []
    prev_store = None
    for tsub in range(8):
        cps = []
        for cch in range(NCH):
            bk = cch % 4
            tt = tpose(X[:, cch * 1024 + tsub * 128: cch * 1024 + (tsub + 1) * 128],
                       IDF[:, :], pbank[bk][0:128, 0:128],
                       waits=(res_tok[0], banks[bk], prev_store))
            cp = vop(lambda e, cch=cch, bk=bk: e.tensor_copy(
                OUTS[:, cch * 128:(cch + 1) * 128], pbank[bk][0:128, 0:128]),
                waits=(tt,))
            banks[bk] = cp
            cps.append(cp)
        tok_o = dma("OS", out_d[tsub * 128:(tsub + 1) * 128, :], OUTS[:, :],
                    waits=(cps[-1],))
        prev_store = tok_o
        out_toks.append(tok_o)
    semh = C.semh
    fin = ("OS", C.cnt["OS"])
    C.q["sync"].append(lambda e, fin=fin: e.wait_ge(semh[fin[0]], fin[1]))
    scopeE.close()

    with nc.Block() as block:
        @block.tensor
        def _(e):
            for fn in C.q["tensor"]:
                fn(e)

        @block.vector
        def _(e):
            for fn in C.q["vector"]:
                fn(e)

        @block.scalar
        def _(e):
            for fn in C.q["scalar"]:
                fn(e)

        @block.gpsimd
        def _(e):
            for fn in C.q["gpsimd"]:
                fn(e)

        @block.sync
        def _(e):
            for fn in C.q["sync"]:
                fn(e)
    st.close()
    return nc


_NC_CACHE = [None]


def kernel(**inputs):
    in_maps = _host_prep(inputs)
    if _NC_CACHE[0] is None:
        _NC_CACHE[0] = build_program()
    nc = _NC_CACHE[0]
    res = run_bass_kernel_spmd(nc, in_maps, core_ids=list(range(8)))
    if os.environ.get("MK_DEBUG") == "1":
        kernel._dbg = res.results
    out = np.zeros((4, 2048, 2048), np.float32)
    for c in range(8):
        pair, par = c // 2, c % 2
        out[pair, par * 1024:(par + 1) * 1024, :] = res.results[c]["out"]
    return out


# version 07 (final)
# speedup vs baseline: 1.0503x; 1.0081x over previous
import contextlib
import os
import numpy as np
import ml_dtypes
import concourse.bass as bass
import concourse.mybir as mybir
from concourse.bass_utils import run_bass_kernel_spmd

F32 = mybir.dt.float32
BF16 = mybir.dt.bfloat16
AX = mybir.AluOpType
ACT = mybir.ActivationFunctionType
XL = mybir.AxisListType.X

BF = ml_dtypes.bfloat16
MAGIC = float(np.float32(1.5 * 2 ** 23))
NEG = -1.0e9
D = 2048
NCH = 16
LAM = float(np.float32(1.0 / np.sqrt(8192.0)))
SQD = float(np.sqrt(2048.0))
QPERM = [0, 4, 1, 5, 2, 6, 3, 7, 8, 12, 9, 13, 10, 14, 11, 15]
QMATS = [("w1", 2048, 8192), ("w2", 8192, 2048), ("wg", 2048, 8192),
         ("wu", 2048, 8192), ("wd", 8192, 2048)]
NELEM_Q = float(2048 * 8192)


def _hadamard(n):
    H = np.array([[1.0]], np.float32)
    while H.shape[0] < n:
        H = np.block([[H, H], [H, -H]])
    return H.astype(np.float32)


def _host_prep(inputs):
    f = lambda k: np.ascontiguousarray(np.asarray(inputs[k], np.float32))
    x = f("L0_attn_x")
    xT = np.ascontiguousarray(x.transpose(0, 2, 1))
    w_qkvT = np.ascontiguousarray(f("L0_attn_w_qkv").T)
    w_outT = np.ascontiguousarray(f("L0_attn_w_out").T)
    norm_w = f("L0_attn_norm_w").reshape(1, 2048)
    pm = f("L0_attn_pm")
    w1T = np.ascontiguousarray(f("L0_ffn_w1").T)
    w2T = np.ascontiguousarray(f("L0_ffn_w2").T)
    gT = np.ascontiguousarray(f("L1_ffn_w_gate").T)
    uT = np.ascontiguousarray(f("L1_ffn_w_up").T)
    dT = np.ascontiguousarray(f("L1_ffn_w_down").T)
    cidx = np.arange(64)[:, None, None]
    hidx = np.arange(2)[None, :, None]
    aidx = np.arange(64)[None, None, :]
    jmap = (aidx * 128 + 2 * cidx + hidx).reshape(-1)
    dTP = np.ascontiguousarray(dT[jmap])
    wmod0 = f("L0_wmod"); wmod1 = f("L1_wmod")
    c0 = f("L0_c"); c1 = f("L1_c")
    Wq = f("L1_attn_Wq"); Wk = f("L1_attn_Wk")
    Wv = f("L1_attn_Wv"); Wo = f("L1_attn_Wo")

    idb = np.eye(128, dtype=np.float32).astype(BF)
    idf = np.eye(128, dtype=np.float32)
    onescol = np.ones((128, 1), np.float32).astype(BF)
    onesrow = np.ones((1, 128), np.float32)
    one11 = np.ones((1, 1), np.float32)
    inv0 = 1.0 / (10000.0 ** (np.arange(0, 128, 2, dtype=np.float32) / 128.0))
    fr = np.arange(2048, dtype=np.float32)[:, None] * inv0[None, :]
    fr = np.repeat(fr, 2, axis=-1)
    l0cos_all = np.cos(fr).T.astype(BF)
    l0sin_all = np.sin(fr).T.astype(BF)
    P0 = np.zeros((128, 128), np.float32)
    for i in range(64):
        P0[2 * i, 2 * i + 1] = -1.0
        P0[2 * i + 1, 2 * i] = 1.0
    prot0 = np.ascontiguousarray(P0.T).astype(BF)
    inv1 = 1.0 / (10000.0 ** (np.arange(32, dtype=np.float32) * 2.0 / 64.0))
    ang = np.arange(2048, dtype=np.float32)[:, None] * inv1[None, :]
    c1t = np.cos(ang).T; s1t = np.sin(ang).T
    l1cos = np.zeros((128, 2048), np.float32)
    l1sin = np.zeros((128, 2048), np.float32)
    for hh in range(2):
        for half in range(2):
            rows = slice(hh * 64 + half * 32, hh * 64 + half * 32 + 32)
            l1cos[rows] = c1t
            l1sin[rows] = s1t
    l1cos = l1cos.astype(BF); l1sin = l1sin.astype(BF)
    P1 = np.zeros((128, 128), np.float32)
    for hh in range(2):
        b0 = hh * 64
        for i in range(32):
            P1[b0 + i, b0 + 32 + i] = -1.0
            P1[b0 + 32 + i, b0 + i] = 1.0
    prot1 = np.ascontiguousarray(P1.T).astype(BF)
    mb = np.full((512, 528), NEG, np.float32)
    mb[:, :32] = 0.0
    qi = np.arange(496)[:, None]
    kj = np.arange(496)[None, :]
    mb[16:, 32:] = np.where(kj <= qi, np.float32(0), np.float32(NEG))
    mbT = mb.T
    l0mask = np.ascontiguousarray(np.concatenate(
        [mbT[16 + kc * 128:16 + (kc + 1) * 128, :] for kc in range(4)],
        axis=1)).astype(BF)
    kk = np.arange(128)[:, None]
    qq = np.arange(512)[None, :]
    l1mask = np.concatenate(
        [np.where(kk <= qq - 128 * j, np.float32(0), np.float32(NEG))
         for j in range(4)], axis=1).astype(BF)
    h128 = _hadamard(128).astype(BF)
    h64 = _hadamard(64)
    bd = np.zeros((128, 128), np.float32)
    bd[:64, :64] = h64
    bd[64:, 64:] = h64
    bd64 = bd.astype(BF)
    pmkt_sb = np.zeros((128, 256), np.float32)
    for h in range(16):
        pmkt_sb[:, h * 16:(h + 1) * 16] = pm[0][h].T
    pmv_sb = np.zeros((16, 2048), np.float32)
    for h in range(16):
        pmv_sb[:, h * 128:(h + 1) * 128] = pm[1][h]

    shared = dict(wqkvT=w_qkvT, woutT=w_outT, normw=norm_w, Wo=Wo,
                  idb=idb, idf=idf, onescol=onescol, onesrow=onesrow,
                  one11=one11, prot0=prot0, prot1=prot1, l1cos=l1cos,
                  l1sin=l1sin, l0mask=l0mask, l1mask=l1mask, h128=h128,
                  bd64=bd64, pmkt=pmkt_sb, pmv=pmv_sb)
    in_maps = []
    for c in range(8):
        pair, par = c // 2, c % 2
        m = dict(shared)
        m["xT"] = np.ascontiguousarray(xT[pair][:, par * 1024:(par + 1) * 1024])
        m["l0cos"] = np.ascontiguousarray(l0cos_all[:, par * 1024:(par + 1) * 1024])
        m["l0sin"] = np.ascontiguousarray(l0sin_all[:, par * 1024:(par + 1) * 1024])
        heads = [par * 16 + j for j in QPERM]
        m["Wq_own"] = np.ascontiguousarray(
            np.concatenate([Wq[:, h * 64:(h + 1) * 64] for h in heads], axis=1))
        m["Wk_own"] = np.ascontiguousarray(Wk[:, par * 256:(par + 1) * 256])
        m["Wv_own"] = np.ascontiguousarray(Wv[:, par * 256:(par + 1) * 256])
        for name, I, O in QMATS:
            src = {"w1": w1T, "w2": w2T, "wg": gT, "wu": uT, "wd": dTP}[name]
            rs = I // 8
            m[name + "_sl"] = np.ascontiguousarray(src[c * rs:(c + 1) * rs])
        m["wmod0_sl"] = np.ascontiguousarray(wmod0[c * 256:(c + 1) * 256])
        m["wmod1_sl"] = np.ascontiguousarray(wmod1[c * 256:(c + 1) * 256])
        oh = np.zeros((128, 768), np.float32)
        oh[:, pair::4] = 1.0
        m["ohmask"] = oh
        m["c0s"] = np.ascontiguousarray(c0[:, c * 256:(c + 1) * 256])
        m["c1s"] = np.ascontiguousarray(c1[:, c * 256:(c + 1) * 256])
        in_maps.append(m)
    return in_maps


class Ctx:
    def __init__(self):
        self.q = {e: [] for e in ("tensor", "vector", "scalar", "gpsimd", "sync")}
        self.semh = {}
        self.cnt = {}

    def push(self, eng, fn, waits=(), inc=None, incn=1):
        waits = tuple(w for w in waits if w is not None)
        if eng == "vector":
            if self.cnt.get("eV", 0) > 0:
                waits = waits + (("eV", self.cnt["eV"]),)
            if inc is None:
                inc = "eV"
        elif eng == "scalar":
            if self.cnt.get("eS", 0) > 0:
                waits = waits + (("eS", self.cnt["eS"]),)
            if inc is None:
                inc = "eS"
        tok = None
        if inc is not None:
            self.cnt[inc] += incn
            tok = (inc, self.cnt[inc])
        semh = self.semh

        def thunk(e, fn=fn, waits=waits, inc=inc, incn=incn):
            for (s, v) in waits:
                e.wait_ge(semh[s], v)
            inst = fn(e)
            if inc is not None:
                inst.then_inc(semh[inc], incn)
        self.q[eng].append(thunk)
        return tok

    def barrier(self, exclude=()):
        snap = [(s, v) for s, v in self.cnt.items() if v > 0 and s not in exclude]
        semh = self.semh
        for eng in self.q:
            def thunk(e, snap=tuple(snap)):
                for (s, v) in snap:
                    e.wait_ge(semh[s], v)
            self.q[eng].append(thunk)

    def soft_barrier(self):
        self.barrier(exclude=("cC", "PC"))


def build_program():
    nc = bass.Bass("TRN2", target_bir_lowering=False)
    C = Ctx()

    def din(name, shape, dt=F32):
        return nc.dram_tensor(name, shape, dt, kind="ExternalInput").ap()

    xT_d = din("xT", [2048, 1024])
    wqkvT_d = din("wqkvT", [2048, 6144])
    woutT_d = din("woutT", [2048, 2048])
    normw_d = din("normw", [1, 2048])
    pmkt_d = din("pmkt", [128, 256])
    pmv_d = din("pmv", [16, 2048])
    Wo_d = din("Wo", [2048, 2048])
    Wq_d = din("Wq_own", [2048, 1024])
    Wk_d = din("Wk_own", [2048, 256])
    Wv_d = din("Wv_own", [2048, 256])
    idb_d = din("idb", [128, 128], BF16)
    idf_d = din("idf", [128, 128])
    onescol_d = din("onescol", [128, 1], BF16)
    onesrow_d = din("onesrow", [1, 128])
    one11_d = din("one11", [1, 1])
    prot0_d = din("prot0", [128, 128], BF16)
    prot1_d = din("prot1", [128, 128], BF16)
    l0cos_d = din("l0cos", [128, 1024], BF16)
    l0sin_d = din("l0sin", [128, 1024], BF16)
    l1cos_d = din("l1cos", [128, 2048], BF16)
    l1sin_d = din("l1sin", [128, 2048], BF16)
    l0mask_d = din("l0mask", [128, 2048], BF16)
    l1mask_d = din("l1mask", [128, 2048], BF16)
    h128_d = din("h128", [128, 128], BF16)
    bd64_d = din("bd64", [128, 128], BF16)
    wmod_sl_d = {0: din("wmod0_sl", [256, 12288]), 1: din("wmod1_sl", [256, 12288])}
    oh_d = din("ohmask", [128, 768])
    c0s_d = din("c0s", [4, 256])
    c1s_d = din("c1s", [4, 256])
    qsl_d = {name: din(name + "_sl", [I // 8, O]) for name, I, O in QMATS}
    out_d = nc.dram_tensor("out", [1024, 2048], F32, kind="ExternalOutput").ap()
    DBG = os.environ.get("MK_DEBUG") == "1"
    dbg = {}
    if DBG:
        dbg["modv"] = nc.dram_tensor("dbg_modv", [128, 192], F32, kind="ExternalOutput").ap()
        dbg["cw"] = nc.dram_tensor("dbg_cw", [1, 8], F32, kind="ExternalOutput").ap()
        dbg["cwq"] = nc.dram_tensor("dbg_cwq", [1, 8], F32, kind="ExternalOutput").ap()
        dbg["stg"] = nc.dram_tensor("dbg_stg", [1, 8], F32, kind="ExternalOutput").ap()
        dbg["mp"] = nc.dram_tensor("dbg_mp", [128, 768], F32, kind="ExternalOutput").ap()
        dbg["ra"] = nc.dram_tensor("dbg_ra", [1, 512], F32, kind="ExternalOutput").ap()
        dbg["rb"] = nc.dram_tensor("dbg_rb", [1, 512], F32, kind="ExternalOutput").ap()
        dbg["sm"] = nc.dram_tensor("dbg_sm", [1, 512], F32, kind="ExternalOutput").ap()
        dbg["ssr"] = nc.dram_tensor("dbg_ssr", [1, 512], F32, kind="ExternalOutput").ap()
        dbg["flag"] = [0]
        dbg["h"] = nc.dram_tensor("dbg_h", [128, 8192], BF16, kind="ExternalOutput").ap()
        dbg["xn"] = nc.dram_tensor("dbg_xn", [128, 8192], BF16, kind="ExternalOutput").ap()
        dbg["x1"] = nc.dram_tensor("dbg_x1", [128, 16384], F32, kind="ExternalOutput").ap()
        dbg["x2"] = nc.dram_tensor("dbg_x2", [128, 16384], F32, kind="ExternalOutput").ap()
        dbg["x3"] = nc.dram_tensor("dbg_x3", [128, 16384], F32, kind="ExternalOutput").ap()
        dbg["ctx"] = nc.dram_tensor("dbg_ctx", [128, 8192], BF16, kind="ExternalOutput").ap()
        dbg["xq"] = nc.dram_tensor("dbg_xq", [128, 4096], BF16, kind="ExternalOutput").ap()
        dbg["mid"] = nc.dram_tensor("dbg_mid", [128, 16384], BF16, kind="ExternalOutput").ap()
        dbg["mint"] = nc.dram_tensor("dbg_mint", [128, 16384], BF16, kind="ExternalOutput").ap()
        dbg["q1ss"] = nc.dram_tensor("dbg_q1ss", [1, 256], F32, kind="ExternalOutput").ap()
        dbg["q1mx"] = nc.dram_tensor("dbg_q1mx", [1, 256], F32, kind="ExternalOutput").ap()
        dbg["q1cb"] = nc.dram_tensor("dbg_q1cb", [1, 256], F32, kind="ExternalOutput").ap()
        dbg["q1os"] = nc.dram_tensor("dbg_q1os", [1, 256], F32, kind="ExternalOutput").ap()
        dbg["q2cb"] = nc.dram_tensor("dbg_q2cb", [1, 256], F32, kind="ExternalOutput").ap()
        dbg["q2os"] = nc.dram_tensor("dbg_q2os", [1, 256], F32, kind="ExternalOutput").ap()
        dbg["hffn"] = nc.dram_tensor("dbg_hffn", [128, 8192], BF16, kind="ExternalOutput").ap()
        dbg["w1q"] = nc.dram_tensor("dbg_w1q", [2048, 8192], BF16, kind="ExternalOutput").ap()
        dbg["w2q"] = nc.dram_tensor("dbg_w2q", [8192, 2048], BF16, kind="ExternalOutput").ap()

    modb_i = nc.dram_tensor("modb_i", [128, 768], F32)
    modb_o = nc.dram_tensor("modb_o", [128, 768], F32)
    absb_i = nc.dram_tensor("absb_i", [1, 8], F32)
    absb_o = nc.dram_tensor("absb_o", [1, 8], F32)
    wq_i = {name: nc.dram_tensor("wqi_" + name, [I // 8, O], BF16)
            for name, I, O in QMATS}
    wq_o = {name: nc.dram_tensor("wqo_" + name, [I, O], BF16,
                                 addr_space="Shared")
            for name, I, O in QMATS}
    wbf = {nm: nc.dram_tensor("wbf_" + nm, shp, BF16)
           for nm, shp in [("wqkv", [2048, 6144]), ("wout", [2048, 2048]),
                           ("Wq", [2048, 1024]), ("Wk", [2048, 256]),
                           ("Wv", [2048, 256]), ("Wo", [2048, 2048])]}
    hb_i = nc.dram_tensor("hb_i", [2048, 1024], BF16)
    hb_o = nc.dram_tensor("hb_o", [4096, 1024], BF16)
    shufb = nc.dram_tensor("shufb", [128, 8192], BF16)
    ctx2b = nc.dram_tensor("ctx2b", [2048, 1024], BF16)

    st = contextlib.ExitStack()
    sb = lambda n, s, d: st.enter_context(nc.sbuf_tensor(n, s, d))
    psa = lambda n, s, d: st.enter_context(nc.psum_tensor(n, s, d))

    X = sb("X", [128, NCH * 1024], F32)
    H = sb("H", [128, NCH * 512], BF16)
    WB = [sb("WBa", [128, 4096], BF16), sb("WBb", [128, 4096], BF16)]
    TMP = sb("TMPa", [128, 512], F32)
    TMP2 = sb("TMPb", [128, 512], F32)
    TMP3 = sb("TMPc", [128, 4], F32)
    BCA = sb("BCAx", [128, 512], F32)
    BCB = sb("BCBx", [128, 512], F32)
    XB = sb("XBx", [128, 1024], BF16)
    XB2 = sb("XB2x", [128, 1024], BF16)
    IDB = sb("IDBx", [128, 128], BF16)
    IDF = sb("IDFx", [128, 128], F32)
    ONESC = sb("ONESCx", [128, 1], BF16)
    ONESR = sb("ONESRx", [1, 128], F32)
    ONE11 = sb("ONE11x", [1, 1], F32)
    PROT1 = sb("PROT1x", [128, 128], BF16)
    H128 = sb("H128x", [128, 128], BF16)
    BD64 = sb("BD64x", [128, 128], BF16)
    NORMW = sb("NORMWx", [128, 16], F32)
    MODV = sb("MODVx", [128, 192], F32)
    R_SM = sb("R_SMx", [1, 512], F32)
    R_SS = sb("R_SSx", [1, 512], F32)
    R_A = sb("R_Ax", [1, 512], F32)
    R_B = sb("R_Bx", [1, 512], F32)
    R_MX = sb("R_MXx", [1, 512], F32)
    R_T = sb("R_Tx", [1, 512], F32)
    R_OS = sb("R_OSx", [1, 512], F32)
    R_CW = sb("R_CWx", [1, 8], F32)
    R_SQ = sb("R_SQx", [1, 8], F32)
    R_CWQ = sb("R_CWQx", [1, 8], F32)
    MCOL = sb("MCOLx", [128, 20], F32)
    SCL = sb("SCLx", [128, 2], F32)
    DCOL = sb("DCOLx", [128, 1], F32)
    EPSC = sb("EPSCx", [1, 2], F32)

    pbank = [psa("pb%d" % i, [128, 512], F32) for i in range(4)]
    pSTAT = psa("pSTATx", [128, 512], F32)
    pSTAT2 = psa("pSTAT2x", [128, 512], F32)
    pBF = psa("pBFx", [128, 1024], BF16)
    pSM = psa("pSMx", [128, 512], F32)

    DSTREAMS = ("CS", "WB0", "WB1", "QS", "WQ", "MB", "HS", "HB", "CX",
                "CXL", "OS", "FW", "XS", "PC")
    for s in ("cC", "eT", "eV", "eS") + DSTREAMS:
        C.semh[s] = st.enter_context(nc.semaphore(s))
        C.cnt[s] = 0

    QMAP = {"CS": "sync", "WB0": "sync", "WB1": "scalar", "QS": "sync",
            "WQ": "sync", "MB": "sync", "HS": "sync", "HB": "sync",
            "CX": "sync", "CXL": "sync", "OS": "sync", "FW": "gpsimd",
            "XS": "sync", "PC": "gpsimd"}

    SERIAL = {"CS", "MB", "HB", "CX", "OS", "WQ", "QS", "HS", "XS"}

    def dma(stream, out, in_, waits=(), queue=None):
        q = queue or QMAP[stream]
        w = tuple(waits)
        if stream in SERIAL and C.cnt[stream] > 0:
            w = w + ((stream, C.cnt[stream]),)
        return C.push(q, lambda e, o=out, i=in_: e.dma_start(out=o, in_=i),
                      waits=w, inc=stream, incn=16)

    def scur(stream):
        return (stream, C.cnt[stream])

    def mm(lhsT, rhs, out, start, stop, waits=(), inc=None):
        return C.push("tensor",
                      lambda e, l=lhsT, r=rhs, o=out, s1=start, s2=stop:
                      e.matmul(o, l, r, start=s1, stop=s2),
                      waits=waits, inc=inc)

    def tpose(in_, ident, out, waits=(), inc="eT"):
        return C.push("tensor", lambda e, i=in_, d=ident, o=out:
                      e.transpose(o, i, d), waits=waits, inc=inc)

    def vop(fn, waits=(), inc="eV"):
        return C.push("vector", fn, waits=waits, inc=inc)

    def sop(fn, waits=(), inc="eS"):
        return C.push("scalar", fn, waits=waits, inc=inc)

    def wsync(eng, waits):
        waits = [w for w in waits if w is not None]
        semh = C.semh

        def thunk(e, waits=tuple(waits)):
            for (sname, v) in waits:
                e.wait_ge(semh[sname], v)
        C.q[eng].append(thunk)

    banks = [None, None, None, None]
    pSM_free = [None]
    pBF_free = [None]
    stat_free = [None, None]
    res_tok = [None]
    wb_free = [None, None]
    wb_i = [0]

    def bcast_row(row_ap, out_sb, n, waits):
        t1 = mm(ONESR[:, :], row_ap, pSM[0:128, 0:n], True, True,
                waits=tuple(waits) + (pSM_free[0],), inc="eT")
        t2 = vop(lambda e, o=out_sb, n=n: e.tensor_copy(o[:, 0:n], pSM[0:128, 0:n]),
                 waits=(t1,))
        pSM_free[0] = t2
        return t2

    def negbias_from_col(col_ap, scale, outcol, waits):
        t1 = tpose(col_ap, IDF[:, :], pSM[0:1, 0:128],
                   waits=tuple(waits) + (pSM_free[0],))
        t2 = vop(lambda e: e.reduce_max(R_T[0:1, 0:1], pSM[0:1, 0:128], XL),
                 waits=(t1,))
        t3 = mm(ONESR[:, :], R_T[0:1, 0:1], pSM[0:128, 0:1], True, True,
                waits=(t2,), inc="eT")
        t4 = vop(lambda e, oc=outcol, sc=scale: e.tensor_scalar(
            MCOL[:, oc:oc + 1], pSM[0:128, 0:1], -sc, None, AX.mult), waits=(t3,))
        pSM_free[0] = t4
        return t4

    def absmax_to_row(tiles, N, waits):
        t = vop(lambda e, a=tiles[0], N=N: e.tensor_mul(TMP2[:, 0:N], a, a),
                waits=tuple(waits))
        for ap_ in tiles[1:]:
            vop(lambda e, a=ap_, N=N: e.tensor_mul(TMP[:, 0:N], a, a))
            t = vop(lambda e, N=N: e.tensor_max(TMP2[:, 0:N], TMP2[:, 0:N],
                                                TMP[:, 0:N]))
        last = None
        for j in range(N // 128):
            tt = tpose(TMP2[:, j * 128:(j + 1) * 128], IDF[:, :],
                       pSM[0:128, 0:128], waits=(t, pSM_free[0]))
            tr = vop(lambda e: e.reduce_max(TMP3[:, 0:1], pSM[0:128, 0:128], XL),
                     waits=(tt,))
            pSM_free[0] = tr
            tt2 = tpose(TMP3[:, 0:1], IDF[:, :], pSM[0:1, 0:128],
                        waits=(tr, pSM_free[0]))
            last = vop(lambda e, j=j: e.tensor_copy(
                R_MX[0:1, j * 128:(j + 1) * 128], pSM[0:1, 0:128]), waits=(tt2,))
            pSM_free[0] = last
        sop(lambda e, N=N: e.activation(R_MX[0:1, 0:N], R_MX[0:1, 0:N], ACT.Ln),
            waits=(last,))
        last = sop(lambda e, N=N: e.activation(R_MX[0:1, 0:N], R_MX[0:1, 0:N],
                                               ACT.Exp, scale=0.5))
        return last

    def ln_modulated(l, modofs, x_tok0, N, t_modv):
        base = l * 96 + modofs
        m1 = m2 = None
        m2s = [None] * NCH
        for cch in range(NCH):
            sl_ = (cch % 2) * 512
            xc = X[:, cch * 1024 + x_tok0: cch * 1024 + x_tok0 + N]
            t_cast = sop(lambda e, xc=xc, N=N, sl_=sl_: e.copy(
                XB[:, sl_:sl_ + N], xc),
                waits=(res_tok[0], m2s[cch - 2] if cch >= 2 else None))
            t_sq = vop(lambda e, N=N, sl_=sl_: e.tensor_mul(
                XB2[:, sl_:sl_ + N], XB[:, sl_:sl_ + N], XB[:, sl_:sl_ + N]),
                waits=(t_cast,))
            m1 = mm(ONESC[:, :], XB[:, sl_:sl_ + N], pSTAT[0:1, 0:N],
                    cch == 0, cch == NCH - 1,
                    waits=(t_cast, stat_free[0]) if cch == 0 else (t_cast,),
                    inc="eT")
            m2 = mm(ONESC[:, :], XB2[:, sl_:sl_ + N], pSTAT2[0:1, 0:N],
                    cch == 0, cch == NCH - 1,
                    waits=(t_sq, stat_free[1]) if cch == 0 else (t_sq,),
                    inc="eT")
            m2s[cch] = m2
        t_mean = vop(lambda e, N=N: e.tensor_scalar(
            R_SM[0:1, 0:N], pSTAT[0:1, 0:N], 1.0 / D, None, AX.mult), waits=(m1,))
        t_ss = vop(lambda e, N=N: e.tensor_scalar(
            R_SS[0:1, 0:N], pSTAT2[0:1, 0:N], 1.0 / D, None, AX.mult), waits=(m2,))
        stat_free[0] = t_mean
        stat_free[1] = t_ss
        vop(lambda e, N=N: e.tensor_mul(R_T[0:1, 0:N], R_SM[0:1, 0:N], R_SM[0:1, 0:N]))
        t_var = vop(lambda e, N=N: e.tensor_sub(R_SS[0:1, 0:N], R_SS[0:1, 0:N],
                                                R_T[0:1, 0:N]))
        t_ln = sop(lambda e, N=N: e.activation(R_A[0:1, 0:N], R_SS[0:1, 0:N],
                                               ACT.Ln, bias=EPSC[0:1, 0:1]),
                   waits=(t_var,))
        t_a = sop(lambda e, N=N: e.activation(R_A[0:1, 0:N], R_A[0:1, 0:N],
                                              ACT.Exp, scale=-0.5), waits=(t_ln,))
        t_bm = vop(lambda e, N=N: e.tensor_mul(R_B[0:1, 0:N], R_SM[0:1, 0:N],
                                               R_A[0:1, 0:N]), waits=(t_a,))
        t_bn = vop(lambda e, N=N: e.tensor_scalar(R_B[0:1, 0:N], R_B[0:1, 0:N],
                                                  -1.0, None, AX.mult))
        if DBG and not dbg["flag"][0]:
            dbg["flag"][0] = 1
            dma("CS", dbg["ra"], R_A[0:1, :], waits=(t_a,))
            dma("CS", dbg["rb"], R_B[0:1, :], waits=(t_bn,))
            dma("CS", dbg["sm"], R_SM[0:1, :], waits=(t_mean,))
            dma("CS", dbg["ssr"], R_SS[0:1, :], waits=(t_var,))
        tA = bcast_row(R_A[0:1, 0:N], BCA, N, [t_a])
        tB = bcast_row(R_B[0:1, 0:N], BCB, N, [t_bn])
        done = None
        for cch in range(NCH):
            xc = X[:, cch * 1024 + x_tok0: cch * 1024 + x_tok0 + N]
            hs = H[:, cch * 512: cch * 512 + N]
            vop(lambda e, xc=xc, N=N: e.tensor_mul(TMP[:, 0:N], xc, BCA[:, 0:N]),
                waits=(tA, tB) if cch == 0 else ())
            vop(lambda e, N=N: e.tensor_add(TMP[:, 0:N], TMP[:, 0:N], BCB[:, 0:N]))
            done = vop(lambda e, h=hs, cch=cch, N=N, base=base: e.tensor_scalar(
                h, TMP[:, 0:N], MODV[:, base + 16 + cch:base + 17 + cch],
                MODV[:, base + cch:base + 1 + cch], AX.mult, AX.add),
                waits=(t_modv,) if cch == 0 else ())
        return done

    def wload(src_slice_ap, waits=(), queue=None):
        b = wb_i[0]
        wb_i[0] = 1 - b
        tok = dma("WB%d" % b, WB[b][:, :].rearrange("p (kc m) -> p kc m", m=256),
                  src_slice_ap.rearrange("(kc p) m -> p kc m", p=128),
                  waits=tuple(waits) + (wb_free[b],), queue=queue)
        return b, tok

    for dst, src in [(IDB, idb_d), (IDF, idf_d), (ONESC, onescol_d),
                     (ONESR, onesrow_d), (ONE11, one11_d), (PROT1, prot1_d),
                     (H128, h128_d), (BD64, bd64_d)]:
        tok_consts = dma("CS", dst[:, :], src)
    t_x = dma("XS", X[:, :].rearrange("p (c t) -> p c t", t=1024),
              xT_d.rearrange("(c p) t -> p c t", p=128))
    res_tok[0] = t_x
    vop(lambda e: e.memset(EPSC[0:1, 0:1], 1e-5))
    vop(lambda e: e.memset(EPSC[0:1, 1:2], 1e-6))

    scope0 = contextlib.ExitStack()
    sb0 = lambda n, s, d: scope0.enter_context(nc.sbuf_tensor(n, s, d))
    NWROW = sb0("NWROWx", [1, 2048], F32)
    CSL = sb0("CSLx", [4, 512], F32)
    CSLB = sb0("CSLBx", [4, 512], BF16)
    SILC = sb0("SILCx", [128, 16], BF16)
    MP = sb0("MPx", [128, 768], F32)
    OH = sb0("OHx", [128, 768], F32)
    QSL = sb0("QSLx", [128, 8192], F32)
    QSL2 = sb0("QSL2x", [128, 8192], F32)
    QBW = H
    STG = sb0("STGx", [1, 8], F32)

    t_nwr = dma("CS", NWROW[:, :], normw_d)
    nw_last = None
    for cch in range(NCH):
        nw_last = mm(NWROW[0:1, cch * 128:(cch + 1) * 128], ONE11[:, :],
                     pSM[0:128, cch:cch + 1], True, True,
                     waits=(t_nwr, tok_consts, pSM_free[0]) if cch == 0 else (),
                     inc="eT" if cch == NCH - 1 else None)
    t_nw = vop(lambda e: e.tensor_copy(NORMW[:, :], pSM[0:128, 0:16]),
               waits=(nw_last,))
    pSM_free[0] = t_nw

    t_c0 = dma("CS", CSL[:, 0:256], c0s_d)
    t_c1 = dma("CS", CSL[:, 256:512], c1s_d)
    t_sil = sop(lambda e: e.activation(CSL[:, :], CSL[:, :], ACT.Silu),
                waits=(t_c0, t_c1))
    t_silb = sop(lambda e: e.copy(CSLB[:, :], CSL[:, :]))
    tp_last = None
    for l in range(2):
        for kc in range(2):
            tp_last = tpose(CSLB[0:4, l * 256 + kc * 128: l * 256 + (kc + 1) * 128],
                            IDB[0:4, 0:4],
                            pBF[:, (l * 2 + kc) * 4:(l * 2 + kc) * 4 + 4],
                            waits=(t_silb, tok_consts, pBF_free[0]))
    t_silc = vop(lambda e: e.tensor_copy(SILC[:, :], pBF[0:128, 0:16]),
                 waits=(tp_last,))
    pBF_free[0] = t_silc

    for l in range(2):
        for sl in range(6):
            wsl = wmod_sl_d[l][:, sl * 2048:(sl + 1) * 2048]
            b = wb_i[0]
            wb_i[0] = 1 - b
            tok_w = dma("WB%d" % b,
                        WB[b][:, :].rearrange("p (kc m) -> p kc m", m=2048),
                        wsl.rearrange("(kc p) m -> p kc m", p=128),
                        waits=(wb_free[b],), queue="gpsimd")
            last_mm = None
            for ml in range(16):
                mc = sl * 16 + ml
                bk = mc % 2
                mm(WB[b][:, ml * 128: (ml + 1) * 128], SILC[0:128, l * 8: l * 8 + 4],
                   pbank[bk][0:128, 0:4], True, False,
                   waits=(tok_w, t_silc, banks[bk]))
                last_mm = mm(WB[b][:, 2048 + ml * 128: 2048 + (ml + 1) * 128],
                             SILC[0:128, l * 8 + 4: l * 8 + 8],
                             pbank[bk][0:128, 0:4], False, True, inc="eT")
                ev = vop(lambda e, mc=mc, l=l, bk=bk: e.tensor_copy(
                    MP[:, l * 384 + mc * 4: l * 384 + mc * 4 + 4],
                    pbank[bk][0:128, 0:4]), waits=(last_mm,))
                banks[bk] = ev
                if ml == 15:
                    wb_free[b] = last_mm
    t_mpd = dma("MB", modb_i.ap(), MP[:, :], waits=(banks[0], banks[1]))
    t_ar_mod = C.push("gpsimd", lambda e: e.collective_compute(
        "AllReduce", AX.add, replica_groups=[list(range(8))],
        ins=[modb_i.ap()], outs=[modb_o.ap()]), waits=(t_mpd,), inc="cC")
    pc_tok = {}
    pc_tok["wqkv"] = dma("PC", wbf["wqkv"].ap(), wqkvT_d)

    QBUFS = [QSL, QSL2]
    qb_last = [None, None]
    qb_i = [0]
    prev = None
    for mi, (name, I, O) in enumerate(QMATS):
        na = I // 8 // 128
        for hf in range(2):
            qb = qb_i[0]
            qb_i[0] = 1 - qb
            tok_l = dma("QS", QBUFS[qb][:, :].rearrange("p (a o) -> p a o", o=O),
                        qsl_d[name].rearrange("(a p) o -> p a o", p=128)
                        [:, hf * na // 2:(hf + 1) * na // 2, :],
                        waits=(qb_last[qb],))
            prev = vop(lambda e, hf=hf, qb=qb: e.reduce_sum(
                TMP3[:, hf + 1:hf + 2], QBUFS[qb][:, :], XL,
                apply_absolute_value=True), waits=(tok_l,))
            qb_last[qb] = prev
        t_add = vop(lambda e: e.tensor_add(TMP3[:, 1:2], TMP3[:, 1:2], TMP3[:, 2:3]))
        tt = tpose(TMP3[:, 1:2], IDF[:, :], pSM[0:1, 0:128],
                   waits=(t_add, pSM_free[0]))
        ts_ = vop(lambda e, mi=mi: e.reduce_sum(
            STG[0:1, mi:mi + 1], pSM[0:1, 0:128], XL), waits=(tt,))
        pSM_free[0] = ts_
        prev = ts_
    if DBG:
        dma("CS", dbg["stg"], STG[0:1, :], waits=(prev,))
    t_absd = dma("HS", absb_i.ap(), STG[0:1, :], waits=(prev,))
    t_ar_abs = C.push("gpsimd", lambda e: e.collective_compute(
        "AllReduce", AX.add, replica_groups=[list(range(8))],
        ins=[absb_i.ap()], outs=[absb_o.ap()]), waits=(t_absd,), inc="cC")

    t_oh = dma("CS", OH[:, :], oh_d)
    t_mpb = dma("MB", MP[:, :], modb_o.ap(), waits=(t_ar_mod,))
    vop(lambda e: e.tensor_mul(MP[:, :], MP[:, :], OH[:, :]), waits=(t_mpb, t_oh))
    t_sel = vop(lambda e: e.tensor_reduce(
        MODV[:, :], MP[:, :].rearrange("p (m r) -> p m r", r=4), XL, AX.add))
    if DBG:
        dma("CS", dbg["mp"], MP[:, :], waits=(t_sel,))
    t_modv = t_sel
    prev_fix = t_sel
    for l in range(2):
        base = l * 96
        for gofs in (32, 80):
            cols = slice(base + gofs, base + gofs + 16)
            vop(lambda e, c=cols: e.tensor_scalar(
                TMP[:, 0:16], MODV[:, c], 0.0, None, AX.is_ge), waits=(t_sel,))
            tb = vop(lambda e: e.tensor_scalar(
                TMP[:, 0:16], TMP[:, 0:16], 2.0, -1.0, AX.mult, AX.add))
            tc2 = sop(lambda e, c=cols: e.activation(
                TMP2[:, 0:16], MODV[:, c], ACT.Abs), waits=(t_sel, prev_fix))
            td = vop(lambda e: e.tensor_scalar(
                TMP2[:, 0:16], TMP2[:, 0:16], 0.25, None, AX.max), waits=(tc2,))
            t_modv = vop(lambda e, c=cols: e.tensor_mul(
                MODV[:, c], TMP[:, 0:16], TMP2[:, 0:16]), waits=(tb, td))
            prev_fix = t_modv
        for cofs in (16, 64):
            cols = slice(base + cofs, base + cofs + 16)
            t_modv = vop(lambda e, c=cols: e.tensor_scalar(
                MODV[:, c], MODV[:, c], 1.0, None, AX.add), waits=(t_sel,))

    t_absb = dma("HS", R_CW[0:1, 0:8], absb_o.ap(), waits=(t_ar_abs,))
    vop(lambda e: e.tensor_scalar(R_CW[0:1, 0:8], R_CW[0:1, 0:8],
                                  1.0 / NELEM_Q, None, AX.mult), waits=(t_absb,))
    vop(lambda e: e.tensor_scalar(R_CW[0:1, 0:8], R_CW[0:1, 0:8],
                                  1e-5, None, AX.max))
    t_sq_r = vop(lambda e: e.reciprocal(R_SQ[0:1, 0:8], R_CW[0:1, 0:8]))
    vop(lambda e: e.tensor_scalar(R_CWQ[0:1, 0:2], R_CW[0:1, 0:2],
                                  1.0 / 127.0, None, AX.mult))
    t_cwq = vop(lambda e: e.tensor_scalar(R_CWQ[0:1, 2:5], R_CW[0:1, 2:5],
                                          1.0 / 128.0, None, AX.mult))
    if DBG:
        dma("CS", dbg["modv"], MODV[:, :], waits=(t_modv,))
        dma("CS", dbg["cw"], R_CW[0:1, :], waits=(t_cwq,))
        dma("CS", dbg["cwq"], R_CWQ[0:1, :], waits=(t_cwq,))
    ag_toks = {}
    last = None
    for mi, (name, I, O) in enumerate(QMATS):
        tb = mm(ONESR[:, :], R_SQ[0:1, mi:mi + 1], pSM[0:128, 0:1], True, True,
                waits=(t_sq_r, pSM_free[0]), inc="eT")
        tsc = vop(lambda e: e.tensor_copy(SCL[:, 0:1], pSM[0:128, 0:1]), waits=(tb,))
        pSM_free[0] = tsc
        na = I // 8 // 128
        for hf in range(2):
            qb = qb_i[0]
            qb_i[0] = 1 - qb
            tok_l = dma("QS", QBUFS[qb][:, :].rearrange("p (a o) -> p a o", o=O),
                        qsl_d[name].rearrange("(a p) o -> p a o", p=128)
                        [:, hf * na // 2:(hf + 1) * na // 2, :],
                        waits=(qb_last[qb],))
            sop(lambda e, qb=qb: e.activation(
                QBUFS[qb][:, :], QBUFS[qb][:, :], ACT.Copy, bias=MAGIC,
                scale=SCL[:, 0:1]), waits=(tok_l, tsc))
            s2 = sop(lambda e, qb=qb: e.activation(
                QBUFS[qb][:, :], QBUFS[qb][:, :], ACT.Copy, bias=-MAGIC))
            t3 = vop(lambda e, qb=qb: e.tensor_scalar(
                QBW[:, :], QBUFS[qb][:, :], -1.0, 1.0, AX.max, AX.min),
                waits=(s2, last))
            qb_last[qb] = t3
            last = dma("WQ", wq_i[name].ap().rearrange("(a p) o -> p a o", p=128)
                       [:, hf * na // 2:(hf + 1) * na // 2, :],
                       QBW[:, :].rearrange("p (a o) -> p a o", o=O),
                       waits=(t3,), queue="scalar")
        ag_toks[name] = C.push("gpsimd", lambda e, n=name: e.collective_compute(
            "AllGather", AX.bypass, replica_groups=[list(range(8))],
            ins=[wq_i[n].ap()], outs=[wq_o[n].ap()]), waits=(scur("WQ"),), inc="cC")
    for nm, src_d in [("wout", woutT_d), ("Wq", Wq_d), ("Wk", Wk_d),
                      ("Wv", Wv_d), ("Wo", Wo_d)]:
        pc_tok[nm] = dma("PC", wbf[nm].ap(), src_d)
    scope0.close()
    C.soft_barrier()

    scopeA = contextlib.ExitStack()
    sbA = lambda n, s, d: scopeA.enter_context(nc.sbuf_tensor(n, s, d))
    L0COS = sbA("L0COSx", [128, 1024], BF16)
    L0SIN = sbA("L0SINx", [128, 1024], BF16)
    PROT0 = sbA("PROT0x", [128, 128], BF16)
    L0MASK = sbA("L0MASKx", [128, 2048], BF16)
    PMKT = sbA("PMKTx", [128, 256], BF16)
    PMV = sbA("PMVx", [16, 2048], BF16)
    QT = sbA("QTx", [128, 2 * 512], BF16)
    KT = sbA("KTx", [128, 2 * 512], BF16)
    VV = sbA("VVx", [128, 4 * 2048], BF16)
    PP = sbA("PPx", [128, 4 * 512], BF16)
    PPP = sbA("PPPx", [16, 512], BF16)
    CTX = sbA("CTXx", [128, 16 * 512], BF16)
    DNB = sbA("DNBx", [128, 512], F32)
    dma("CS", L0COS[:, :], l0cos_d)
    dma("CS", L0SIN[:, :], l0sin_d)
    tok_prot0 = dma("CS", PROT0[:, :], prot0_d)
    t_msk0 = dma("CS", L0MASK[:, :], l0mask_d)
    tok_ropetab = t_msk0
    PMKF = sbA("PMKFx", [128, 256], F32)
    PMVF = sbA("PMVFx", [16, 2048], F32)
    t_pmkf = dma("CS", PMKF[:, :], pmkt_d)
    t_pmvf = dma("CS", PMVF[:, :], pmv_d)
    t_pmk = vop(lambda e: e.tensor_copy(PMKT[:, :], PMKF[:, :]),
                waits=(t_pmkf,))
    t_pmv = vop(lambda e: e.tensor_copy(PMV[:, :], PMVF[:, :]),
                waits=(t_pmvf,))
    SC0 = float(1.0 / np.sqrt(128.0))

    def rope_apply(dst, src_raw_bf, rot_psum_bank, costab, sintab, tok_mm):
        vop(lambda e: e.tensor_mul(TMP[:, 0:512], src_raw_bf, costab),
            waits=(tok_ropetab,))
        vop(lambda e: e.tensor_mul(TMP2[:, 0:512], rot_psum_bank[0:128, 0:512],
                                   sintab), waits=(tok_mm,))
        return vop(lambda e, d=dst: e.tensor_add(d, TMP[:, 0:512], TMP2[:, 0:512]))

    for w in range(2):
        tok0 = w * 512
        wsync("vector", [("eT", C.cnt["eT"])])
        t_h = ln_modulated(0, 0, tok0, 512, t_modv)
        m2 = None
        m2s = [None] * NCH
        for cch in range(NCH):
            sl_ = (cch % 2) * 512
            hs = H[:, cch * 512:(cch + 1) * 512]
            t_sq = vop(lambda e, h=hs, sl_=sl_: e.tensor_mul(
                XB2[:, sl_:sl_ + 512], h, h),
                waits=((t_h,) if cch == 0 else ()) +
                ((m2s[cch - 2],) if cch >= 2 else ()))
            m2 = mm(ONESC[:, :], XB2[:, sl_:sl_ + 512], pSTAT[0:1, 0:512],
                    cch == 0, cch == NCH - 1,
                    waits=(t_sq, stat_free[0]) if cch == 0 else (t_sq,),
                    inc="eT")
            m2s[cch] = m2
        t_ss = vop(lambda e: e.tensor_scalar(R_SS[0:1, 0:512], pSTAT[0:1, 0:512],
                                             1.0 / D, None, AX.mult), waits=(m2,))
        stat_free[0] = t_ss
        sop(lambda e: e.activation(R_A[0:1, 0:512], R_SS[0:1, 0:512], ACT.Ln,
                                   bias=EPSC[0:1, 1:2]), waits=(t_ss,))
        t_a = sop(lambda e: e.activation(R_A[0:1, 0:512], R_A[0:1, 0:512],
                                         ACT.Exp, scale=-0.5))
        if DBG and w == 0:
            dma("CS", dbg["h"], H[:, :], waits=(t_h,))
        tA = bcast_row(R_A[0:1, 0:512], BCA, 512, [t_a])
        t_xn = None
        for cch in range(NCH):
            hs = H[:, cch * 512:(cch + 1) * 512]
            vop(lambda e, h=hs: e.tensor_mul(h, h, BCA[:, 0:512]),
                waits=(tA, scur("CS")) if cch == 0 else ())
            t_xn = vop(lambda e, h=hs, cch=cch: e.tensor_scalar(
                h, h, NORMW[:, cch:cch + 1], None, AX.mult),
                waits=(t_nw,) if cch == 0 else ())
        if DBG and w == 0:
            dma("CS", dbg["xn"], H[:, :], waits=(t_xn,))
        for sl in range(8):
            b, tok_w = wload(wbf["wqkv"].ap()[:, 4096 + sl * 256: 4096 + (sl + 1) * 256], waits=(pc_tok["wqkv"],))
            lastm = None
            for ts4 in range(4):
                bk = ts4 % 2
                for kc in range(NCH):
                    lastm = mm(H[:, kc * 512 + ts4 * 128: kc * 512 + (ts4 + 1) * 128],
                               WB[b][:, kc * 256:(kc + 1) * 256],
                               pbank[bk][0:128, 0:256], kc == 0, kc == NCH - 1,
                               waits=(tok_w, t_xn, banks[bk]) if kc == 0 else (),
                               inc="eT" if kc == NCH - 1 else None)
                ev = vop(lambda e, ts4=ts4, sl=sl, bk=bk: e.tensor_copy(
                    VV[:, ts4 * 2048 + sl * 256: ts4 * 2048 + (sl + 1) * 256],
                    pbank[bk][0:128, 0:256]), waits=(lastm,))
                banks[bk] = ev
            wb_free[b] = lastm
        tok_vv = ("eV", C.cnt["eV"])
        for hp in range(8):
            heads = (2 * hp, 2 * hp + 1)
            b_q, tok_wq = wload(wbf["wqkv"].ap()[:, hp * 256:(hp + 1) * 256], waits=(pc_tok["wqkv"],))
            b_k, tok_wk = wload(wbf["wqkv"].ap()[:, 2048 + hp * 256: 2048 + (hp + 1) * 256], waits=(pc_tok["wqkv"],))
            for which, (bw, tok_w, buf) in enumerate(
                    [(b_q, tok_wq, QT), (b_k, tok_wk, KT)]):
                for ml in range(2):
                    bk = 2 + ml
                    lastm = None
                    for kc in range(NCH):
                        lastm = mm(WB[bw][:, kc * 256 + ml * 128:
                                          kc * 256 + (ml + 1) * 128],
                                   H[:, kc * 512:(kc + 1) * 512],
                                   pbank[bk][0:128, 0:512], kc == 0, kc == NCH - 1,
                                   waits=(tok_w, t_xn, banks[bk]) if kc == 0 else (),
                                   inc="eT" if kc == NCH - 1 else None)
                    t_raw = vop(lambda e, bk=bk, ml=ml: e.tensor_copy(
                        XB[:, ml * 512:(ml + 1) * 512], pbank[bk][0:128, 0:512]),
                        waits=(lastm,))
                    banks[bk] = t_raw
                    t_rot = mm(PROT0[:, :], XB[:, ml * 512:(ml + 1) * 512],
                               pbank[bk][0:128, 0:512], True, True,
                               waits=(t_raw, tok_prot0), inc="eT")
                    tr = rope_apply(buf[:, ml * 512:(ml + 1) * 512],
                                    XB[:, ml * 512:(ml + 1) * 512],
                                    pbank[bk],
                                    L0COS[:, tok0:tok0 + 512],
                                    L0SIN[:, tok0:tok0 + 512], t_rot)
                    banks[bk] = tr
            wb_free[b_q] = ("eT", C.cnt["eT"])
            wb_free[b_k] = ("eT", C.cnt["eT"])
            tok_rope = ("eV", C.cnt["eV"])
            for hh_i, hh in enumerate(heads):
                qtile = QT[:, hh_i * 512:(hh_i + 1) * 512]
                ktile = KT[:, hh_i * 512:(hh_i + 1) * 512]
                sc_toks = []
                for kc in range(4):
                    tkk = mm(ktile[:, kc * 128:(kc + 1) * 128], qtile,
                             pbank[kc][0:128, 0:512], True, True,
                             waits=(banks[kc], tok_rope), inc="eT")
                    sc_toks.append(tkk)
                tpp = mm(PMKT[:, hh * 16:(hh + 1) * 16], qtile,
                         pSTAT2[0:16, 0:512], True, True,
                         waits=(t_pmk, stat_free[1], tok_rope), inc="eT")
                ev_toks = []
                for kc in range(4):
                    ev = vop(lambda e, kc=kc: e.tensor_tensor(
                        PP[:, kc * 512:(kc + 1) * 512], pbank[kc][0:128, 0:512],
                        L0MASK[:, kc * 512:(kc + 1) * 512], AX.add),
                        waits=(sc_toks[kc], t_msk0))
                    banks[kc] = ev
                    ev_toks.append(ev)
                evp = vop(lambda e: e.tensor_copy(PPP[0:16, 0:512],
                                                  pSTAT2[0:16, 0:512]), waits=(tpp,))
                stat_free[1] = evp
                for kc in range(4):
                    vop(lambda e, kc=kc: e.reduce_max(
                        MCOL[:, kc:kc + 1], PP[:, kc * 512:(kc + 1) * 512], XL),
                        waits=(ev_toks[kc],))
                vop(lambda e: e.reduce_max(MCOL[0:16, 4:5], PPP[0:16, 0:512], XL),
                    waits=(evp,))
                vop(lambda e: e.tensor_max(MCOL[:, 5:6], MCOL[:, 0:1], MCOL[:, 1:2]))
                vop(lambda e: e.tensor_max(MCOL[:, 6:7], MCOL[:, 2:3], MCOL[:, 3:4]))
                vop(lambda e: e.tensor_max(MCOL[:, 7:8], MCOL[:, 5:6], MCOL[:, 6:7]))
                t_mx = vop(lambda e: e.tensor_max(MCOL[0:16, 7:8], MCOL[0:16, 7:8],
                                                  MCOL[0:16, 4:5]))
                t_nb = negbias_from_col(MCOL[:, 7:8], SC0, 8, [t_mx])
                ex_toks = []
                for kc in range(4):
                    ex = sop(lambda e, kc=kc: e.activation(
                        PP[:, kc * 512:(kc + 1) * 512],
                        PP[:, kc * 512:(kc + 1) * 512],
                        ACT.Exp, bias=MCOL[:, 8:9], scale=SC0), waits=(t_nb,))
                    ex_toks.append(ex)
                exp_p = sop(lambda e: e.activation(
                    PPP[0:16, 0:512], PPP[0:16, 0:512], ACT.Exp,
                    bias=MCOL[0:16, 8:9], scale=SC0), waits=(t_nb,))
                dn = None
                for kc in range(4):
                    dn = mm(ONESC[:, :], PP[:, kc * 512:(kc + 1) * 512],
                            pSTAT[0:1, 0:512], kc == 0, False,
                            waits=(ex_toks[kc], stat_free[0] if kc == 0 else None))
                dn = mm(ONESC[0:16, :], PPP[0:16, 0:512], pSTAT[0:1, 0:512],
                        False, True, waits=(exp_p,), inc="eT")
                t_rc = vop(lambda e: e.reciprocal(R_T[0:1, 0:512],
                                                  pSTAT[0:1, 0:512]), waits=(dn,))
                stat_free[0] = t_rc
                t_dnb = bcast_row(R_T[0:1, 0:512], DNB, 512, [t_rc])
                av = None
                for kc in range(4):
                    av = mm(VV[:, kc * 2048 + hh * 128: kc * 2048 + (hh + 1) * 128],
                            PP[:, kc * 512:(kc + 1) * 512],
                            pbank[0][0:128, 0:512], kc == 0, False,
                            waits=(ex_toks[kc], banks[0], tok_vv) if kc == 0
                            else (ex_toks[kc],))
                av = mm(PMV[0:16, hh * 128:(hh + 1) * 128], PPP[0:16, 0:512],
                        pbank[0][0:128, 0:512], False, True,
                        waits=(exp_p, t_pmv), inc="eT")
                t_ctx = vop(lambda e, hh=hh: e.tensor_mul(
                    CTX[:, hh * 512:(hh + 1) * 512], pbank[0][0:128, 0:512],
                    DNB[:, 0:512]), waits=(av, t_dnb))
                banks[0] = t_ctx
        if DBG and w == 0:
            dma("CS", dbg["ctx"], CTX[:, :], waits=(("eV", C.cnt["eV"]),))
        tok_ctx = ("eV", C.cnt["eV"])
        for sl in range(8):
            b, tok_w = wload(wbf["wout"].ap()[:, sl * 256:(sl + 1) * 256], waits=(pc_tok["wout"],))
            for ml in range(2):
                oc = sl * 2 + ml
                bk = ml
                lastm = None
                for kc in range(NCH):
                    lastm = mm(WB[b][:, kc * 256 + ml * 128:
                                     kc * 256 + (ml + 1) * 128],
                               CTX[:, kc * 512:(kc + 1) * 512],
                               pbank[bk][0:128, 0:512], kc == 0, kc == NCH - 1,
                               waits=(tok_w, banks[bk], tok_ctx) if kc == 0 else (),
                               inc="eT" if kc == NCH - 1 else None)
                ev = vop(lambda e, oc=oc, bk=bk, tok0=tok0: e.scalar_tensor_tensor(
                    X[:, oc * 1024 + tok0: oc * 1024 + tok0 + 512],
                    pbank[bk][0:128, 0:512], MODV[:, 32 + oc:33 + oc],
                    X[:, oc * 1024 + tok0: oc * 1024 + tok0 + 512],
                    AX.mult, AX.add), waits=(lastm, t_modv))
                banks[bk] = ev
            wb_free[b] = lastm
        res_tok[0] = banks[1]
    if DBG:
        dma("CS", dbg["x1"], X[:, :], waits=(res_tok[0],))
    scopeA.close()
    C.soft_barrier()

    scopeB = contextlib.ExitStack()
    sbB = lambda n, s, d: scopeB.enter_context(nc.sbuf_tensor(n, s, d))
    XQ = sbB("XQx", [128, 16 * 256], BF16)
    MID = sbB("MIDx", [128, 64 * 256], BF16)

    def quant_rows_rms(N, cw_col, waits=(), sqd=SQD):
        sop(lambda e, N=N: e.activation(R_A[0:1, 0:N], R_SS[0:1, 0:N], ACT.Ln),
            waits=tuple(waits))
        t_exp = sop(lambda e, N=N: e.activation(R_A[0:1, 0:N], R_A[0:1, 0:N],
                                                ACT.Exp, scale=-0.5))
        vop(lambda e, N=N: e.tensor_scalar(R_A[0:1, 0:N], R_A[0:1, 0:N],
                                           1e12, None, AX.min), waits=(t_exp,))
        t_sr = vop(lambda e, N=N, sqd=sqd: e.tensor_scalar(
            R_A[0:1, 0:N], R_A[0:1, 0:N], sqd, None, AX.mult))
        vop(lambda e, N=N: e.tensor_mul(R_T[0:1, 0:N], R_A[0:1, 0:N],
                                        R_MX[0:1, 0:N]))
        vop(lambda e, N=N: e.tensor_scalar(R_T[0:1, 0:N], R_T[0:1, 0:N],
                                           1e-5, None, AX.max))
        vop(lambda e, N=N: e.reciprocal(R_B[0:1, 0:N], R_T[0:1, 0:N]))
        vop(lambda e, N=N: e.tensor_scalar(R_B[0:1, 0:N], R_B[0:1, 0:N],
                                           127.0, None, AX.mult))
        t_cb = vop(lambda e, N=N: e.tensor_mul(R_A[0:1, 0:N], R_A[0:1, 0:N],
                                               R_B[0:1, 0:N]))
        t_os = vop(lambda e, N=N, cw=cw_col: e.tensor_scalar(
            R_OS[0:1, 0:N], R_T[0:1, 0:N], R_CWQ[0:1, cw:cw + 1], None, AX.mult),
            waits=(t_cwq,))
        return t_cb, t_os

    if DBG:
        dma("CS", dbg["w1q"], wq_o["w1"].ap(), waits=(ag_toks["w1"],))
        dma("CS", dbg["w2q"], wq_o["w2"].ap(), waits=(ag_toks["w2"],))
    for g in range(4):
        tok0 = g * 256
        N = 256
        wsync("vector", [("eT", C.cnt["eT"])])
        t_h = ln_modulated(0, 48, tok0, N, t_modv)
        if DBG and g == 0:
            dma("CS", dbg["hffn"], H[:, :], waits=(t_h,))
        m2 = None
        m2s = [None] * NCH
        for cch in range(NCH):
            sl_ = (cch % 2) * 512
            hs = H[:, cch * 512: cch * 512 + N]
            t_sq = vop(lambda e, h=hs, sl_=sl_: e.tensor_mul(
                XB2[:, sl_:sl_ + N], h, h),
                waits=((t_h,) if cch == 0 else ()) +
                ((m2s[cch - 2],) if cch >= 2 else ()))
            m2 = mm(ONESC[:, :], XB2[:, sl_:sl_ + N], pSTAT[0:1, 0:N],
                    cch == 0, cch == NCH - 1,
                    waits=(t_sq, stat_free[0]) if cch == 0 else (t_sq,),
                    inc="eT")
            m2s[cch] = m2
        t_ssr = vop(lambda e: e.tensor_copy(R_SS[0:1, 0:N], pSTAT[0:1, 0:N]),
                    waits=(m2,))
        stat_free[0] = t_ssr
        t_amx = absmax_to_row([H[:, cch * 512: cch * 512 + N]
                               for cch in range(NCH)], N, [t_h])
        t_cb, t_os = quant_rows_rms(N, 0, waits=(t_ssr, t_amx))
        tA = bcast_row(R_A[0:1, 0:N], BCA, N, [t_cb])
        tq = None
        for cch in range(NCH):
            hs = H[:, cch * 512: cch * 512 + N]
            vop(lambda e, h=hs: e.tensor_mul(TMP[:, 0:N], h, BCA[:, 0:N]),
                waits=(tA,) if cch == 0 else ())
            tq = vop(lambda e, cch=cch: e.tensor_scalar(
                XQ[:, cch * 256: cch * 256 + N], TMP[:, 0:N], MAGIC, -MAGIC,
                AX.add, AX.add))
        tB = bcast_row(R_OS[0:1, 0:N], BCB, N, [t_os])
        if DBG and g == 0:
            dma("CS", dbg["q1ss"], R_SS[0:1, 0:256], waits=(t_ssr,))
            dma("CS", dbg["q1mx"], R_MX[0:1, 0:256], waits=(t_amx,))
            dma("CS", dbg["q1cb"], R_A[0:1, 0:256], waits=(t_cb,))
            dma("CS", dbg["q1os"], R_OS[0:1, 0:256], waits=(t_os,))
            dma("CS", dbg["xq"], XQ[:, :], waits=(tq,))
            wsync("vector", [scur("CS")])
            wsync("scalar", [scur("CS")])
        for sl in range(32):
            b, tok_w = wload(wq_o["w1"].ap()[:, sl * 256:(sl + 1) * 256],
                             waits=(ag_toks["w1"],), queue="sync")
            for ml in range(2):
                oc = sl * 2 + ml
                bk = ml
                lastm = None
                for kc in range(NCH):
                    lastm = mm(WB[b][:, kc * 256 + ml * 128:
                                     kc * 256 + (ml + 1) * 128],
                               XQ[:, kc * 256:(kc + 1) * 256],
                               pbank[bk][0:128, 0:N], kc == 0, kc == NCH - 1,
                               waits=(tok_w, tq, banks[bk]) if kc == 0 else (),
                               inc="eT" if kc == NCH - 1 else None)
                t1 = vop(lambda e, bk=bk: e.tensor_mul(
                    TMP2[:, 0:N], pbank[bk][0:128, 0:N], BCB[:, 0:N]),
                    waits=(lastm, tB))
                banks[bk] = t1
                sop(lambda e: e.activation(TMP2[:, 0:N], TMP2[:, 0:N], ACT.Gelu),
                    waits=(t1,))
                g2 = sop(lambda e, oc=oc: e.activation(
                    MID[:, oc * 256: oc * 256 + N], TMP2[:, 0:N], ACT.Gelu))
                vop(lambda e: e.tensor_copy(TMP3[:, 3:4], TMP3[:, 3:4]),
                    waits=(g2,))
            wb_free[b] = lastm
        tok_gelu = ("eS", C.cnt["eS"])
        if DBG and g == 0:
            dma("CS", dbg["mid"], MID[:, :], waits=(tok_gelu,))
            wsync("vector", [scur("CS")])
        m2 = None
        m2s = [None] * 64
        for cch in range(64):
            sl_ = (cch % 2) * 512
            ms = MID[:, cch * 256: cch * 256 + N]
            t_sq = vop(lambda e, h=ms, sl_=sl_: e.tensor_mul(
                XB2[:, sl_:sl_ + N], h, h),
                waits=((tok_gelu,) if cch == 0 else ()) +
                ((m2s[cch - 2],) if cch >= 2 else ()))
            m2 = mm(ONESC[:, :], XB2[:, sl_:sl_ + N], pSTAT[0:1, 0:N],
                    cch == 0, cch == 63,
                    waits=(t_sq, stat_free[0]) if cch == 0 else (t_sq,),
                    inc="eT")
            m2s[cch] = m2
        t_ssr = vop(lambda e: e.tensor_copy(R_SS[0:1, 0:N], pSTAT[0:1, 0:N]),
                    waits=(m2,))
        stat_free[0] = t_ssr
        t_amx2 = absmax_to_row([MID[:, cch * 256: cch * 256 + N]
                                for cch in range(64)], N, [tok_gelu])
        t_cb, t_os = quant_rows_rms(N, 1, waits=(t_ssr, t_amx2),
                                    sqd=float(np.sqrt(8192.0)))
        tA = bcast_row(R_A[0:1, 0:N], BCA, N, [t_cb])
        tq = None
        for cch in range(64):
            ms = MID[:, cch * 256: cch * 256 + N]
            vop(lambda e, h=ms: e.tensor_mul(TMP[:, 0:N], h, BCA[:, 0:N]),
                waits=(tA,) if cch == 0 else ())
            tq = vop(lambda e, h=ms: e.tensor_scalar(
                h, TMP[:, 0:N], MAGIC, -MAGIC, AX.add, AX.add))
        tB = bcast_row(R_OS[0:1, 0:N], BCB, N, [t_os])
        if DBG and g == 0:
            dma("CS", dbg["q2cb"], R_A[0:1, 0:256], waits=(t_cb,))
            dma("CS", dbg["q2os"], R_OS[0:1, 0:256], waits=(t_os,))
            dma("CS", dbg["mint"], MID[:, :], waits=(tq,))
            wsync("vector", [scur("CS")])
            wsync("scalar", [scur("CS")])
        for sl in range(8):
            lastml = [None, None]
            for jcg in range(4):
                b, tok_w = wload(wq_o["w2"].ap()
                                 [jcg * 2048:(jcg + 1) * 2048,
                                  sl * 256:(sl + 1) * 256],
                                 waits=(ag_toks["w2"],), queue="sync")
                for ml in range(2):
                    bk = ml
                    for kc in range(NCH):
                        jc = jcg * 16 + kc
                        lastml[ml] = mm(WB[b][:, kc * 256 + ml * 128:
                                              kc * 256 + (ml + 1) * 128],
                                        MID[:, jc * 256:(jc + 1) * 256],
                                        pbank[bk][0:128, 0:N],
                                        jcg == 0 and kc == 0,
                                        jcg == 3 and kc == NCH - 1,
                                        waits=(tok_w, tq, banks[bk])
                                        if (jcg == 0 and kc == 0) else (tok_w,)
                                        if kc == 0 else (),
                                        inc="eT" if kc == NCH - 1 else None)
                wb_free[b] = lastml[1]
            for ml in range(2):
                oc = sl * 2 + ml
                bk = ml
                t1 = vop(lambda e, bk=bk: e.tensor_mul(
                    TMP2[:, 0:N], pbank[bk][0:128, 0:N], BCB[:, 0:N]),
                    waits=(lastml[ml], tB))
                ev = vop(lambda e, oc=oc, tok0=tok0: e.scalar_tensor_tensor(
                    X[:, oc * 1024 + tok0: oc * 1024 + tok0 + N],
                    TMP2[:, 0:N], MODV[:, 80 + oc:81 + oc],
                    X[:, oc * 1024 + tok0: oc * 1024 + tok0 + N],
                    AX.mult, AX.add), waits=(t_modv,))
                banks[bk] = ev
        res_tok[0] = banks[1]
    if DBG:
        dma("CS", dbg["x2"], X[:, :], waits=(res_tok[0],))
    scopeB.close()
    C.soft_barrier()

    for qt_own in range(2):
        wsync("vector", [("eT", C.cnt["eT"]), scur("HB")])
        t_h = ln_modulated(1, 0, qt_own * 512, 512, t_modv)
        dma("HB",
            hb_i.ap().rearrange("(c p) t -> p c t", p=128)
            [:, :, qt_own * 512:(qt_own + 1) * 512],
            H[:, :].rearrange("p (c t) -> p c t", t=512), waits=(t_h,))
    t_hag = C.push("gpsimd", lambda e: e.collective_compute(
        "AllGather", AX.bypass, replica_groups=[[0, 1], [2, 3], [4, 5], [6, 7]],
        ins=[hb_i.ap()], outs=[hb_o.ap()]),
        waits=(scur("HB"),), inc="cC")
    C.soft_barrier()

    scopeC = contextlib.ExitStack()
    sbC = lambda n, s, d: scopeC.enter_context(nc.sbuf_tensor(n, s, d))
    L1COS = sbC("L1COSx", [128, 2048], BF16)
    L1SIN = sbC("L1SINx", [128, 2048], BF16)
    L1MASK = sbC("L1MASKx", [128, 2048], BF16)
    KT1 = sbC("KT1x", [128, 2 * 2048], BF16)
    VV1 = sbC("VV1x", [128, 16 * 260], BF16)
    QT1 = sbC("QT1x", [128, 8 * 512], BF16)
    PP1 = sbC("PP1x", [128, 16 * 512], BF16)
    C2Q = sbC("C2Qx", [128, 4 * 1024], BF16)
    dma("CS", L1COS[:, :], l1cos_d)
    tok_rt1 = dma("CS", L1SIN[:, :], l1sin_d)
    t_msk1 = dma("CS", L1MASK[:, :], l1mask_d)
    t_vones = vop(lambda e: e.memset(VV1[:, :], 1.0))
    SC1 = 0.125

    def rope512(dst, raw_bf, bank, cos_ap, sin_ap, tok_mm):
        vop(lambda e: e.tensor_mul(TMP[:, 0:512], raw_bf, cos_ap),
            waits=(tok_rt1,))
        vop(lambda e: e.tensor_mul(TMP2[:, 0:512], bank[0:128, 0:512], sin_ap),
            waits=(tok_mm,))
        return vop(lambda e, d=dst: e.tensor_add(d, TMP[:, 0:512], TMP2[:, 0:512]))

    ctx_dma_toks = []
    for qt in range(4):
        nkc = 4 * (qt + 1)
        rb = (qt // 2) * 2048
        c0c = (qt % 2) * 512
        t_hs = dma("HS", H[:, :].rearrange("p (c t) -> p c t", t=512),
                   hb_o.ap()[rb:rb + 2048, c0c:c0c + 512]
                   .rearrange("(c p) t -> p c t", p=128),
                   waits=(t_hag, ("eT", C.cnt["eT"])))
        b_k, tok_wk = wload(wbf["Wk"].ap()[:, :], waits=(pc_tok["Wk"],))
        for oc in range(2):
            bk = 2
            lastm = None
            for kc in range(NCH):
                lastm = mm(WB[b_k][:, kc * 256 + oc * 128:
                                   kc * 256 + (oc + 1) * 128],
                           H[:, kc * 512:(kc + 1) * 512],
                           pbank[bk][0:128, 0:512], kc == 0, kc == NCH - 1,
                           waits=(tok_wk, t_hs, banks[bk]) if kc == 0 else (),
                           inc="eT" if kc == NCH - 1 else None)
            t_raw = vop(lambda e, bk=bk: e.tensor_copy(XB[:, 0:512],
                                                       pbank[bk][0:128, 0:512]),
                        waits=(lastm,))
            banks[bk] = t_raw
            t_rot = mm(PROT1[:, :], XB[:, 0:512], pbank[bk][0:128, 0:512],
                       True, True, waits=(t_raw,), inc="eT")
            tr = rope512(KT1[:, oc * 2048 + qt * 512: oc * 2048 + (qt + 1) * 512],
                         XB[:, 0:512], pbank[bk],
                         L1COS[:, qt * 512:(qt + 1) * 512],
                         L1SIN[:, qt * 512:(qt + 1) * 512], t_rot)
            banks[bk] = tr
        wb_free[b_k] = lastm
        b_v, tok_wv = wload(wbf["Wv"].ap()[:, :], waits=(pc_tok["Wv"],))
        for ts4 in range(4):
            gkc = qt * 4 + ts4
            bk = 3
            lastm = None
            for kc in range(NCH):
                lastm = mm(H[:, kc * 512 + ts4 * 128: kc * 512 + (ts4 + 1) * 128],
                           WB[b_v][:, kc * 256:(kc + 1) * 256],
                           pbank[bk][0:128, 0:256], kc == 0, kc == NCH - 1,
                           waits=(tok_wv, t_hs, banks[bk]) if kc == 0 else (),
                           inc="eT" if kc == NCH - 1 else None)
            ev = None
            for lg in range(4):
                ev = vop(lambda e, gkc=gkc, lg=lg, bk=bk: e.tensor_copy(
                    VV1[:, gkc * 260 + lg * 65: gkc * 260 + lg * 65 + 64],
                    pbank[bk][0:128, lg * 64:(lg + 1) * 64]),
                    waits=(lastm, t_vones) if lg == 0 else ())
            banks[bk] = ev
        wb_free[b_v] = lastm
        for sl in range(4):
            b_q, tok_wq = wload(wbf["Wq"].ap()[:, sl * 256:(sl + 1) * 256], waits=(pc_tok["Wq"],))
            for ml in range(2):
                pc = sl * 2 + ml
                bk = 2
                lastm = None
                for kc in range(NCH):
                    lastm = mm(WB[b_q][:, kc * 256 + ml * 128:
                                       kc * 256 + (ml + 1) * 128],
                               H[:, kc * 512:(kc + 1) * 512],
                               pbank[bk][0:128, 0:512], kc == 0, kc == NCH - 1,
                               waits=(tok_wq, t_hs, banks[bk]) if kc == 0 else (),
                               inc="eT" if kc == NCH - 1 else None)
                t_raw = vop(lambda e, bk=bk: e.tensor_copy(
                    XB[:, 0:512], pbank[bk][0:128, 0:512]), waits=(lastm,))
                banks[bk] = t_raw
                t_rot = mm(PROT1[:, :], XB[:, 0:512], pbank[bk][0:128, 0:512],
                           True, True, waits=(t_raw,), inc="eT")
                tr = rope512(QT1[:, pc * 512:(pc + 1) * 512], XB[:, 0:512],
                             pbank[bk], L1COS[:, qt * 512:(qt + 1) * 512],
                             L1SIN[:, qt * 512:(qt + 1) * 512], t_rot)
                banks[bk] = tr
            wb_free[b_q] = lastm
        tok_rope1 = ("eV", C.cnt["eV"])
        tok_c2free = scur("CX")
        for pc in range(8):
            for half in range(2):
                hloc = QPERM[2 * pc + half]
                lg = hloc // 4
                kchunk = lg // 2
                krow = (lg % 2) * 64
                qv = QT1[half * 64:half * 64 + 64, pc * 512:(pc + 1) * 512]
                ev_toks = []
                for kc in range(nkc):
                    bk = kc % 2
                    tkk = mm(KT1[krow:krow + 64,
                                 kchunk * 2048 + kc * 128:
                                 kchunk * 2048 + (kc + 1) * 128],
                             qv, pbank[bk][0:128, 0:512], True, True,
                             waits=(banks[bk], tok_rope1), inc="eT")
                    j = kc - 4 * qt
                    if 0 <= j < 4:
                        ev = vop(lambda e, kc=kc, j=j, bk=bk: e.tensor_tensor(
                            PP1[:, kc * 512:(kc + 1) * 512],
                            pbank[bk][0:128, 0:512],
                            L1MASK[:, j * 512:(j + 1) * 512], AX.add),
                            waits=(tkk, t_msk1))
                    else:
                        ev = vop(lambda e, kc=kc, bk=bk: e.tensor_copy(
                            PP1[:, kc * 512:(kc + 1) * 512],
                            pbank[bk][0:128, 0:512]), waits=(tkk,))
                    banks[bk] = ev
                    ev_toks.append(ev)
                for kc in range(nkc):
                    vop(lambda e, kc=kc: e.reduce_max(
                        MCOL[:, kc:kc + 1], PP1[:, kc * 512:(kc + 1) * 512], XL),
                        waits=(ev_toks[kc],))
                t_mx = vop(lambda e, nkc=nkc: e.tensor_reduce(
                    MCOL[:, 16:17], MCOL[:, 0:nkc], XL, AX.max))
                t_nb = negbias_from_col(MCOL[:, 16:17], SC1, 17, [t_mx])
                ex_toks = []
                for kc in range(nkc):
                    ex = sop(lambda e, kc=kc: e.activation(
                        PP1[:, kc * 512:(kc + 1) * 512],
                        PP1[:, kc * 512:(kc + 1) * 512], ACT.Exp,
                        bias=MCOL[:, 17:18], scale=SC1), waits=(t_nb,))
                    ex_toks.append(ex)
                for qsub in range(4):
                    av = None
                    for kc in range(nkc):
                        av = mm(PP1[:, kc * 512 + qsub * 128:
                                    kc * 512 + (qsub + 1) * 128],
                                VV1[:, kc * 260 + lg * 65: kc * 260 + lg * 65 + 65],
                                pbank[3][0:128, 0:65], kc == 0, kc == nkc - 1,
                                waits=(ex_toks[kc], banks[3], tok_rope1) if kc == 0
                                else (ex_toks[kc],),
                                inc="eT" if kc == nkc - 1 else None)
                    t_rc = vop(lambda e: e.reciprocal(DCOL[:, 0:1],
                                                      pbank[3][0:128, 64:65]),
                               waits=(av,))
                    t_ev = vop(lambda e, qsub=qsub, hloc=hloc: e.tensor_scalar(
                        C2Q[:, qsub * 1024 + hloc * 64:
                            qsub * 1024 + hloc * 64 + 64],
                        pbank[3][0:128, 0:64], DCOL[:, 0:1], None, AX.mult),
                        waits=(tok_c2free,) if (pc == 0 and half == 0 and
                                                qsub == 0) else ())
                    banks[3] = t_ev
        tok_c2 = dma("CX",
                     ctx2b.ap()[qt * 512:(qt + 1) * 512, :]
                     .rearrange("(qc p) hd -> p qc hd", p=128),
                     C2Q[:, :].rearrange("p (qc hd) -> p qc hd", hd=1024),
                     waits=(banks[3],))
        ctx_dma_toks.append(tok_c2)
    CTXR = PP1
    tok_pp1_free = ("eT", C.cnt["eT"])
    tok_pp1_s = ("eS", C.cnt["eS"])
    for ng in range(2):
        ld = None
        for qc in range(16):
            ld = dma("CXL", CTXR[:, qc * 512:(qc + 1) * 512],
                     ctx2b.ap()[qc * 128:(qc + 1) * 128,
                                ng * 512:(ng + 1) * 512],
                     waits=(scur("CX"), tok_pp1_free, tok_pp1_s,
                            ("eT", C.cnt["eT"]))
                     if qc == 0 else ())
        for sl in range(8):
            b, tok_w = wload(wbf["Wo"].ap()[:, sl * 256:(sl + 1) * 256], waits=(pc_tok["Wo"],))
            for ml in range(2):
                oc = sl * 2 + ml
                bk = ml
                lastm = None
                for qc in range(16):
                    lastm = mm(WB[b][:, qc * 256 + ml * 128:
                                     qc * 256 + (ml + 1) * 128],
                               CTXR[:, qc * 512:(qc + 1) * 512],
                               pbank[bk][0:128, 0:512], qc == 0, qc == 15,
                               waits=(tok_w, scur("CXL"), banks[bk])
                               if qc == 0 else (),
                               inc="eT" if qc == 15 else None)
                ev = vop(lambda e, oc=oc, ng=ng, bk=bk: e.scalar_tensor_tensor(
                    X[:, oc * 1024 + ng * 512: oc * 1024 + (ng + 1) * 512],
                    pbank[bk][0:128, 0:512], MODV[:, 96 + 32 + oc:96 + 33 + oc],
                    X[:, oc * 1024 + ng * 512: oc * 1024 + (ng + 1) * 512],
                    AX.mult, AX.add), waits=(lastm, t_modv))
                banks[bk] = ev
            wb_free[b] = lastm
    res_tok[0] = banks[1]
    if DBG:
        dma("CS", dbg["x3"], X[:, :], waits=(res_tok[0],))
    scopeC.close()
    C.soft_barrier()

    scopeD = contextlib.ExitStack()
    sbD = lambda n, s, d: scopeD.enter_context(nc.sbuf_tensor(n, s, d))
    XQ1 = sbD("XQ1x", [128, 16 * 256], BF16)
    INTER = sbD("INTERx", [128, 64 * 256], BF16)
    R1 = sbD("R1x", [128, 64 * 128], BF16)

    for g in range(4):
        tok0 = g * 256
        N = 256
        wsync("vector", [("eT", C.cnt["eT"])])
        t_h = ln_modulated(1, 48, tok0, N, t_modv)
        t_amx = absmax_to_row([H[:, cch * 512: cch * 512 + N]
                               for cch in range(NCH)], N, [t_h])
        vop(lambda e: e.tensor_scalar(R_T[0:1, 0:N], R_MX[0:1, 0:N],
                                      1e-5, None, AX.max), waits=(t_amx,))
        vop(lambda e: e.reciprocal(R_A[0:1, 0:N], R_T[0:1, 0:N]))
        t_cb = vop(lambda e: e.tensor_scalar(R_A[0:1, 0:N], R_A[0:1, 0:N],
                                             128.0, None, AX.mult))
        t_osg = vop(lambda e: e.tensor_scalar(R_OS[0:1, 0:N], R_T[0:1, 0:N],
                                              R_CWQ[0:1, 2:3], None, AX.mult),
                    waits=(t_cwq,))
        t_osu = vop(lambda e: e.tensor_scalar(R_SS[0:1, 0:N], R_T[0:1, 0:N],
                                              R_CWQ[0:1, 3:4], None, AX.mult))
        tA = bcast_row(R_A[0:1, 0:N], BCA, N, [t_cb])
        tq = None
        for cch in range(NCH):
            hs = H[:, cch * 512: cch * 512 + N]
            vop(lambda e, h=hs: e.tensor_mul(TMP[:, 0:N], h, BCA[:, 0:N]),
                waits=(tA,) if cch == 0 else ())
            vop(lambda e: e.tensor_scalar(TMP[:, 0:N], TMP[:, 0:N],
                                          MAGIC, -MAGIC, AX.add, AX.add))
            tq = vop(lambda e, cch=cch: e.tensor_scalar(
                XQ1[:, cch * 256: cch * 256 + N], TMP[:, 0:N],
                127.0, None, AX.min))
        tBg = bcast_row(R_OS[0:1, 0:N], BCB, N, [t_osg])
        tBu = bcast_row(R_SS[0:1, 0:N], BCA, N, [t_osu, tq])
        for sl in range(32):
            bg, tok_g = wload(wq_o["wg"].ap()[:, sl * 256:(sl + 1) * 256],
                              waits=(ag_toks["wg"],), queue="sync")
            bu, tok_u = wload(wq_o["wu"].ap()[:, sl * 256:(sl + 1) * 256],
                              waits=(ag_toks["wu"],), queue="sync")
            for ml in range(2):
                oc = sl * 2 + ml
                lg_ = None
                lu_ = None
                for kc in range(NCH):
                    lg_ = mm(WB[bg][:, kc * 256 + ml * 128:
                                    kc * 256 + (ml + 1) * 128],
                             XQ1[:, kc * 256:(kc + 1) * 256],
                             pbank[0][0:128, 0:N], kc == 0, kc == NCH - 1,
                             waits=(tok_g, tq, banks[0]) if kc == 0 else (),
                             inc="eT" if kc == NCH - 1 else None)
                for kc in range(NCH):
                    lu_ = mm(WB[bu][:, kc * 256 + ml * 128:
                                    kc * 256 + (ml + 1) * 128],
                             XQ1[:, kc * 256:(kc + 1) * 256],
                             pbank[1][0:128, 0:N], kc == 0, kc == NCH - 1,
                             waits=(tok_u, banks[1]) if kc == 0 else (),
                             inc="eT" if kc == NCH - 1 else None)
                t1 = vop(lambda e: e.tensor_mul(TMP[:, 0:N],
                                                pbank[0][0:128, 0:N], BCB[:, 0:N]),
                         waits=(lg_, tBg))
                banks[0] = t1
                t_sil2 = sop(lambda e: e.activation(TMP[:, 0:N], TMP[:, 0:N],
                                                    ACT.Silu), waits=(t1,))
                t2 = vop(lambda e: e.tensor_mul(TMP2[:, 0:N],
                                                pbank[1][0:128, 0:N], BCA[:, 0:N]),
                         waits=(lu_, tBu))
                banks[1] = t2
                t3 = vop(lambda e, oc=oc: e.tensor_mul(
                    INTER[:, oc * 256: oc * 256 + N], TMP[:, 0:N], TMP2[:, 0:N]),
                    waits=(t_sil2,))
            wb_free[bg] = lg_
            wb_free[bu] = lu_
        tok_inter = ("eV", C.cnt["eV"])
        for sub in range(2):
            st1 = None
            for a in range(64):
                bk = a % 2
                tm = mm(H128[:, :], INTER[:, a * 256 + sub * 128:
                                          a * 256 + (sub + 1) * 128],
                        pbank[bk][0:128, 0:128], True, True,
                        waits=(banks[bk], tok_inter) if a < 2 else (banks[bk],),
                        inc="eT")
                st1 = vop(lambda e, a=a, sub=sub, bk=bk: e.tensor_copy(
                    INTER[:, a * 256 + sub * 128: a * 256 + (sub + 1) * 128],
                    pbank[bk][0:128, 0:128]), waits=(tm,))
                banks[bk] = st1
            t_dump = dma("FW",
                         shufb.ap().rearrange("p (a t) -> p a t", t=128),
                         INTER[:, :].rearrange("p (a t2) -> p a t2", t2=256)
                         [:, :, sub * 128:(sub + 1) * 128], waits=(st1,))
            g_toks = []
            for hh in range(2):
                g_toks.append(dma(
                    "FW",
                    R1[hh * 64:(hh + 1) * 64, :]
                    .rearrange("a (c t) -> a c t", c=64, t=128),
                    shufb.ap().rearrange("(c two) (a t) -> two a c t",
                                         c=64, two=2, a=64, t=128)[hh],
                    waits=(t_dump,)))
            st2 = None
            for cc in range(64):
                bk = cc % 2
                tm = mm(BD64[:, :], R1[:, cc * 128:(cc + 1) * 128],
                        pbank[bk][0:128, 0:128], True, True,
                        waits=(banks[bk],) + tuple(g_toks) if cc < 2
                        else (banks[bk],), inc="eT")
                st2 = vop(lambda e, cc=cc, sub=sub, bk=bk: e.tensor_copy(
                    INTER[:, cc * 256 + sub * 128: cc * 256 + (sub + 1) * 128],
                    pbank[bk][0:128, 0:128]),
                    waits=(tm, t_dump) if cc == 0 else (tm,))
                banks[bk] = st2
        t_amx3 = absmax_to_row([INTER[:, cch * 256: cch * 256 + N]
                                for cch in range(64)], N, [st2])
        vop(lambda e: e.tensor_scalar(R_T[0:1, 0:N], R_MX[0:1, 0:N],
                                      LAM, None, AX.mult), waits=(t_amx3,))
        vop(lambda e: e.tensor_scalar(R_T[0:1, 0:N], R_T[0:1, 0:N],
                                      1e-5, None, AX.max))
        vop(lambda e: e.reciprocal(R_A[0:1, 0:N], R_T[0:1, 0:N]))
        t_cb = vop(lambda e: e.tensor_scalar(R_A[0:1, 0:N], R_A[0:1, 0:N],
                                             128.0 * LAM, None, AX.mult))
        t_osd = vop(lambda e: e.tensor_scalar(R_OS[0:1, 0:N], R_T[0:1, 0:N],
                                              R_CWQ[0:1, 4:5], None, AX.mult),
                    waits=(t_cwq,))
        tA = bcast_row(R_A[0:1, 0:N], BCA, N, [t_cb])
        tq = None
        for cch in range(64):
            ms = INTER[:, cch * 256: cch * 256 + N]
            vop(lambda e, h=ms: e.tensor_mul(TMP[:, 0:N], h, BCA[:, 0:N]),
                waits=(tA,) if cch == 0 else ())
            vop(lambda e: e.tensor_scalar(TMP[:, 0:N], TMP[:, 0:N],
                                          MAGIC, -MAGIC, AX.add, AX.add))
            tq = vop(lambda e, h=ms: e.tensor_scalar(
                h, TMP[:, 0:N], 127.0, None, AX.min))
        tB = bcast_row(R_OS[0:1, 0:N], BCB, N, [t_osd])
        for sl in range(8):
            lastml = [None, None]
            for jcg in range(4):
                b, tok_w = wload(wq_o["wd"].ap()
                                 [jcg * 2048:(jcg + 1) * 2048,
                                  sl * 256:(sl + 1) * 256],
                                 waits=(ag_toks["wd"],), queue="sync")
                for ml in range(2):
                    bk = ml
                    for kc in range(NCH):
                        jc = jcg * 16 + kc
                        lastml[ml] = mm(WB[b][:, kc * 256 + ml * 128:
                                              kc * 256 + (ml + 1) * 128],
                                        INTER[:, jc * 256:(jc + 1) * 256],
                                        pbank[bk][0:128, 0:N],
                                        jcg == 0 and kc == 0,
                                        jcg == 3 and kc == NCH - 1,
                                        waits=(tok_w, tq, banks[bk])
                                        if (jcg == 0 and kc == 0) else (tok_w,)
                                        if kc == 0 else (),
                                        inc="eT" if kc == NCH - 1 else None)
                wb_free[b] = lastml[1]
            for ml in range(2):
                oc = sl * 2 + ml
                bk = ml
                vop(lambda e, bk=bk: e.tensor_mul(
                    TMP2[:, 0:N], pbank[bk][0:128, 0:N], BCB[:, 0:N]),
                    waits=(lastml[ml], tB))
                ev = vop(lambda e, oc=oc, tok0=tok0: e.scalar_tensor_tensor(
                    X[:, oc * 1024 + tok0: oc * 1024 + tok0 + N],
                    TMP2[:, 0:N], MODV[:, 96 + 80 + oc:96 + 81 + oc],
                    X[:, oc * 1024 + tok0: oc * 1024 + tok0 + N],
                    AX.mult, AX.add), waits=(t_modv,))
                banks[bk] = ev
        res_tok[0] = banks[1]
    scopeD.close()
    C.soft_barrier()

    scopeE = contextlib.ExitStack()
    OUTS = scopeE.enter_context(nc.sbuf_tensor("OUTSx", [128, 2048], F32))
    out_toks = []
    prev_store = None
    for tsub in range(8):
        cps = []
        for cch in range(NCH):
            bk = cch % 4
            tt = tpose(X[:, cch * 1024 + tsub * 128: cch * 1024 + (tsub + 1) * 128],
                       IDF[:, :], pbank[bk][0:128, 0:128],
                       waits=(res_tok[0], banks[bk], prev_store))
            cp = vop(lambda e, cch=cch, bk=bk: e.tensor_copy(
                OUTS[:, cch * 128:(cch + 1) * 128], pbank[bk][0:128, 0:128]),
                waits=(tt,))
            banks[bk] = cp
            cps.append(cp)
        tok_o = dma("OS", out_d[tsub * 128:(tsub + 1) * 128, :], OUTS[:, :],
                    waits=(cps[-1],))
        prev_store = tok_o
        out_toks.append(tok_o)
    semh = C.semh
    fin = ("OS", C.cnt["OS"])
    C.q["sync"].append(lambda e, fin=fin: e.wait_ge(semh[fin[0]], fin[1]))
    scopeE.close()

    with nc.Block() as block:
        @block.tensor
        def _(e):
            for fn in C.q["tensor"]:
                fn(e)

        @block.vector
        def _(e):
            for fn in C.q["vector"]:
                fn(e)

        @block.scalar
        def _(e):
            for fn in C.q["scalar"]:
                fn(e)

        @block.gpsimd
        def _(e):
            for fn in C.q["gpsimd"]:
                fn(e)

        @block.sync
        def _(e):
            for fn in C.q["sync"]:
                fn(e)
    st.close()
    return nc


_NC_CACHE = [None]


def kernel(**inputs):
    in_maps = _host_prep(inputs)
    if _NC_CACHE[0] is None:
        _NC_CACHE[0] = build_program()
    nc = _NC_CACHE[0]
    res = run_bass_kernel_spmd(nc, in_maps, core_ids=list(range(8)))
    if os.environ.get("MK_DEBUG") == "1":
        kernel._dbg = res.results
    out = np.zeros((4, 2048, 2048), np.float32)
    for c in range(8):
        pair, par = c // 2, c % 2
        out[pair, par * 1024:(par + 1) * 1024, :] = res.results[c]["out"]
    return out


# version 09
# speedup vs baseline: 1.0588x; 1.0081x over previous
import contextlib
import os
import numpy as np
import ml_dtypes
import concourse.bass as bass
import concourse.mybir as mybir
from concourse.bass_utils import run_bass_kernel_spmd

F32 = mybir.dt.float32
BF16 = mybir.dt.bfloat16
AX = mybir.AluOpType
ACT = mybir.ActivationFunctionType
XL = mybir.AxisListType.X

BF = ml_dtypes.bfloat16
MAGIC = float(np.float32(1.5 * 2 ** 23))
NEG = -1.0e9
D = 2048
NCH = 16
LAM = float(np.float32(1.0 / np.sqrt(8192.0)))
SQD = float(np.sqrt(2048.0))
QPERM = [0, 4, 1, 5, 2, 6, 3, 7, 8, 12, 9, 13, 10, 14, 11, 15]
QMATS = [("w1", 2048, 8192), ("w2", 8192, 2048), ("wg", 2048, 8192),
         ("wu", 2048, 8192), ("wd", 8192, 2048)]
NELEM_Q = float(2048 * 8192)


def _hadamard(n):
    H = np.array([[1.0]], np.float32)
    while H.shape[0] < n:
        H = np.block([[H, H], [H, -H]])
    return H.astype(np.float32)


def _host_prep(inputs):
    f = lambda k: np.ascontiguousarray(np.asarray(inputs[k], np.float32))
    x = f("L0_attn_x")
    xT = np.ascontiguousarray(x.transpose(0, 2, 1))
    w_qkvT = np.ascontiguousarray(f("L0_attn_w_qkv").T)
    w_outT = np.ascontiguousarray(f("L0_attn_w_out").T)
    norm_w = f("L0_attn_norm_w").reshape(1, 2048)
    pm = f("L0_attn_pm")
    w1T = np.ascontiguousarray(f("L0_ffn_w1").T)
    w2T = np.ascontiguousarray(f("L0_ffn_w2").T)
    gT = np.ascontiguousarray(f("L1_ffn_w_gate").T)
    uT = np.ascontiguousarray(f("L1_ffn_w_up").T)
    dT = np.ascontiguousarray(f("L1_ffn_w_down").T)
    cidx = np.arange(64)[:, None, None]
    hidx = np.arange(2)[None, :, None]
    aidx = np.arange(64)[None, None, :]
    jmap = (aidx * 128 + 2 * cidx + hidx).reshape(-1)
    dTP = np.ascontiguousarray(dT[jmap])
    wmod0 = f("L0_wmod"); wmod1 = f("L1_wmod")
    c0 = f("L0_c"); c1 = f("L1_c")
    Wq = f("L1_attn_Wq"); Wk = f("L1_attn_Wk")
    Wv = f("L1_attn_Wv"); Wo = f("L1_attn_Wo")

    idb = np.eye(128, dtype=np.float32).astype(BF)
    idf = np.eye(128, dtype=np.float32)
    onescol = np.ones((128, 1), np.float32).astype(BF)
    onesrow = np.ones((1, 128), np.float32)
    one11 = np.ones((1, 1), np.float32)
    inv0 = 1.0 / (10000.0 ** (np.arange(0, 128, 2, dtype=np.float32) / 128.0))
    fr = np.arange(2048, dtype=np.float32)[:, None] * inv0[None, :]
    fr = np.repeat(fr, 2, axis=-1)
    l0cos_all = np.cos(fr).T.astype(BF)
    l0sin_all = np.sin(fr).T.astype(BF)
    P0 = np.zeros((128, 128), np.float32)
    for i in range(64):
        P0[2 * i, 2 * i + 1] = -1.0
        P0[2 * i + 1, 2 * i] = 1.0
    prot0 = np.ascontiguousarray(P0.T).astype(BF)
    inv1 = 1.0 / (10000.0 ** (np.arange(32, dtype=np.float32) * 2.0 / 64.0))
    ang = np.arange(2048, dtype=np.float32)[:, None] * inv1[None, :]
    c1t = np.cos(ang).T; s1t = np.sin(ang).T
    l1cos = np.zeros((128, 2048), np.float32)
    l1sin = np.zeros((128, 2048), np.float32)
    for hh in range(2):
        for half in range(2):
            rows = slice(hh * 64 + half * 32, hh * 64 + half * 32 + 32)
            l1cos[rows] = c1t
            l1sin[rows] = s1t
    l1cos = l1cos.astype(BF); l1sin = l1sin.astype(BF)
    P1 = np.zeros((128, 128), np.float32)
    for hh in range(2):
        b0 = hh * 64
        for i in range(32):
            P1[b0 + i, b0 + 32 + i] = -1.0
            P1[b0 + 32 + i, b0 + i] = 1.0
    prot1 = np.ascontiguousarray(P1.T).astype(BF)
    mb = np.full((512, 528), NEG, np.float32)
    mb[:, :32] = 0.0
    qi = np.arange(496)[:, None]
    kj = np.arange(496)[None, :]
    mb[16:, 32:] = np.where(kj <= qi, np.float32(0), np.float32(NEG))
    mbT = mb.T
    l0mask = np.ascontiguousarray(np.concatenate(
        [mbT[16 + kc * 128:16 + (kc + 1) * 128, :] for kc in range(4)],
        axis=1)).astype(BF)
    kk = np.arange(128)[:, None]
    qq = np.arange(512)[None, :]
    l1mask = np.concatenate(
        [np.where(kk <= qq - 128 * j, np.float32(0), np.float32(NEG))
         for j in range(4)], axis=1).astype(BF)
    h128 = _hadamard(128).astype(BF)
    h64 = _hadamard(64)
    bd = np.zeros((128, 128), np.float32)
    bd[:64, :64] = h64
    bd[64:, 64:] = h64
    bd64 = bd.astype(BF)
    pmkt_sb = np.zeros((128, 256), np.float32)
    for h in range(16):
        pmkt_sb[:, h * 16:(h + 1) * 16] = pm[0][h].T
    pmv_sb = np.zeros((16, 2048), np.float32)
    for h in range(16):
        pmv_sb[:, h * 128:(h + 1) * 128] = pm[1][h]

    shared = dict(wqkvT=w_qkvT, woutT=w_outT, normw=norm_w, Wo=Wo,
                  idb=idb, idf=idf, onescol=onescol, onesrow=onesrow,
                  one11=one11, prot0=prot0, prot1=prot1, l1cos=l1cos,
                  l1sin=l1sin, l0mask=l0mask, l1mask=l1mask, h128=h128,
                  bd64=bd64, pmkt=pmkt_sb, pmv=pmv_sb)
    in_maps = []
    for c in range(8):
        pair, par = c // 2, c % 2
        m = dict(shared)
        m["xT"] = np.ascontiguousarray(xT[pair][:, par * 1024:(par + 1) * 1024])
        m["l0cos"] = np.ascontiguousarray(l0cos_all[:, par * 1024:(par + 1) * 1024])
        m["l0sin"] = np.ascontiguousarray(l0sin_all[:, par * 1024:(par + 1) * 1024])
        heads = [par * 16 + j for j in QPERM]
        m["Wq_own"] = np.ascontiguousarray(
            np.concatenate([Wq[:, h * 64:(h + 1) * 64] for h in heads], axis=1))
        m["Wk_own"] = np.ascontiguousarray(Wk[:, par * 256:(par + 1) * 256])
        m["Wv_own"] = np.ascontiguousarray(Wv[:, par * 256:(par + 1) * 256])
        for name, I, O in QMATS:
            src = {"w1": w1T, "w2": w2T, "wg": gT, "wu": uT, "wd": dTP}[name]
            rs = I // 8
            m[name + "_sl"] = np.ascontiguousarray(src[c * rs:(c + 1) * rs])
        m["wmod0_sl"] = np.ascontiguousarray(wmod0[c * 256:(c + 1) * 256])
        m["wmod1_sl"] = np.ascontiguousarray(wmod1[c * 256:(c + 1) * 256])
        oh = np.zeros((128, 768), np.float32)
        oh[:, pair::4] = 1.0
        m["ohmask"] = oh
        m["c0s"] = np.ascontiguousarray(c0[:, c * 256:(c + 1) * 256])
        m["c1s"] = np.ascontiguousarray(c1[:, c * 256:(c + 1) * 256])
        in_maps.append(m)
    return in_maps


class Ctx:
    def __init__(self):
        self.q = {e: [] for e in ("tensor", "vector", "scalar", "gpsimd", "sync")}
        self.semh = {}
        self.cnt = {}

    def push(self, eng, fn, waits=(), inc=None, incn=1):
        waits = tuple(w for w in waits if w is not None)
        if eng == "vector":
            if self.cnt.get("eV", 0) > 0:
                waits = waits + (("eV", self.cnt["eV"]),)
            if inc is None:
                inc = "eV"
        elif eng == "scalar":
            if self.cnt.get("eS", 0) > 0:
                waits = waits + (("eS", self.cnt["eS"]),)
            if inc is None:
                inc = "eS"
        tok = None
        if inc is not None:
            self.cnt[inc] += incn
            tok = (inc, self.cnt[inc])
        semh = self.semh

        def thunk(e, fn=fn, waits=waits, inc=inc, incn=incn):
            for (s, v) in waits:
                e.wait_ge(semh[s], v)
            inst = fn(e)
            if inc is not None:
                inst.then_inc(semh[inc], incn)
        self.q[eng].append(thunk)
        return tok

    def barrier(self, exclude=()):
        snap = [(s, v) for s, v in self.cnt.items() if v > 0 and s not in exclude]
        semh = self.semh
        for eng in self.q:
            def thunk(e, snap=tuple(snap)):
                for (s, v) in snap:
                    e.wait_ge(semh[s], v)
            self.q[eng].append(thunk)

    def soft_barrier(self):
        self.barrier(exclude=("cC", "PC"))


def build_program():
    nc = bass.Bass("TRN2", target_bir_lowering=False)
    C = Ctx()

    def din(name, shape, dt=F32):
        return nc.dram_tensor(name, shape, dt, kind="ExternalInput").ap()

    xT_d = din("xT", [2048, 1024])
    wqkvT_d = din("wqkvT", [2048, 6144])
    woutT_d = din("woutT", [2048, 2048])
    normw_d = din("normw", [1, 2048])
    pmkt_d = din("pmkt", [128, 256])
    pmv_d = din("pmv", [16, 2048])
    Wo_d = din("Wo", [2048, 2048])
    Wq_d = din("Wq_own", [2048, 1024])
    Wk_d = din("Wk_own", [2048, 256])
    Wv_d = din("Wv_own", [2048, 256])
    idb_d = din("idb", [128, 128], BF16)
    idf_d = din("idf", [128, 128])
    onescol_d = din("onescol", [128, 1], BF16)
    onesrow_d = din("onesrow", [1, 128])
    one11_d = din("one11", [1, 1])
    prot0_d = din("prot0", [128, 128], BF16)
    prot1_d = din("prot1", [128, 128], BF16)
    l0cos_d = din("l0cos", [128, 1024], BF16)
    l0sin_d = din("l0sin", [128, 1024], BF16)
    l1cos_d = din("l1cos", [128, 2048], BF16)
    l1sin_d = din("l1sin", [128, 2048], BF16)
    l0mask_d = din("l0mask", [128, 2048], BF16)
    l1mask_d = din("l1mask", [128, 2048], BF16)
    h128_d = din("h128", [128, 128], BF16)
    bd64_d = din("bd64", [128, 128], BF16)
    wmod_sl_d = {0: din("wmod0_sl", [256, 12288]), 1: din("wmod1_sl", [256, 12288])}
    oh_d = din("ohmask", [128, 768])
    c0s_d = din("c0s", [4, 256])
    c1s_d = din("c1s", [4, 256])
    qsl_d = {name: din(name + "_sl", [I // 8, O]) for name, I, O in QMATS}
    out_d = nc.dram_tensor("out", [1024, 2048], F32, kind="ExternalOutput").ap()
    DBG = os.environ.get("MK_DEBUG") == "1"
    dbg = {}
    if DBG:
        dbg["modv"] = nc.dram_tensor("dbg_modv", [128, 192], F32, kind="ExternalOutput").ap()
        dbg["cw"] = nc.dram_tensor("dbg_cw", [1, 8], F32, kind="ExternalOutput").ap()
        dbg["cwq"] = nc.dram_tensor("dbg_cwq", [1, 8], F32, kind="ExternalOutput").ap()
        dbg["stg"] = nc.dram_tensor("dbg_stg", [1, 8], F32, kind="ExternalOutput").ap()
        dbg["mp"] = nc.dram_tensor("dbg_mp", [128, 768], F32, kind="ExternalOutput").ap()
        dbg["ra"] = nc.dram_tensor("dbg_ra", [1, 512], F32, kind="ExternalOutput").ap()
        dbg["rb"] = nc.dram_tensor("dbg_rb", [1, 512], F32, kind="ExternalOutput").ap()
        dbg["sm"] = nc.dram_tensor("dbg_sm", [1, 512], F32, kind="ExternalOutput").ap()
        dbg["ssr"] = nc.dram_tensor("dbg_ssr", [1, 512], F32, kind="ExternalOutput").ap()
        dbg["flag"] = [0]
        dbg["h"] = nc.dram_tensor("dbg_h", [128, 8192], BF16, kind="ExternalOutput").ap()
        dbg["xn"] = nc.dram_tensor("dbg_xn", [128, 8192], BF16, kind="ExternalOutput").ap()
        dbg["x1"] = nc.dram_tensor("dbg_x1", [128, 16384], F32, kind="ExternalOutput").ap()
        dbg["x2"] = nc.dram_tensor("dbg_x2", [128, 16384], F32, kind="ExternalOutput").ap()
        dbg["x3"] = nc.dram_tensor("dbg_x3", [128, 16384], F32, kind="ExternalOutput").ap()
        dbg["ctx"] = nc.dram_tensor("dbg_ctx", [128, 8192], BF16, kind="ExternalOutput").ap()
        dbg["xq"] = nc.dram_tensor("dbg_xq", [128, 4096], BF16, kind="ExternalOutput").ap()
        dbg["mid"] = nc.dram_tensor("dbg_mid", [128, 16384], BF16, kind="ExternalOutput").ap()
        dbg["mint"] = nc.dram_tensor("dbg_mint", [128, 16384], BF16, kind="ExternalOutput").ap()
        dbg["q1ss"] = nc.dram_tensor("dbg_q1ss", [1, 256], F32, kind="ExternalOutput").ap()
        dbg["q1mx"] = nc.dram_tensor("dbg_q1mx", [1, 256], F32, kind="ExternalOutput").ap()
        dbg["q1cb"] = nc.dram_tensor("dbg_q1cb", [1, 256], F32, kind="ExternalOutput").ap()
        dbg["q1os"] = nc.dram_tensor("dbg_q1os", [1, 256], F32, kind="ExternalOutput").ap()
        dbg["q2cb"] = nc.dram_tensor("dbg_q2cb", [1, 256], F32, kind="ExternalOutput").ap()
        dbg["q2os"] = nc.dram_tensor("dbg_q2os", [1, 256], F32, kind="ExternalOutput").ap()
        dbg["hffn"] = nc.dram_tensor("dbg_hffn", [128, 8192], BF16, kind="ExternalOutput").ap()
        dbg["w1q"] = nc.dram_tensor("dbg_w1q", [2048, 8192], BF16, kind="ExternalOutput").ap()
        dbg["w2q"] = nc.dram_tensor("dbg_w2q", [8192, 2048], BF16, kind="ExternalOutput").ap()

    modb_i = nc.dram_tensor("modb_i", [128, 768], F32)
    modb_o = nc.dram_tensor("modb_o", [128, 768], F32)
    absb_i = nc.dram_tensor("absb_i", [1, 8], F32)
    absb_o = nc.dram_tensor("absb_o", [1, 8], F32)
    wq_i = {name: nc.dram_tensor("wqi_" + name, [I // 8, O], BF16)
            for name, I, O in QMATS}
    wq_o = {name: nc.dram_tensor("wqo_" + name, [I, O], BF16,
                                 addr_space="Shared")
            for name, I, O in QMATS}
    wbf = {nm: nc.dram_tensor("wbf_" + nm, shp, BF16)
           for nm, shp in [("wqkv", [2048, 6144]), ("wout", [2048, 2048]),
                           ("Wq", [2048, 1024]), ("Wk", [2048, 256]),
                           ("Wv", [2048, 256]), ("Wo", [2048, 2048])]}
    hb_i = nc.dram_tensor("hb_i", [2048, 1024], BF16)
    hb_o = nc.dram_tensor("hb_o", [4096, 1024], BF16)
    shufb = nc.dram_tensor("shufb", [128, 8192], BF16)
    ctx2b = nc.dram_tensor("ctx2b", [2048, 1024], BF16)

    st = contextlib.ExitStack()
    sb = lambda n, s, d: st.enter_context(nc.sbuf_tensor(n, s, d))
    psa = lambda n, s, d: st.enter_context(nc.psum_tensor(n, s, d))

    X = sb("X", [128, NCH * 1024], F32)
    H = sb("H", [128, NCH * 512], BF16)
    WB = [sb("WBa", [128, 4096], BF16), sb("WBb", [128, 4096], BF16)]
    TMP = sb("TMPa", [128, 512], F32)
    TMP2 = sb("TMPb", [128, 512], F32)
    TMP3 = sb("TMPc", [128, 4], F32)
    BCA = sb("BCAx", [128, 512], F32)
    BCB = sb("BCBx", [128, 512], F32)
    XB = sb("XBx", [128, 1024], BF16)
    XB2 = sb("XB2x", [128, 1024], BF16)
    IDB = sb("IDBx", [128, 128], BF16)
    IDF = sb("IDFx", [128, 128], F32)
    ONESC = sb("ONESCx", [128, 1], BF16)
    ONESR = sb("ONESRx", [1, 128], F32)
    ONE11 = sb("ONE11x", [1, 1], F32)
    PROT1 = sb("PROT1x", [128, 128], BF16)
    H128 = sb("H128x", [128, 128], BF16)
    BD64 = sb("BD64x", [128, 128], BF16)
    NORMW = sb("NORMWx", [128, 16], F32)
    MODV = sb("MODVx", [128, 192], F32)
    R_SM = sb("R_SMx", [1, 512], F32)
    R_SS = sb("R_SSx", [1, 512], F32)
    R_A = sb("R_Ax", [1, 512], F32)
    R_B = sb("R_Bx", [1, 512], F32)
    R_MX = sb("R_MXx", [1, 512], F32)
    R_T = sb("R_Tx", [1, 512], F32)
    R_OS = sb("R_OSx", [1, 512], F32)
    R_CW = sb("R_CWx", [1, 8], F32)
    R_SQ = sb("R_SQx", [1, 8], F32)
    R_CWQ = sb("R_CWQx", [1, 8], F32)
    MCOL = sb("MCOLx", [128, 20], F32)
    SCL = sb("SCLx", [128, 2], F32)
    DCOL = sb("DCOLx", [128, 1], F32)
    EPSC = sb("EPSCx", [1, 2], F32)

    pbank = [psa("pb%d" % i, [128, 512], F32) for i in range(4)]
    pSTAT = psa("pSTATx", [128, 512], F32)
    pSTAT2 = psa("pSTAT2x", [128, 512], F32)
    pBF = psa("pBFx", [128, 1024], BF16)
    pSM = psa("pSMx", [128, 512], F32)

    DSTREAMS = ("CS", "WB0", "WB1", "QS", "WQ", "MB", "HS", "HB", "CX",
                "CXL", "OS", "FW", "XS", "PC")
    for s in ("cC", "eT", "eV", "eS") + DSTREAMS:
        C.semh[s] = st.enter_context(nc.semaphore(s))
        C.cnt[s] = 0

    QMAP = {"CS": "sync", "WB0": "sync", "WB1": "scalar", "QS": "sync",
            "WQ": "sync", "MB": "sync", "HS": "sync", "HB": "sync",
            "CX": "sync", "CXL": "sync", "OS": "sync", "FW": "gpsimd",
            "XS": "sync", "PC": "gpsimd"}

    SERIAL = {"CS", "MB", "HB", "CX", "OS", "WQ", "QS", "HS", "XS"}

    def dma(stream, out, in_, waits=(), queue=None):
        q = queue or QMAP[stream]
        w = tuple(waits)
        if stream in SERIAL and C.cnt[stream] > 0:
            w = w + ((stream, C.cnt[stream]),)
        return C.push(q, lambda e, o=out, i=in_: e.dma_start(out=o, in_=i),
                      waits=w, inc=stream, incn=16)

    def scur(stream):
        return (stream, C.cnt[stream])

    def mm(lhsT, rhs, out, start, stop, waits=(), inc=None):
        return C.push("tensor",
                      lambda e, l=lhsT, r=rhs, o=out, s1=start, s2=stop:
                      e.matmul(o, l, r, start=s1, stop=s2),
                      waits=waits, inc=inc)

    def tpose(in_, ident, out, waits=(), inc="eT"):
        return C.push("tensor", lambda e, i=in_, d=ident, o=out:
                      e.transpose(o, i, d), waits=waits, inc=inc)

    def vop(fn, waits=(), inc="eV"):
        return C.push("vector", fn, waits=waits, inc=inc)

    def sop(fn, waits=(), inc="eS"):
        return C.push("scalar", fn, waits=waits, inc=inc)

    def wsync(eng, waits):
        waits = [w for w in waits if w is not None]
        semh = C.semh

        def thunk(e, waits=tuple(waits)):
            for (sname, v) in waits:
                e.wait_ge(semh[sname], v)
        C.q[eng].append(thunk)

    banks = [None, None, None, None]
    pSM_free = [None]
    pBF_free = [None]
    stat_free = [None, None]
    res_tok = [None]
    wb_free = [None, None]
    wb_i = [0]

    def bcast_row(row_ap, out_sb, n, waits):
        t1 = mm(ONESR[:, :], row_ap, pSM[0:128, 0:n], True, True,
                waits=tuple(waits) + (pSM_free[0],), inc="eT")
        t2 = vop(lambda e, o=out_sb, n=n: e.tensor_copy(o[:, 0:n], pSM[0:128, 0:n]),
                 waits=(t1,))
        pSM_free[0] = t2
        return t2

    def negbias_from_col(col_ap, scale, outcol, waits):
        t1 = tpose(col_ap, IDF[:, :], pSM[0:1, 0:128],
                   waits=tuple(waits) + (pSM_free[0],))
        t2 = vop(lambda e: e.reduce_max(R_T[0:1, 0:1], pSM[0:1, 0:128], XL),
                 waits=(t1,))
        t3 = mm(ONESR[:, :], R_T[0:1, 0:1], pSM[0:128, 0:1], True, True,
                waits=(t2,), inc="eT")
        t4 = vop(lambda e, oc=outcol, sc=scale: e.tensor_scalar(
            MCOL[:, oc:oc + 1], pSM[0:128, 0:1], -sc, None, AX.mult), waits=(t3,))
        pSM_free[0] = t4
        return t4

    def absmax_to_row(tiles, N, waits):
        t = vop(lambda e, a=tiles[0], N=N: e.tensor_mul(TMP2[:, 0:N], a, a),
                waits=tuple(waits))
        for ap_ in tiles[1:]:
            vop(lambda e, a=ap_, N=N: e.tensor_mul(TMP[:, 0:N], a, a))
            t = vop(lambda e, N=N: e.tensor_max(TMP2[:, 0:N], TMP2[:, 0:N],
                                                TMP[:, 0:N]))
        last = None
        for j in range(N // 128):
            tt = tpose(TMP2[:, j * 128:(j + 1) * 128], IDF[:, :],
                       pSM[0:128, 0:128], waits=(t, pSM_free[0]))
            tr = vop(lambda e: e.reduce_max(TMP3[:, 0:1], pSM[0:128, 0:128], XL),
                     waits=(tt,))
            pSM_free[0] = tr
            tt2 = tpose(TMP3[:, 0:1], IDF[:, :], pSM[0:1, 0:128],
                        waits=(tr, pSM_free[0]))
            last = vop(lambda e, j=j: e.tensor_copy(
                R_MX[0:1, j * 128:(j + 1) * 128], pSM[0:1, 0:128]), waits=(tt2,))
            pSM_free[0] = last
        sop(lambda e, N=N: e.activation(R_MX[0:1, 0:N], R_MX[0:1, 0:N], ACT.Ln),
            waits=(last,))
        last = sop(lambda e, N=N: e.activation(R_MX[0:1, 0:N], R_MX[0:1, 0:N],
                                               ACT.Exp, scale=0.5))
        return last

    def ln_modulated(l, modofs, x_tok0, N, t_modv):
        base = l * 96 + modofs
        m1 = m2 = None
        m2s = [None] * NCH
        for cch in range(NCH):
            sl_ = (cch % 2) * 512
            xc = X[:, cch * 1024 + x_tok0: cch * 1024 + x_tok0 + N]
            t_cast = sop(lambda e, xc=xc, N=N, sl_=sl_: e.copy(
                XB[:, sl_:sl_ + N], xc),
                waits=(res_tok[0], m2s[cch - 2] if cch >= 2 else None))
            t_sq = vop(lambda e, N=N, sl_=sl_: e.tensor_mul(
                XB2[:, sl_:sl_ + N], XB[:, sl_:sl_ + N], XB[:, sl_:sl_ + N]),
                waits=(t_cast,))
            m1 = mm(ONESC[:, :], XB[:, sl_:sl_ + N], pSTAT[0:1, 0:N],
                    cch == 0, cch == NCH - 1,
                    waits=(t_cast, stat_free[0]) if cch == 0 else (t_cast,),
                    inc="eT")
            m2 = mm(ONESC[:, :], XB2[:, sl_:sl_ + N], pSTAT2[0:1, 0:N],
                    cch == 0, cch == NCH - 1,
                    waits=(t_sq, stat_free[1]) if cch == 0 else (t_sq,),
                    inc="eT")
            m2s[cch] = m2
        t_mean = vop(lambda e, N=N: e.tensor_scalar(
            R_SM[0:1, 0:N], pSTAT[0:1, 0:N], 1.0 / D, None, AX.mult), waits=(m1,))
        t_ss = vop(lambda e, N=N: e.tensor_scalar(
            R_SS[0:1, 0:N], pSTAT2[0:1, 0:N], 1.0 / D, None, AX.mult), waits=(m2,))
        stat_free[0] = t_mean
        stat_free[1] = t_ss
        vop(lambda e, N=N: e.tensor_mul(R_T[0:1, 0:N], R_SM[0:1, 0:N], R_SM[0:1, 0:N]))
        t_var = vop(lambda e, N=N: e.tensor_sub(R_SS[0:1, 0:N], R_SS[0:1, 0:N],
                                                R_T[0:1, 0:N]))
        t_ln = sop(lambda e, N=N: e.activation(R_A[0:1, 0:N], R_SS[0:1, 0:N],
                                               ACT.Ln, bias=EPSC[0:1, 0:1]),
                   waits=(t_var,))
        t_a = sop(lambda e, N=N: e.activation(R_A[0:1, 0:N], R_A[0:1, 0:N],
                                              ACT.Exp, scale=-0.5), waits=(t_ln,))
        t_bm = vop(lambda e, N=N: e.tensor_mul(R_B[0:1, 0:N], R_SM[0:1, 0:N],
                                               R_A[0:1, 0:N]), waits=(t_a,))
        t_bn = vop(lambda e, N=N: e.tensor_scalar(R_B[0:1, 0:N], R_B[0:1, 0:N],
                                                  -1.0, None, AX.mult))
        if DBG and not dbg["flag"][0]:
            dbg["flag"][0] = 1
            dma("CS", dbg["ra"], R_A[0:1, :], waits=(t_a,))
            dma("CS", dbg["rb"], R_B[0:1, :], waits=(t_bn,))
            dma("CS", dbg["sm"], R_SM[0:1, :], waits=(t_mean,))
            dma("CS", dbg["ssr"], R_SS[0:1, :], waits=(t_var,))
        tA = bcast_row(R_A[0:1, 0:N], BCA, N, [t_a])
        tB = bcast_row(R_B[0:1, 0:N], BCB, N, [t_bn])
        done = None
        for cch in range(NCH):
            xc = X[:, cch * 1024 + x_tok0: cch * 1024 + x_tok0 + N]
            hs = H[:, cch * 512: cch * 512 + N]
            vop(lambda e, xc=xc, N=N: e.tensor_mul(TMP[:, 0:N], xc, BCA[:, 0:N]),
                waits=(tA, tB) if cch == 0 else ())
            vop(lambda e, N=N: e.tensor_add(TMP[:, 0:N], TMP[:, 0:N], BCB[:, 0:N]))
            done = vop(lambda e, h=hs, cch=cch, N=N, base=base: e.tensor_scalar(
                h, TMP[:, 0:N], MODV[:, base + 16 + cch:base + 17 + cch],
                MODV[:, base + cch:base + 1 + cch], AX.mult, AX.add),
                waits=(t_modv,) if cch == 0 else ())
        return done

    def wload(src_slice_ap, waits=(), queue=None):
        b = wb_i[0]
        wb_i[0] = 1 - b
        tok = dma("WB%d" % b, WB[b][:, :].rearrange("p (kc m) -> p kc m", m=256),
                  src_slice_ap.rearrange("(kc p) m -> p kc m", p=128),
                  waits=tuple(waits) + (wb_free[b],), queue=queue)
        return b, tok

    for dst, src in [(IDB, idb_d), (IDF, idf_d), (ONESC, onescol_d),
                     (ONESR, onesrow_d), (ONE11, one11_d), (PROT1, prot1_d),
                     (H128, h128_d), (BD64, bd64_d)]:
        tok_consts = dma("CS", dst[:, :], src)
    t_x = dma("XS", X[:, :].rearrange("p (c t) -> p c t", t=1024),
              xT_d.rearrange("(c p) t -> p c t", p=128))
    res_tok[0] = t_x
    vop(lambda e: e.memset(EPSC[0:1, 0:1], 1e-5))
    vop(lambda e: e.memset(EPSC[0:1, 1:2], 1e-6))

    scope0 = contextlib.ExitStack()
    sb0 = lambda n, s, d: scope0.enter_context(nc.sbuf_tensor(n, s, d))
    NWROW = sb0("NWROWx", [1, 2048], F32)
    CSL = sb0("CSLx", [4, 512], F32)
    CSLB = sb0("CSLBx", [4, 512], BF16)
    SILC = sb0("SILCx", [128, 16], BF16)
    MP = sb0("MPx", [128, 768], F32)
    OH = sb0("OHx", [128, 768], F32)
    QSL = sb0("QSLx", [128, 8192], F32)
    QSL2 = sb0("QSL2x", [128, 8192], F32)
    QBW = H
    STG = sb0("STGx", [1, 8], F32)

    t_nwr = dma("CS", NWROW[:, :], normw_d)
    nw_last = None
    for cch in range(NCH):
        nw_last = mm(NWROW[0:1, cch * 128:(cch + 1) * 128], ONE11[:, :],
                     pSM[0:128, cch:cch + 1], True, True,
                     waits=(t_nwr, tok_consts, pSM_free[0]) if cch == 0 else (),
                     inc="eT" if cch == NCH - 1 else None)
    t_nw = vop(lambda e: e.tensor_copy(NORMW[:, :], pSM[0:128, 0:16]),
               waits=(nw_last,))
    pSM_free[0] = t_nw

    t_c0 = dma("CS", CSL[:, 0:256], c0s_d)
    t_c1 = dma("CS", CSL[:, 256:512], c1s_d)
    t_sil = sop(lambda e: e.activation(CSL[:, :], CSL[:, :], ACT.Silu),
                waits=(t_c0, t_c1))
    t_silb = sop(lambda e: e.copy(CSLB[:, :], CSL[:, :]))
    tp_last = None
    for l in range(2):
        for kc in range(2):
            tp_last = tpose(CSLB[0:4, l * 256 + kc * 128: l * 256 + (kc + 1) * 128],
                            IDB[0:4, 0:4],
                            pBF[:, (l * 2 + kc) * 4:(l * 2 + kc) * 4 + 4],
                            waits=(t_silb, tok_consts, pBF_free[0]))
    t_silc = vop(lambda e: e.tensor_copy(SILC[:, :], pBF[0:128, 0:16]),
                 waits=(tp_last,))
    pBF_free[0] = t_silc

    for l in range(2):
        for sl in range(6):
            wsl = wmod_sl_d[l][:, sl * 2048:(sl + 1) * 2048]
            b = wb_i[0]
            wb_i[0] = 1 - b
            tok_w = dma("WB%d" % b,
                        WB[b][:, :].rearrange("p (kc m) -> p kc m", m=2048),
                        wsl.rearrange("(kc p) m -> p kc m", p=128),
                        waits=(wb_free[b],), queue="gpsimd")
            last_mm = None
            for ml in range(16):
                mc = sl * 16 + ml
                bk = mc % 2
                mm(WB[b][:, ml * 128: (ml + 1) * 128], SILC[0:128, l * 8: l * 8 + 4],
                   pbank[bk][0:128, 0:4], True, False,
                   waits=(tok_w, t_silc, banks[bk]))
                last_mm = mm(WB[b][:, 2048 + ml * 128: 2048 + (ml + 1) * 128],
                             SILC[0:128, l * 8 + 4: l * 8 + 8],
                             pbank[bk][0:128, 0:4], False, True, inc="eT")
                ev = vop(lambda e, mc=mc, l=l, bk=bk: e.tensor_copy(
                    MP[:, l * 384 + mc * 4: l * 384 + mc * 4 + 4],
                    pbank[bk][0:128, 0:4]), waits=(last_mm,))
                banks[bk] = ev
                if ml == 15:
                    wb_free[b] = last_mm
    t_mpd = dma("MB", modb_i.ap(), MP[:, :], waits=(banks[0], banks[1]))
    t_ar_mod = C.push("gpsimd", lambda e: e.collective_compute(
        "AllReduce", AX.add, replica_groups=[list(range(8))],
        ins=[modb_i.ap()], outs=[modb_o.ap()]), waits=(t_mpd,), inc="cC")
    pc_tok = {}
    pc_tok["wqkv"] = dma("PC", wbf["wqkv"].ap(), wqkvT_d)

    QBUFS = [QSL, QSL2]
    qb_last = [None, None]
    qb_i = [0]
    prev = None
    for mi, (name, I, O) in enumerate(QMATS):
        na = I // 8 // 128
        for hf in range(2):
            qb = qb_i[0]
            qb_i[0] = 1 - qb
            tok_l = dma("QS", QBUFS[qb][:, :].rearrange("p (a o) -> p a o", o=O),
                        qsl_d[name].rearrange("(a p) o -> p a o", p=128)
                        [:, hf * na // 2:(hf + 1) * na // 2, :],
                        waits=(qb_last[qb],))
            prev = vop(lambda e, hf=hf, qb=qb: e.reduce_sum(
                TMP3[:, hf + 1:hf + 2], QBUFS[qb][:, :], XL,
                apply_absolute_value=True), waits=(tok_l,))
            qb_last[qb] = prev
        t_add = vop(lambda e: e.tensor_add(TMP3[:, 1:2], TMP3[:, 1:2], TMP3[:, 2:3]))
        tt = tpose(TMP3[:, 1:2], IDF[:, :], pSM[0:1, 0:128],
                   waits=(t_add, pSM_free[0]))
        ts_ = vop(lambda e, mi=mi: e.reduce_sum(
            STG[0:1, mi:mi + 1], pSM[0:1, 0:128], XL), waits=(tt,))
        pSM_free[0] = ts_
        prev = ts_
    if DBG:
        dma("CS", dbg["stg"], STG[0:1, :], waits=(prev,))
    t_absd = dma("HS", absb_i.ap(), STG[0:1, :], waits=(prev,))
    t_ar_abs = C.push("gpsimd", lambda e: e.collective_compute(
        "AllReduce", AX.add, replica_groups=[list(range(8))],
        ins=[absb_i.ap()], outs=[absb_o.ap()]), waits=(t_absd,), inc="cC")

    t_oh = dma("CS", OH[:, :], oh_d)
    t_mpb = dma("MB", MP[:, :], modb_o.ap(), waits=(t_ar_mod,))
    vop(lambda e: e.tensor_mul(MP[:, :], MP[:, :], OH[:, :]), waits=(t_mpb, t_oh))
    t_sel = vop(lambda e: e.tensor_reduce(
        MODV[:, :], MP[:, :].rearrange("p (m r) -> p m r", r=4), XL, AX.add))
    if DBG:
        dma("CS", dbg["mp"], MP[:, :], waits=(t_sel,))
    t_modv = t_sel
    prev_fix = t_sel
    for l in range(2):
        base = l * 96
        for gofs in (32, 80):
            cols = slice(base + gofs, base + gofs + 16)
            vop(lambda e, c=cols: e.tensor_scalar(
                TMP[:, 0:16], MODV[:, c], 0.0, None, AX.is_ge), waits=(t_sel,))
            tb = vop(lambda e: e.tensor_scalar(
                TMP[:, 0:16], TMP[:, 0:16], 2.0, -1.0, AX.mult, AX.add))
            tc2 = sop(lambda e, c=cols: e.activation(
                TMP2[:, 0:16], MODV[:, c], ACT.Abs), waits=(t_sel, prev_fix))
            td = vop(lambda e: e.tensor_scalar(
                TMP2[:, 0:16], TMP2[:, 0:16], 0.25, None, AX.max), waits=(tc2,))
            t_modv = vop(lambda e, c=cols: e.tensor_mul(
                MODV[:, c], TMP[:, 0:16], TMP2[:, 0:16]), waits=(tb, td))
            prev_fix = t_modv
        for cofs in (16, 64):
            cols = slice(base + cofs, base + cofs + 16)
            t_modv = vop(lambda e, c=cols: e.tensor_scalar(
                MODV[:, c], MODV[:, c], 1.0, None, AX.add), waits=(t_sel,))

    t_absb = dma("HS", R_CW[0:1, 0:8], absb_o.ap(), waits=(t_ar_abs,))
    vop(lambda e: e.tensor_scalar(R_CW[0:1, 0:8], R_CW[0:1, 0:8],
                                  1.0 / NELEM_Q, None, AX.mult), waits=(t_absb,))
    vop(lambda e: e.tensor_scalar(R_CW[0:1, 0:8], R_CW[0:1, 0:8],
                                  1e-5, None, AX.max))
    t_sq_r = vop(lambda e: e.reciprocal(R_SQ[0:1, 0:8], R_CW[0:1, 0:8]))
    vop(lambda e: e.tensor_scalar(R_CWQ[0:1, 0:2], R_CW[0:1, 0:2],
                                  1.0 / 127.0, None, AX.mult))
    t_cwq = vop(lambda e: e.tensor_scalar(R_CWQ[0:1, 2:5], R_CW[0:1, 2:5],
                                          1.0 / 128.0, None, AX.mult))
    if DBG:
        dma("CS", dbg["modv"], MODV[:, :], waits=(t_modv,))
        dma("CS", dbg["cw"], R_CW[0:1, :], waits=(t_cwq,))
        dma("CS", dbg["cwq"], R_CWQ[0:1, :], waits=(t_cwq,))
    ag_toks = {}
    last = None
    for mi, (name, I, O) in enumerate(QMATS):
        tb = mm(ONESR[:, :], R_SQ[0:1, mi:mi + 1], pSM[0:128, 0:1], True, True,
                waits=(t_sq_r, pSM_free[0]), inc="eT")
        tsc = vop(lambda e: e.tensor_copy(SCL[:, 0:1], pSM[0:128, 0:1]), waits=(tb,))
        pSM_free[0] = tsc
        na = I // 8 // 128
        for hf in range(2):
            qb = qb_i[0]
            qb_i[0] = 1 - qb
            tok_l = dma("QS", QBUFS[qb][:, :].rearrange("p (a o) -> p a o", o=O),
                        qsl_d[name].rearrange("(a p) o -> p a o", p=128)
                        [:, hf * na // 2:(hf + 1) * na // 2, :],
                        waits=(qb_last[qb],))
            sop(lambda e, qb=qb: e.activation(
                QBUFS[qb][:, :], QBUFS[qb][:, :], ACT.Copy, bias=MAGIC,
                scale=SCL[:, 0:1]), waits=(tok_l, tsc))
            s2 = sop(lambda e, qb=qb: e.activation(
                QBUFS[qb][:, :], QBUFS[qb][:, :], ACT.Copy, bias=-MAGIC))
            t3 = vop(lambda e, qb=qb: e.tensor_scalar(
                QBW[:, :], QBUFS[qb][:, :], -1.0, 1.0, AX.max, AX.min),
                waits=(s2, last))
            qb_last[qb] = t3
            last = dma("WQ", wq_i[name].ap().rearrange("(a p) o -> p a o", p=128)
                       [:, hf * na // 2:(hf + 1) * na // 2, :],
                       QBW[:, :].rearrange("p (a o) -> p a o", o=O),
                       waits=(t3,), queue="scalar")
        ag_toks[name] = C.push("gpsimd", lambda e, n=name: e.collective_compute(
            "AllGather", AX.bypass, replica_groups=[list(range(8))],
            ins=[wq_i[n].ap()], outs=[wq_o[n].ap()]), waits=(scur("WQ"),), inc="cC")
    for nm, src_d in [("wout", woutT_d), ("Wq", Wq_d), ("Wk", Wk_d),
                      ("Wv", Wv_d), ("Wo", Wo_d)]:
        pc_tok[nm] = dma("PC", wbf[nm].ap(), src_d)
    scope0.close()
    C.soft_barrier()

    scopeA = contextlib.ExitStack()
    sbA = lambda n, s, d: scopeA.enter_context(nc.sbuf_tensor(n, s, d))
    L0COS = sbA("L0COSx", [128, 1024], BF16)
    L0SIN = sbA("L0SINx", [128, 1024], BF16)
    PROT0 = sbA("PROT0x", [128, 128], BF16)
    L0MASK = sbA("L0MASKx", [128, 2048], BF16)
    PMKT = sbA("PMKTx", [128, 256], BF16)
    PMV = sbA("PMVx", [16, 2048], BF16)
    QT = sbA("QTx", [128, 2 * 512], BF16)
    KT = sbA("KTx", [128, 2 * 512], BF16)
    VV = sbA("VVx", [128, 4 * 2048], BF16)
    PP = sbA("PPx", [128, 4 * 512], BF16)
    PPP = sbA("PPPx", [16, 512], BF16)
    CTX = sbA("CTXx", [128, 16 * 512], BF16)
    DNB = sbA("DNBx", [128, 512], F32)
    dma("CS", L0COS[:, :], l0cos_d)
    dma("CS", L0SIN[:, :], l0sin_d)
    tok_prot0 = dma("CS", PROT0[:, :], prot0_d)
    t_msk0 = dma("CS", L0MASK[:, :], l0mask_d)
    tok_ropetab = t_msk0
    PMKF = sbA("PMKFx", [128, 256], F32)
    PMVF = sbA("PMVFx", [16, 2048], F32)
    t_pmkf = dma("CS", PMKF[:, :], pmkt_d)
    t_pmvf = dma("CS", PMVF[:, :], pmv_d)
    t_pmk = vop(lambda e: e.tensor_copy(PMKT[:, :], PMKF[:, :]),
                waits=(t_pmkf,))
    t_pmv = vop(lambda e: e.tensor_copy(PMV[:, :], PMVF[:, :]),
                waits=(t_pmvf,))
    SC0 = float(1.0 / np.sqrt(128.0))

    def rope_apply(dst, src_raw_bf, rot_psum_bank, costab, sintab, tok_mm):
        vop(lambda e: e.tensor_mul(TMP[:, 0:512], src_raw_bf, costab),
            waits=(tok_ropetab,))
        vop(lambda e: e.tensor_mul(TMP2[:, 0:512], rot_psum_bank[0:128, 0:512],
                                   sintab), waits=(tok_mm,))
        return vop(lambda e, d=dst: e.tensor_add(d, TMP[:, 0:512], TMP2[:, 0:512]))

    for w in range(2):
        tok0 = w * 512
        wsync("vector", [("eT", C.cnt["eT"])])
        t_h = ln_modulated(0, 0, tok0, 512, t_modv)
        m2 = None
        m2s = [None] * NCH
        for cch in range(NCH):
            sl_ = (cch % 2) * 512
            hs = H[:, cch * 512:(cch + 1) * 512]
            t_sq = vop(lambda e, h=hs, sl_=sl_: e.tensor_mul(
                XB2[:, sl_:sl_ + 512], h, h),
                waits=((t_h,) if cch == 0 else ()) +
                ((m2s[cch - 2],) if cch >= 2 else ()))
            m2 = mm(ONESC[:, :], XB2[:, sl_:sl_ + 512], pSTAT[0:1, 0:512],
                    cch == 0, cch == NCH - 1,
                    waits=(t_sq, stat_free[0]) if cch == 0 else (t_sq,),
                    inc="eT")
            m2s[cch] = m2
        t_ss = vop(lambda e: e.tensor_scalar(R_SS[0:1, 0:512], pSTAT[0:1, 0:512],
                                             1.0 / D, None, AX.mult), waits=(m2,))
        stat_free[0] = t_ss
        sop(lambda e: e.activation(R_A[0:1, 0:512], R_SS[0:1, 0:512], ACT.Ln,
                                   bias=EPSC[0:1, 1:2]), waits=(t_ss,))
        t_a = sop(lambda e: e.activation(R_A[0:1, 0:512], R_A[0:1, 0:512],
                                         ACT.Exp, scale=-0.5))
        if DBG and w == 0:
            dma("CS", dbg["h"], H[:, :], waits=(t_h,))
        tA = bcast_row(R_A[0:1, 0:512], BCA, 512, [t_a])
        t_xn = None
        for cch in range(NCH):
            hs = H[:, cch * 512:(cch + 1) * 512]
            vop(lambda e, h=hs: e.tensor_mul(h, h, BCA[:, 0:512]),
                waits=(tA, scur("CS")) if cch == 0 else ())
            t_xn = vop(lambda e, h=hs, cch=cch: e.tensor_scalar(
                h, h, NORMW[:, cch:cch + 1], None, AX.mult),
                waits=(t_nw,) if cch == 0 else ())
        if DBG and w == 0:
            dma("CS", dbg["xn"], H[:, :], waits=(t_xn,))
        for sl in range(8):
            b, tok_w = wload(wbf["wqkv"].ap()[:, 4096 + sl * 256: 4096 + (sl + 1) * 256], waits=(pc_tok["wqkv"],))
            lastm = None
            for ts4 in range(4):
                bk = ts4 % 2
                for kc in range(NCH):
                    lastm = mm(H[:, kc * 512 + ts4 * 128: kc * 512 + (ts4 + 1) * 128],
                               WB[b][:, kc * 256:(kc + 1) * 256],
                               pbank[bk][0:128, 0:256], kc == 0, kc == NCH - 1,
                               waits=(tok_w, t_xn, banks[bk]) if kc == 0 else (),
                               inc="eT" if kc == NCH - 1 else None)
                ev = vop(lambda e, ts4=ts4, sl=sl, bk=bk: e.tensor_copy(
                    VV[:, ts4 * 2048 + sl * 256: ts4 * 2048 + (sl + 1) * 256],
                    pbank[bk][0:128, 0:256]), waits=(lastm,))
                banks[bk] = ev
            wb_free[b] = lastm
        tok_vv = ("eV", C.cnt["eV"])
        for hp in range(8):
            heads = (2 * hp, 2 * hp + 1)
            b_q, tok_wq = wload(wbf["wqkv"].ap()[:, hp * 256:(hp + 1) * 256], waits=(pc_tok["wqkv"],))
            b_k, tok_wk = wload(wbf["wqkv"].ap()[:, 2048 + hp * 256: 2048 + (hp + 1) * 256], waits=(pc_tok["wqkv"],))
            for which, (bw, tok_w, buf) in enumerate(
                    [(b_q, tok_wq, QT), (b_k, tok_wk, KT)]):
                for ml in range(2):
                    bk = 2 + ml
                    lastm = None
                    for kc in range(NCH):
                        lastm = mm(WB[bw][:, kc * 256 + ml * 128:
                                          kc * 256 + (ml + 1) * 128],
                                   H[:, kc * 512:(kc + 1) * 512],
                                   pbank[bk][0:128, 0:512], kc == 0, kc == NCH - 1,
                                   waits=(tok_w, t_xn, banks[bk]) if kc == 0 else (),
                                   inc="eT" if kc == NCH - 1 else None)
                    t_raw = vop(lambda e, bk=bk, ml=ml: e.tensor_copy(
                        XB[:, ml * 512:(ml + 1) * 512], pbank[bk][0:128, 0:512]),
                        waits=(lastm,))
                    banks[bk] = t_raw
                    t_rot = mm(PROT0[:, :], XB[:, ml * 512:(ml + 1) * 512],
                               pbank[bk][0:128, 0:512], True, True,
                               waits=(t_raw, tok_prot0), inc="eT")
                    tr = rope_apply(buf[:, ml * 512:(ml + 1) * 512],
                                    XB[:, ml * 512:(ml + 1) * 512],
                                    pbank[bk],
                                    L0COS[:, tok0:tok0 + 512],
                                    L0SIN[:, tok0:tok0 + 512], t_rot)
                    banks[bk] = tr
            wb_free[b_q] = ("eT", C.cnt["eT"])
            wb_free[b_k] = ("eT", C.cnt["eT"])
            tok_rope = ("eV", C.cnt["eV"])
            for hh_i, hh in enumerate(heads):
                qtile = QT[:, hh_i * 512:(hh_i + 1) * 512]
                ktile = KT[:, hh_i * 512:(hh_i + 1) * 512]
                sc_toks = []
                for kc in range(4):
                    tkk = mm(ktile[:, kc * 128:(kc + 1) * 128], qtile,
                             pbank[kc][0:128, 0:512], True, True,
                             waits=(banks[kc], tok_rope), inc="eT")
                    sc_toks.append(tkk)
                tpp = mm(PMKT[:, hh * 16:(hh + 1) * 16], qtile,
                         pSTAT2[0:16, 0:512], True, True,
                         waits=(t_pmk, stat_free[1], tok_rope), inc="eT")
                ev_toks = []
                for kc in range(4):
                    ev = vop(lambda e, kc=kc: e.tensor_tensor(
                        PP[:, kc * 512:(kc + 1) * 512], pbank[kc][0:128, 0:512],
                        L0MASK[:, kc * 512:(kc + 1) * 512], AX.add),
                        waits=(sc_toks[kc], t_msk0))
                    banks[kc] = ev
                    ev_toks.append(ev)
                evp = vop(lambda e: e.tensor_copy(PPP[0:16, 0:512],
                                                  pSTAT2[0:16, 0:512]), waits=(tpp,))
                stat_free[1] = evp
                for kc in range(4):
                    vop(lambda e, kc=kc: e.reduce_max(
                        MCOL[:, kc:kc + 1], PP[:, kc * 512:(kc + 1) * 512], XL),
                        waits=(ev_toks[kc],))
                vop(lambda e: e.reduce_max(MCOL[0:16, 4:5], PPP[0:16, 0:512], XL),
                    waits=(evp,))
                vop(lambda e: e.tensor_max(MCOL[:, 5:6], MCOL[:, 0:1], MCOL[:, 1:2]))
                vop(lambda e: e.tensor_max(MCOL[:, 6:7], MCOL[:, 2:3], MCOL[:, 3:4]))
                vop(lambda e: e.tensor_max(MCOL[:, 7:8], MCOL[:, 5:6], MCOL[:, 6:7]))
                t_mx = vop(lambda e: e.tensor_max(MCOL[0:16, 7:8], MCOL[0:16, 7:8],
                                                  MCOL[0:16, 4:5]))
                t_nb = negbias_from_col(MCOL[:, 7:8], SC0, 8, [t_mx])
                ex_toks = []
                for kc in range(4):
                    ex = sop(lambda e, kc=kc: e.activation(
                        PP[:, kc * 512:(kc + 1) * 512],
                        PP[:, kc * 512:(kc + 1) * 512],
                        ACT.Exp, bias=MCOL[:, 8:9], scale=SC0), waits=(t_nb,))
                    ex_toks.append(ex)
                exp_p = sop(lambda e: e.activation(
                    PPP[0:16, 0:512], PPP[0:16, 0:512], ACT.Exp,
                    bias=MCOL[0:16, 8:9], scale=SC0), waits=(t_nb,))
                dn = None
                for kc in range(4):
                    dn = mm(ONESC[:, :], PP[:, kc * 512:(kc + 1) * 512],
                            pSTAT[0:1, 0:512], kc == 0, False,
                            waits=(ex_toks[kc], stat_free[0] if kc == 0 else None))
                dn = mm(ONESC[0:16, :], PPP[0:16, 0:512], pSTAT[0:1, 0:512],
                        False, True, waits=(exp_p,), inc="eT")
                t_rc = vop(lambda e: e.reciprocal(R_T[0:1, 0:512],
                                                  pSTAT[0:1, 0:512]), waits=(dn,))
                stat_free[0] = t_rc
                t_dnb = bcast_row(R_T[0:1, 0:512], DNB, 512, [t_rc])
                av = None
                for kc in range(4):
                    av = mm(VV[:, kc * 2048 + hh * 128: kc * 2048 + (hh + 1) * 128],
                            PP[:, kc * 512:(kc + 1) * 512],
                            pbank[0][0:128, 0:512], kc == 0, False,
                            waits=(ex_toks[kc], banks[0], tok_vv) if kc == 0
                            else (ex_toks[kc],))
                av = mm(PMV[0:16, hh * 128:(hh + 1) * 128], PPP[0:16, 0:512],
                        pbank[0][0:128, 0:512], False, True,
                        waits=(exp_p, t_pmv), inc="eT")
                t_ctx = vop(lambda e, hh=hh: e.tensor_mul(
                    CTX[:, hh * 512:(hh + 1) * 512], pbank[0][0:128, 0:512],
                    DNB[:, 0:512]), waits=(av, t_dnb))
                banks[0] = t_ctx
        if DBG and w == 0:
            dma("CS", dbg["ctx"], CTX[:, :], waits=(("eV", C.cnt["eV"]),))
        tok_ctx = ("eV", C.cnt["eV"])
        for sl in range(8):
            b, tok_w = wload(wbf["wout"].ap()[:, sl * 256:(sl + 1) * 256], waits=(pc_tok["wout"],))
            for ml in range(2):
                oc = sl * 2 + ml
                bk = ml
                lastm = None
                for kc in range(NCH):
                    lastm = mm(WB[b][:, kc * 256 + ml * 128:
                                     kc * 256 + (ml + 1) * 128],
                               CTX[:, kc * 512:(kc + 1) * 512],
                               pbank[bk][0:128, 0:512], kc == 0, kc == NCH - 1,
                               waits=(tok_w, banks[bk], tok_ctx) if kc == 0 else (),
                               inc="eT" if kc == NCH - 1 else None)
                ev = vop(lambda e, oc=oc, bk=bk, tok0=tok0: e.scalar_tensor_tensor(
                    X[:, oc * 1024 + tok0: oc * 1024 + tok0 + 512],
                    pbank[bk][0:128, 0:512], MODV[:, 32 + oc:33 + oc],
                    X[:, oc * 1024 + tok0: oc * 1024 + tok0 + 512],
                    AX.mult, AX.add), waits=(lastm, t_modv))
                banks[bk] = ev
            wb_free[b] = lastm
        res_tok[0] = banks[1]
    if DBG:
        dma("CS", dbg["x1"], X[:, :], waits=(res_tok[0],))
    scopeA.close()
    C.soft_barrier()

    scopeB = contextlib.ExitStack()
    sbB = lambda n, s, d: scopeB.enter_context(nc.sbuf_tensor(n, s, d))
    XQ = sbB("XQx", [128, 16 * 256], BF16)
    MID = sbB("MIDx", [128, 64 * 256], BF16)

    def quant_rows_rms(N, cw_col, waits=(), sqd=SQD):
        sop(lambda e, N=N: e.activation(R_A[0:1, 0:N], R_SS[0:1, 0:N], ACT.Ln),
            waits=tuple(waits))
        t_exp = sop(lambda e, N=N: e.activation(R_A[0:1, 0:N], R_A[0:1, 0:N],
                                                ACT.Exp, scale=-0.5))
        vop(lambda e, N=N: e.tensor_scalar(R_A[0:1, 0:N], R_A[0:1, 0:N],
                                           1e12, None, AX.min), waits=(t_exp,))
        t_sr = vop(lambda e, N=N, sqd=sqd: e.tensor_scalar(
            R_A[0:1, 0:N], R_A[0:1, 0:N], sqd, None, AX.mult))
        vop(lambda e, N=N: e.tensor_mul(R_T[0:1, 0:N], R_A[0:1, 0:N],
                                        R_MX[0:1, 0:N]))
        vop(lambda e, N=N: e.tensor_scalar(R_T[0:1, 0:N], R_T[0:1, 0:N],
                                           1e-5, None, AX.max))
        vop(lambda e, N=N: e.reciprocal(R_B[0:1, 0:N], R_T[0:1, 0:N]))
        vop(lambda e, N=N: e.tensor_scalar(R_B[0:1, 0:N], R_B[0:1, 0:N],
                                           127.0, None, AX.mult))
        t_cb = vop(lambda e, N=N: e.tensor_mul(R_A[0:1, 0:N], R_A[0:1, 0:N],
                                               R_B[0:1, 0:N]))
        t_os = vop(lambda e, N=N, cw=cw_col: e.tensor_scalar(
            R_OS[0:1, 0:N], R_T[0:1, 0:N], R_CWQ[0:1, cw:cw + 1], None, AX.mult),
            waits=(t_cwq,))
        return t_cb, t_os

    if DBG:
        dma("CS", dbg["w1q"], wq_o["w1"].ap(), waits=(ag_toks["w1"],))
        dma("CS", dbg["w2q"], wq_o["w2"].ap(), waits=(ag_toks["w2"],))
    for g in range(4):
        tok0 = g * 256
        N = 256
        wsync("vector", [("eT", C.cnt["eT"])])
        t_h = ln_modulated(0, 48, tok0, N, t_modv)
        if DBG and g == 0:
            dma("CS", dbg["hffn"], H[:, :], waits=(t_h,))
        m2 = None
        m2s = [None] * NCH
        for cch in range(NCH):
            sl_ = (cch % 2) * 512
            hs = H[:, cch * 512: cch * 512 + N]
            t_sq = vop(lambda e, h=hs, sl_=sl_: e.tensor_mul(
                XB2[:, sl_:sl_ + N], h, h),
                waits=((t_h,) if cch == 0 else ()) +
                ((m2s[cch - 2],) if cch >= 2 else ()))
            m2 = mm(ONESC[:, :], XB2[:, sl_:sl_ + N], pSTAT[0:1, 0:N],
                    cch == 0, cch == NCH - 1,
                    waits=(t_sq, stat_free[0]) if cch == 0 else (t_sq,),
                    inc="eT")
            m2s[cch] = m2
        t_ssr = vop(lambda e: e.tensor_copy(R_SS[0:1, 0:N], pSTAT[0:1, 0:N]),
                    waits=(m2,))
        stat_free[0] = t_ssr
        t_amx = absmax_to_row([H[:, cch * 512: cch * 512 + N]
                               for cch in range(NCH)], N, [t_h])
        t_cb, t_os = quant_rows_rms(N, 0, waits=(t_ssr, t_amx))
        tA = bcast_row(R_A[0:1, 0:N], BCA, N, [t_cb])
        tq = None
        for cch in range(NCH):
            hs = H[:, cch * 512: cch * 512 + N]
            vop(lambda e, h=hs: e.tensor_mul(TMP[:, 0:N], h, BCA[:, 0:N]),
                waits=(tA,) if cch == 0 else ())
            tq = vop(lambda e, cch=cch: e.tensor_scalar(
                XQ[:, cch * 256: cch * 256 + N], TMP[:, 0:N], MAGIC, -MAGIC,
                AX.add, AX.add))
        tB = bcast_row(R_OS[0:1, 0:N], BCB, N, [t_os])
        if DBG and g == 0:
            dma("CS", dbg["q1ss"], R_SS[0:1, 0:256], waits=(t_ssr,))
            dma("CS", dbg["q1mx"], R_MX[0:1, 0:256], waits=(t_amx,))
            dma("CS", dbg["q1cb"], R_A[0:1, 0:256], waits=(t_cb,))
            dma("CS", dbg["q1os"], R_OS[0:1, 0:256], waits=(t_os,))
            dma("CS", dbg["xq"], XQ[:, :], waits=(tq,))
            wsync("vector", [scur("CS")])
            wsync("scalar", [scur("CS")])
        for sl in range(32):
            b, tok_w = wload(wq_o["w1"].ap()[:, sl * 256:(sl + 1) * 256],
                             waits=(ag_toks["w1"],), queue="sync")
            for ml in range(2):
                oc = sl * 2 + ml
                bk = oc % 4
                lastm = None
                for kc in range(NCH):
                    lastm = mm(WB[b][:, kc * 256 + ml * 128:
                                     kc * 256 + (ml + 1) * 128],
                               XQ[:, kc * 256:(kc + 1) * 256],
                               pbank[bk][0:128, 0:N], kc == 0, kc == NCH - 1,
                               waits=(tok_w, tq, banks[bk]) if kc == 0 else (),
                               inc="eT" if kc == NCH - 1 else None)
                t1 = vop(lambda e, bk=bk: e.tensor_mul(
                    TMP2[:, 0:N], pbank[bk][0:128, 0:N], BCB[:, 0:N]),
                    waits=(lastm, tB))
                banks[bk] = t1
                sop(lambda e: e.activation(TMP2[:, 0:N], TMP2[:, 0:N], ACT.Gelu),
                    waits=(t1,))
                g2 = sop(lambda e, oc=oc: e.activation(
                    MID[:, oc * 256: oc * 256 + N], TMP2[:, 0:N], ACT.Gelu))
                vop(lambda e: e.tensor_copy(TMP3[:, 3:4], TMP3[:, 3:4]),
                    waits=(g2,))
            wb_free[b] = lastm
        tok_gelu = ("eS", C.cnt["eS"])
        if DBG and g == 0:
            dma("CS", dbg["mid"], MID[:, :], waits=(tok_gelu,))
            wsync("vector", [scur("CS")])
        m2 = None
        m2s = [None] * 64
        for cch in range(64):
            sl_ = (cch % 2) * 512
            ms = MID[:, cch * 256: cch * 256 + N]
            t_sq = vop(lambda e, h=ms, sl_=sl_: e.tensor_mul(
                XB2[:, sl_:sl_ + N], h, h),
                waits=((tok_gelu,) if cch == 0 else ()) +
                ((m2s[cch - 2],) if cch >= 2 else ()))
            m2 = mm(ONESC[:, :], XB2[:, sl_:sl_ + N], pSTAT[0:1, 0:N],
                    cch == 0, cch == 63,
                    waits=(t_sq, stat_free[0]) if cch == 0 else (t_sq,),
                    inc="eT")
            m2s[cch] = m2
        t_ssr = vop(lambda e: e.tensor_copy(R_SS[0:1, 0:N], pSTAT[0:1, 0:N]),
                    waits=(m2,))
        stat_free[0] = t_ssr
        t_amx2 = absmax_to_row([MID[:, cch * 256: cch * 256 + N]
                                for cch in range(64)], N, [tok_gelu])
        t_cb, t_os = quant_rows_rms(N, 1, waits=(t_ssr, t_amx2),
                                    sqd=float(np.sqrt(8192.0)))
        tA = bcast_row(R_A[0:1, 0:N], BCA, N, [t_cb])
        tq = None
        for cch in range(64):
            ms = MID[:, cch * 256: cch * 256 + N]
            vop(lambda e, h=ms: e.tensor_mul(TMP[:, 0:N], h, BCA[:, 0:N]),
                waits=(tA,) if cch == 0 else ())
            tq = vop(lambda e, h=ms: e.tensor_scalar(
                h, TMP[:, 0:N], MAGIC, -MAGIC, AX.add, AX.add))
        tB = bcast_row(R_OS[0:1, 0:N], BCB, N, [t_os])
        if DBG and g == 0:
            dma("CS", dbg["q2cb"], R_A[0:1, 0:256], waits=(t_cb,))
            dma("CS", dbg["q2os"], R_OS[0:1, 0:256], waits=(t_os,))
            dma("CS", dbg["mint"], MID[:, :], waits=(tq,))
            wsync("vector", [scur("CS")])
            wsync("scalar", [scur("CS")])
        for sl in range(8):
            lastml = [None, None]
            for jcg in range(4):
                b, tok_w = wload(wq_o["w2"].ap()
                                 [jcg * 2048:(jcg + 1) * 2048,
                                  sl * 256:(sl + 1) * 256],
                                 waits=(ag_toks["w2"],), queue="sync")
                for ml in range(2):
                    bk = ml
                    for kc in range(NCH):
                        jc = jcg * 16 + kc
                        lastml[ml] = mm(WB[b][:, kc * 256 + ml * 128:
                                              kc * 256 + (ml + 1) * 128],
                                        MID[:, jc * 256:(jc + 1) * 256],
                                        pbank[bk][0:128, 0:N],
                                        jcg == 0 and kc == 0,
                                        jcg == 3 and kc == NCH - 1,
                                        waits=(tok_w, tq, banks[bk])
                                        if (jcg == 0 and kc == 0) else (tok_w,)
                                        if kc == 0 else (),
                                        inc="eT" if kc == NCH - 1 else None)
                wb_free[b] = lastml[1]
            for ml in range(2):
                oc = sl * 2 + ml
                bk = ml
                t1 = vop(lambda e, bk=bk: e.tensor_mul(
                    TMP2[:, 0:N], pbank[bk][0:128, 0:N], BCB[:, 0:N]),
                    waits=(lastml[ml], tB))
                ev = vop(lambda e, oc=oc, tok0=tok0: e.scalar_tensor_tensor(
                    X[:, oc * 1024 + tok0: oc * 1024 + tok0 + N],
                    TMP2[:, 0:N], MODV[:, 80 + oc:81 + oc],
                    X[:, oc * 1024 + tok0: oc * 1024 + tok0 + N],
                    AX.mult, AX.add), waits=(t_modv,))
                banks[bk] = ev
        res_tok[0] = banks[1]
    if DBG:
        dma("CS", dbg["x2"], X[:, :], waits=(res_tok[0],))
    scopeB.close()
    C.soft_barrier()

    for qt_own in range(2):
        wsync("vector", [("eT", C.cnt["eT"]), scur("HB")])
        t_h = ln_modulated(1, 0, qt_own * 512, 512, t_modv)
        dma("HB",
            hb_i.ap().rearrange("(c p) t -> p c t", p=128)
            [:, :, qt_own * 512:(qt_own + 1) * 512],
            H[:, :].rearrange("p (c t) -> p c t", t=512), waits=(t_h,))
    t_hag = C.push("gpsimd", lambda e: e.collective_compute(
        "AllGather", AX.bypass, replica_groups=[[0, 1], [2, 3], [4, 5], [6, 7]],
        ins=[hb_i.ap()], outs=[hb_o.ap()]),
        waits=(scur("HB"),), inc="cC")
    C.soft_barrier()

    scopeC = contextlib.ExitStack()
    sbC = lambda n, s, d: scopeC.enter_context(nc.sbuf_tensor(n, s, d))
    L1COS = sbC("L1COSx", [128, 2048], BF16)
    L1SIN = sbC("L1SINx", [128, 2048], BF16)
    L1MASK = sbC("L1MASKx", [128, 2048], BF16)
    KT1 = sbC("KT1x", [128, 2 * 2048], BF16)
    VV1 = sbC("VV1x", [128, 16 * 260], BF16)
    QT1 = sbC("QT1x", [128, 8 * 512], BF16)
    PP1 = sbC("PP1x", [128, 16 * 512], BF16)
    C2Q = sbC("C2Qx", [128, 4 * 1024], BF16)
    dma("CS", L1COS[:, :], l1cos_d)
    tok_rt1 = dma("CS", L1SIN[:, :], l1sin_d)
    t_msk1 = dma("CS", L1MASK[:, :], l1mask_d)
    t_vones = vop(lambda e: e.memset(VV1[:, :], 1.0))
    SC1 = 0.125

    def rope512(dst, raw_bf, bank, cos_ap, sin_ap, tok_mm):
        vop(lambda e: e.tensor_mul(TMP[:, 0:512], raw_bf, cos_ap),
            waits=(tok_rt1,))
        vop(lambda e: e.tensor_mul(TMP2[:, 0:512], bank[0:128, 0:512], sin_ap),
            waits=(tok_mm,))
        return vop(lambda e, d=dst: e.tensor_add(d, TMP[:, 0:512], TMP2[:, 0:512]))

    ctx_dma_toks = []
    for qt in range(4):
        nkc = 4 * (qt + 1)
        rb = (qt // 2) * 2048
        c0c = (qt % 2) * 512
        t_hs = dma("HS", H[:, :].rearrange("p (c t) -> p c t", t=512),
                   hb_o.ap()[rb:rb + 2048, c0c:c0c + 512]
                   .rearrange("(c p) t -> p c t", p=128),
                   waits=(t_hag, ("eT", C.cnt["eT"])))
        b_k, tok_wk = wload(wbf["Wk"].ap()[:, :], waits=(pc_tok["Wk"],))
        for oc in range(2):
            bk = 2
            lastm = None
            for kc in range(NCH):
                lastm = mm(WB[b_k][:, kc * 256 + oc * 128:
                                   kc * 256 + (oc + 1) * 128],
                           H[:, kc * 512:(kc + 1) * 512],
                           pbank[bk][0:128, 0:512], kc == 0, kc == NCH - 1,
                           waits=(tok_wk, t_hs, banks[bk]) if kc == 0 else (),
                           inc="eT" if kc == NCH - 1 else None)
            t_raw = vop(lambda e, bk=bk: e.tensor_copy(XB[:, 0:512],
                                                       pbank[bk][0:128, 0:512]),
                        waits=(lastm,))
            banks[bk] = t_raw
            t_rot = mm(PROT1[:, :], XB[:, 0:512], pbank[bk][0:128, 0:512],
                       True, True, waits=(t_raw,), inc="eT")
            tr = rope512(KT1[:, oc * 2048 + qt * 512: oc * 2048 + (qt + 1) * 512],
                         XB[:, 0:512], pbank[bk],
                         L1COS[:, qt * 512:(qt + 1) * 512],
                         L1SIN[:, qt * 512:(qt + 1) * 512], t_rot)
            banks[bk] = tr
        wb_free[b_k] = lastm
        b_v, tok_wv = wload(wbf["Wv"].ap()[:, :], waits=(pc_tok["Wv"],))
        for ts4 in range(4):
            gkc = qt * 4 + ts4
            bk = 3
            lastm = None
            for kc in range(NCH):
                lastm = mm(H[:, kc * 512 + ts4 * 128: kc * 512 + (ts4 + 1) * 128],
                           WB[b_v][:, kc * 256:(kc + 1) * 256],
                           pbank[bk][0:128, 0:256], kc == 0, kc == NCH - 1,
                           waits=(tok_wv, t_hs, banks[bk]) if kc == 0 else (),
                           inc="eT" if kc == NCH - 1 else None)
            ev = None
            for lg in range(4):
                ev = vop(lambda e, gkc=gkc, lg=lg, bk=bk: e.tensor_copy(
                    VV1[:, gkc * 260 + lg * 65: gkc * 260 + lg * 65 + 64],
                    pbank[bk][0:128, lg * 64:(lg + 1) * 64]),
                    waits=(lastm, t_vones) if lg == 0 else ())
            banks[bk] = ev
        wb_free[b_v] = lastm
        for sl in range(4):
            b_q, tok_wq = wload(wbf["Wq"].ap()[:, sl * 256:(sl + 1) * 256], waits=(pc_tok["Wq"],))
            for ml in range(2):
                pc = sl * 2 + ml
                bk = 2
                lastm = None
                for kc in range(NCH):
                    lastm = mm(WB[b_q][:, kc * 256 + ml * 128:
                                       kc * 256 + (ml + 1) * 128],
                               H[:, kc * 512:(kc + 1) * 512],
                               pbank[bk][0:128, 0:512], kc == 0, kc == NCH - 1,
                               waits=(tok_wq, t_hs, banks[bk]) if kc == 0 else (),
                               inc="eT" if kc == NCH - 1 else None)
                t_raw = vop(lambda e, bk=bk: e.tensor_copy(
                    XB[:, 0:512], pbank[bk][0:128, 0:512]), waits=(lastm,))
                banks[bk] = t_raw
                t_rot = mm(PROT1[:, :], XB[:, 0:512], pbank[bk][0:128, 0:512],
                           True, True, waits=(t_raw,), inc="eT")
                tr = rope512(QT1[:, pc * 512:(pc + 1) * 512], XB[:, 0:512],
                             pbank[bk], L1COS[:, qt * 512:(qt + 1) * 512],
                             L1SIN[:, qt * 512:(qt + 1) * 512], t_rot)
                banks[bk] = tr
            wb_free[b_q] = lastm
        tok_rope1 = ("eV", C.cnt["eV"])
        tok_c2free = scur("CX")
        for pc in range(8):
            for half in range(2):
                hloc = QPERM[2 * pc + half]
                lg = hloc // 4
                kchunk = lg // 2
                krow = (lg % 2) * 64
                qv = QT1[half * 64:half * 64 + 64, pc * 512:(pc + 1) * 512]
                ev_toks = []
                for kc in range(nkc):
                    bk = kc % 2
                    tkk = mm(KT1[krow:krow + 64,
                                 kchunk * 2048 + kc * 128:
                                 kchunk * 2048 + (kc + 1) * 128],
                             qv, pbank[bk][0:128, 0:512], True, True,
                             waits=(banks[bk], tok_rope1), inc="eT")
                    j = kc - 4 * qt
                    if 0 <= j < 4:
                        ev = vop(lambda e, kc=kc, j=j, bk=bk: e.tensor_tensor(
                            PP1[:, kc * 512:(kc + 1) * 512],
                            pbank[bk][0:128, 0:512],
                            L1MASK[:, j * 512:(j + 1) * 512], AX.add),
                            waits=(tkk, t_msk1))
                    else:
                        ev = vop(lambda e, kc=kc, bk=bk: e.tensor_copy(
                            PP1[:, kc * 512:(kc + 1) * 512],
                            pbank[bk][0:128, 0:512]), waits=(tkk,))
                    banks[bk] = ev
                    ev_toks.append(ev)
                for kc in range(nkc):
                    vop(lambda e, kc=kc: e.reduce_max(
                        MCOL[:, kc:kc + 1], PP1[:, kc * 512:(kc + 1) * 512], XL),
                        waits=(ev_toks[kc],))
                t_mx = vop(lambda e, nkc=nkc: e.tensor_reduce(
                    MCOL[:, 16:17], MCOL[:, 0:nkc], XL, AX.max))
                t_nb = negbias_from_col(MCOL[:, 16:17], SC1, 17, [t_mx])
                ex_toks = []
                for kc in range(nkc):
                    ex = sop(lambda e, kc=kc: e.activation(
                        PP1[:, kc * 512:(kc + 1) * 512],
                        PP1[:, kc * 512:(kc + 1) * 512], ACT.Exp,
                        bias=MCOL[:, 17:18], scale=SC1), waits=(t_nb,))
                    ex_toks.append(ex)
                for qsub in range(4):
                    av = None
                    for kc in range(nkc):
                        av = mm(PP1[:, kc * 512 + qsub * 128:
                                    kc * 512 + (qsub + 1) * 128],
                                VV1[:, kc * 260 + lg * 65: kc * 260 + lg * 65 + 65],
                                pbank[3][0:128, 0:65], kc == 0, kc == nkc - 1,
                                waits=(ex_toks[kc], banks[3], tok_rope1) if kc == 0
                                else (ex_toks[kc],),
                                inc="eT" if kc == nkc - 1 else None)
                    t_rc = vop(lambda e: e.reciprocal(DCOL[:, 0:1],
                                                      pbank[3][0:128, 64:65]),
                               waits=(av,))
                    t_ev = vop(lambda e, qsub=qsub, hloc=hloc: e.tensor_scalar(
                        C2Q[:, qsub * 1024 + hloc * 64:
                            qsub * 1024 + hloc * 64 + 64],
                        pbank[3][0:128, 0:64], DCOL[:, 0:1], None, AX.mult),
                        waits=(tok_c2free,) if (pc == 0 and half == 0 and
                                                qsub == 0) else ())
                    banks[3] = t_ev
        tok_c2 = dma("CX",
                     ctx2b.ap()[qt * 512:(qt + 1) * 512, :]
                     .rearrange("(qc p) hd -> p qc hd", p=128),
                     C2Q[:, :].rearrange("p (qc hd) -> p qc hd", hd=1024),
                     waits=(banks[3],))
        ctx_dma_toks.append(tok_c2)
    CTXR = PP1
    tok_pp1_free = ("eT", C.cnt["eT"])
    tok_pp1_s = ("eS", C.cnt["eS"])
    for ng in range(2):
        ld = None
        for qc in range(16):
            ld = dma("CXL", CTXR[:, qc * 512:(qc + 1) * 512],
                     ctx2b.ap()[qc * 128:(qc + 1) * 128,
                                ng * 512:(ng + 1) * 512],
                     waits=(scur("CX"), tok_pp1_free, tok_pp1_s,
                            ("eT", C.cnt["eT"]))
                     if qc == 0 else ())
        for sl in range(8):
            b, tok_w = wload(wbf["Wo"].ap()[:, sl * 256:(sl + 1) * 256], waits=(pc_tok["Wo"],))
            for ml in range(2):
                oc = sl * 2 + ml
                bk = ml
                lastm = None
                for qc in range(16):
                    lastm = mm(WB[b][:, qc * 256 + ml * 128:
                                     qc * 256 + (ml + 1) * 128],
                               CTXR[:, qc * 512:(qc + 1) * 512],
                               pbank[bk][0:128, 0:512], qc == 0, qc == 15,
                               waits=(tok_w, scur("CXL"), banks[bk])
                               if qc == 0 else (),
                               inc="eT" if qc == 15 else None)
                ev = vop(lambda e, oc=oc, ng=ng, bk=bk: e.scalar_tensor_tensor(
                    X[:, oc * 1024 + ng * 512: oc * 1024 + (ng + 1) * 512],
                    pbank[bk][0:128, 0:512], MODV[:, 96 + 32 + oc:96 + 33 + oc],
                    X[:, oc * 1024 + ng * 512: oc * 1024 + (ng + 1) * 512],
                    AX.mult, AX.add), waits=(lastm, t_modv))
                banks[bk] = ev
            wb_free[b] = lastm
    res_tok[0] = banks[1]
    if DBG:
        dma("CS", dbg["x3"], X[:, :], waits=(res_tok[0],))
    scopeC.close()
    C.soft_barrier()

    scopeD = contextlib.ExitStack()
    sbD = lambda n, s, d: scopeD.enter_context(nc.sbuf_tensor(n, s, d))
    XQ1 = sbD("XQ1x", [128, 16 * 256], BF16)
    INTER = sbD("INTERx", [128, 64 * 256], BF16)
    R1 = sbD("R1x", [128, 64 * 128], BF16)

    for g in range(4):
        tok0 = g * 256
        N = 256
        wsync("vector", [("eT", C.cnt["eT"])])
        t_h = ln_modulated(1, 48, tok0, N, t_modv)
        t_amx = absmax_to_row([H[:, cch * 512: cch * 512 + N]
                               for cch in range(NCH)], N, [t_h])
        vop(lambda e: e.tensor_scalar(R_T[0:1, 0:N], R_MX[0:1, 0:N],
                                      1e-5, None, AX.max), waits=(t_amx,))
        vop(lambda e: e.reciprocal(R_A[0:1, 0:N], R_T[0:1, 0:N]))
        t_cb = vop(lambda e: e.tensor_scalar(R_A[0:1, 0:N], R_A[0:1, 0:N],
                                             128.0, None, AX.mult))
        t_osg = vop(lambda e: e.tensor_scalar(R_OS[0:1, 0:N], R_T[0:1, 0:N],
                                              R_CWQ[0:1, 2:3], None, AX.mult),
                    waits=(t_cwq,))
        t_osu = vop(lambda e: e.tensor_scalar(R_SS[0:1, 0:N], R_T[0:1, 0:N],
                                              R_CWQ[0:1, 3:4], None, AX.mult))
        tA = bcast_row(R_A[0:1, 0:N], BCA, N, [t_cb])
        tq = None
        for cch in range(NCH):
            hs = H[:, cch * 512: cch * 512 + N]
            vop(lambda e, h=hs: e.tensor_mul(TMP[:, 0:N], h, BCA[:, 0:N]),
                waits=(tA,) if cch == 0 else ())
            vop(lambda e: e.tensor_scalar(TMP[:, 0:N], TMP[:, 0:N],
                                          MAGIC, -MAGIC, AX.add, AX.add))
            tq = vop(lambda e, cch=cch: e.tensor_scalar(
                XQ1[:, cch * 256: cch * 256 + N], TMP[:, 0:N],
                127.0, None, AX.min))
        tBg = bcast_row(R_OS[0:1, 0:N], BCB, N, [t_osg])
        tBu = bcast_row(R_SS[0:1, 0:N], BCA, N, [t_osu, tq])
        for sl in range(32):
            bg, tok_g = wload(wq_o["wg"].ap()[:, sl * 256:(sl + 1) * 256],
                              waits=(ag_toks["wg"],), queue="sync")
            bu, tok_u = wload(wq_o["wu"].ap()[:, sl * 256:(sl + 1) * 256],
                              waits=(ag_toks["wu"],), queue="sync")
            for ml in range(2):
                oc = sl * 2 + ml
                bg_k = 2 * (oc % 2)
                bu_k = bg_k + 1
                lg_ = None
                lu_ = None
                for kc in range(NCH):
                    lg_ = mm(WB[bg][:, kc * 256 + ml * 128:
                                    kc * 256 + (ml + 1) * 128],
                             XQ1[:, kc * 256:(kc + 1) * 256],
                             pbank[bg_k][0:128, 0:N], kc == 0, kc == NCH - 1,
                             waits=(tok_g, tq, banks[bg_k]) if kc == 0 else (),
                             inc="eT" if kc == NCH - 1 else None)
                for kc in range(NCH):
                    lu_ = mm(WB[bu][:, kc * 256 + ml * 128:
                                    kc * 256 + (ml + 1) * 128],
                             XQ1[:, kc * 256:(kc + 1) * 256],
                             pbank[bu_k][0:128, 0:N], kc == 0, kc == NCH - 1,
                             waits=(tok_u, banks[bu_k]) if kc == 0 else (),
                             inc="eT" if kc == NCH - 1 else None)
                t1 = vop(lambda e, bg_k=bg_k: e.tensor_mul(
                    TMP[:, 0:N], pbank[bg_k][0:128, 0:N], BCB[:, 0:N]),
                    waits=(lg_, tBg))
                banks[bg_k] = t1
                t_sil2 = sop(lambda e: e.activation(TMP[:, 0:N], TMP[:, 0:N],
                                                    ACT.Silu), waits=(t1,))
                t2 = vop(lambda e, bu_k=bu_k: e.tensor_mul(
                    TMP2[:, 0:N], pbank[bu_k][0:128, 0:N], BCA[:, 0:N]),
                    waits=(lu_, tBu))
                banks[bu_k] = t2
                t3 = vop(lambda e, oc=oc: e.tensor_mul(
                    INTER[:, oc * 256: oc * 256 + N], TMP[:, 0:N], TMP2[:, 0:N]),
                    waits=(t_sil2,))
            wb_free[bg] = lg_
            wb_free[bu] = lu_
        tok_inter = ("eV", C.cnt["eV"])
        for sub in range(2):
            st1 = None
            for a in range(64):
                bk = a % 2
                tm = mm(H128[:, :], INTER[:, a * 256 + sub * 128:
                                          a * 256 + (sub + 1) * 128],
                        pbank[bk][0:128, 0:128], True, True,
                        waits=(banks[bk], tok_inter) if a < 2 else (banks[bk],),
                        inc="eT")
                st1 = vop(lambda e, a=a, sub=sub, bk=bk: e.tensor_copy(
                    INTER[:, a * 256 + sub * 128: a * 256 + (sub + 1) * 128],
                    pbank[bk][0:128, 0:128]), waits=(tm,))
                banks[bk] = st1
            t_dump = dma("FW",
                         shufb.ap().rearrange("p (a t) -> p a t", t=128),
                         INTER[:, :].rearrange("p (a t2) -> p a t2", t2=256)
                         [:, :, sub * 128:(sub + 1) * 128], waits=(st1,))
            g_toks = []
            for hh in range(2):
                g_toks.append(dma(
                    "FW",
                    R1[hh * 64:(hh + 1) * 64, :]
                    .rearrange("a (c t) -> a c t", c=64, t=128),
                    shufb.ap().rearrange("(c two) (a t) -> two a c t",
                                         c=64, two=2, a=64, t=128)[hh],
                    waits=(t_dump,)))
            st2 = None
            for cc in range(64):
                bk = cc % 2
                tm = mm(BD64[:, :], R1[:, cc * 128:(cc + 1) * 128],
                        pbank[bk][0:128, 0:128], True, True,
                        waits=(banks[bk],) + tuple(g_toks) if cc < 2
                        else (banks[bk],), inc="eT")
                st2 = vop(lambda e, cc=cc, sub=sub, bk=bk: e.tensor_copy(
                    INTER[:, cc * 256 + sub * 128: cc * 256 + (sub + 1) * 128],
                    pbank[bk][0:128, 0:128]),
                    waits=(tm, t_dump) if cc == 0 else (tm,))
                banks[bk] = st2
        t_amx3 = absmax_to_row([INTER[:, cch * 256: cch * 256 + N]
                                for cch in range(64)], N, [st2])
        vop(lambda e: e.tensor_scalar(R_T[0:1, 0:N], R_MX[0:1, 0:N],
                                      LAM, None, AX.mult), waits=(t_amx3,))
        vop(lambda e: e.tensor_scalar(R_T[0:1, 0:N], R_T[0:1, 0:N],
                                      1e-5, None, AX.max))
        vop(lambda e: e.reciprocal(R_A[0:1, 0:N], R_T[0:1, 0:N]))
        t_cb = vop(lambda e: e.tensor_scalar(R_A[0:1, 0:N], R_A[0:1, 0:N],
                                             128.0 * LAM, None, AX.mult))
        t_osd = vop(lambda e: e.tensor_scalar(R_OS[0:1, 0:N], R_T[0:1, 0:N],
                                              R_CWQ[0:1, 4:5], None, AX.mult),
                    waits=(t_cwq,))
        tA = bcast_row(R_A[0:1, 0:N], BCA, N, [t_cb])
        tq = None
        for cch in range(64):
            ms = INTER[:, cch * 256: cch * 256 + N]
            vop(lambda e, h=ms: e.tensor_mul(TMP[:, 0:N], h, BCA[:, 0:N]),
                waits=(tA,) if cch == 0 else ())
            vop(lambda e: e.tensor_scalar(TMP[:, 0:N], TMP[:, 0:N],
                                          MAGIC, -MAGIC, AX.add, AX.add))
            tq = vop(lambda e, h=ms: e.tensor_scalar(
                h, TMP[:, 0:N], 127.0, None, AX.min))
        tB = bcast_row(R_OS[0:1, 0:N], BCB, N, [t_osd])
        for sl in range(8):
            lastml = [None, None]
            for jcg in range(4):
                b, tok_w = wload(wq_o["wd"].ap()
                                 [jcg * 2048:(jcg + 1) * 2048,
                                  sl * 256:(sl + 1) * 256],
                                 waits=(ag_toks["wd"],), queue="sync")
                for ml in range(2):
                    bk = ml
                    for kc in range(NCH):
                        jc = jcg * 16 + kc
                        lastml[ml] = mm(WB[b][:, kc * 256 + ml * 128:
                                              kc * 256 + (ml + 1) * 128],
                                        INTER[:, jc * 256:(jc + 1) * 256],
                                        pbank[bk][0:128, 0:N],
                                        jcg == 0 and kc == 0,
                                        jcg == 3 and kc == NCH - 1,
                                        waits=(tok_w, tq, banks[bk])
                                        if (jcg == 0 and kc == 0) else (tok_w,)
                                        if kc == 0 else (),
                                        inc="eT" if kc == NCH - 1 else None)
                wb_free[b] = lastml[1]
            for ml in range(2):
                oc = sl * 2 + ml
                bk = ml
                vop(lambda e, bk=bk: e.tensor_mul(
                    TMP2[:, 0:N], pbank[bk][0:128, 0:N], BCB[:, 0:N]),
                    waits=(lastml[ml], tB))
                ev = vop(lambda e, oc=oc, tok0=tok0: e.scalar_tensor_tensor(
                    X[:, oc * 1024 + tok0: oc * 1024 + tok0 + N],
                    TMP2[:, 0:N], MODV[:, 96 + 80 + oc:96 + 81 + oc],
                    X[:, oc * 1024 + tok0: oc * 1024 + tok0 + N],
                    AX.mult, AX.add), waits=(t_modv,))
                banks[bk] = ev
        res_tok[0] = banks[1]
    scopeD.close()
    C.soft_barrier()

    scopeE = contextlib.ExitStack()
    OUTS = scopeE.enter_context(nc.sbuf_tensor("OUTSx", [128, 2048], F32))
    out_toks = []
    prev_store = None
    for tsub in range(8):
        cps = []
        for cch in range(NCH):
            bk = cch % 4
            tt = tpose(X[:, cch * 1024 + tsub * 128: cch * 1024 + (tsub + 1) * 128],
                       IDF[:, :], pbank[bk][0:128, 0:128],
                       waits=(res_tok[0], banks[bk], prev_store))
            cp = vop(lambda e, cch=cch, bk=bk: e.tensor_copy(
                OUTS[:, cch * 128:(cch + 1) * 128], pbank[bk][0:128, 0:128]),
                waits=(tt,))
            banks[bk] = cp
            cps.append(cp)
        tok_o = dma("OS", out_d[tsub * 128:(tsub + 1) * 128, :], OUTS[:, :],
                    waits=(cps[-1],))
        prev_store = tok_o
        out_toks.append(tok_o)
    semh = C.semh
    fin = ("OS", C.cnt["OS"])
    C.q["sync"].append(lambda e, fin=fin: e.wait_ge(semh[fin[0]], fin[1]))
    scopeE.close()

    with nc.Block() as block:
        @block.tensor
        def _(e):
            for fn in C.q["tensor"]:
                fn(e)

        @block.vector
        def _(e):
            for fn in C.q["vector"]:
                fn(e)

        @block.scalar
        def _(e):
            for fn in C.q["scalar"]:
                fn(e)

        @block.gpsimd
        def _(e):
            for fn in C.q["gpsimd"]:
                fn(e)

        @block.sync
        def _(e):
            for fn in C.q["sync"]:
                fn(e)
    st.close()
    return nc


_NC_CACHE = [None]


def kernel(**inputs):
    in_maps = _host_prep(inputs)
    if _NC_CACHE[0] is None:
        _NC_CACHE[0] = build_program()
    nc = _NC_CACHE[0]
    res = run_bass_kernel_spmd(nc, in_maps, core_ids=list(range(8)))
    if os.environ.get("MK_DEBUG") == "1":
        kernel._dbg = res.results
    out = np.zeros((4, 2048, 2048), np.float32)
    for c in range(8):
        pair, par = c // 2, c % 2
        out[pair, par * 1024:(par + 1) * 1024, :] = res.results[c]["out"]
    return out
